# version 05
# baseline (speedup 1.0000x reference)
import math
import numpy as np
import ml_dtypes
from contextlib import ExitStack

import concourse.bass as bass
import concourse.mybir as mybir
from concourse.bass_utils import run_bass_kernel_spmd

F32 = mybir.dt.float32
BF16 = mybir.dt.bfloat16
AF = mybir.ActivationFunctionType
ALU = mybir.AluOpType
AX = mybir.AxisListType

D = 1024
NCH = 8
NH = 16
HD = 64
EPS = 1e-5
NEG = -1.0e30

SCAP_DEFAULT = 2560


ENGINES = ("tensor", "vector", "scalar", "gpsimd", "sync")
SEMS = ("t", "v", "s", "g", "dsy", "dgp", "dsc", "bar")
ENG_SEM = {"tensor": "t", "vector": "v", "scalar": "s", "gpsimd": "g"}
DMA_SEM = {"sync": "dsy", "gpsimd": "dgp", "scalar": "dsc"}


class Reg:
    __slots__ = ("wrs", "rds")

    def __init__(self):
        self.wrs = []
        self.rds = []


class MK:
    def __init__(self, nc, T):
        self.nc = nc
        self.T = T
        self.TILES = T // 128
        self.q = {e: [] for e in ENGINES}
        self.cnt = {s: 0 for s in SEMS}
        self.hw = {e: {} for e in ENGINES}
        self.nbar = 0
        self.es = ExitStack()
        self.phase_es = None
        self.dram = {}
        self.sub_es = None
        self.sb = {}
        self.regs = {}
        self.p8_reg = [Reg() for _ in range(5)]
        self.pv_reg = Reg()

    def emit(self, eng, method, *args, wait=(), inc=0, **kwargs):
        if eng == "scalar" and method == "tensor_copy":
            method = "copy"
        is_dma = method == "dma_start"
        semname = DMA_SEM[eng] if is_dma else ENG_SEM.get(eng)
        wait = tuple(wait)
        if not is_dma and eng in ("vector", "scalar", "gpsimd"):
            if self.cnt[semname] > 0:
                wait = wait + ((eng, semname, self.cnt[semname]),)
            inc = max(inc, 1)
        waits = []
        for tok in wait:
            if tok is None:
                continue
            peng, ps, pv = tok
            if peng == eng and not ps.startswith("d") and peng == "tensor":
                continue
            if self.hw[eng].get(ps, -1) >= pv:
                continue
            self.hw[eng][ps] = pv
            waits.append((ps, pv))
        incs = []
        tok = None
        if inc:
            self.cnt[semname] += inc
            incs.append((semname, inc))
            tok = (eng, semname, self.cnt[semname])
        self.q[eng].append((method, args, kwargs, tuple(waits), tuple(incs)))
        return tok

    def weights_ready(self):
        for eng in ENGINES:
            self.raw_wait(eng, ("sync", "dsy", self.cnt["dsy"]))

    def vtok(self):
        return ("vector", "v", self.cnt["v"])

    def stok(self):
        return ("scalar", "s", self.cnt["s"])

    def gtok(self):
        return ("gpsimd", "g", self.cnt["g"])

    def ttok(self):
        return ("tensor", "t", self.cnt["t"])

    def dma(self, queue, out, in_, wait=(), **kw):
        sem = DMA_SEM[queue]
        chain = ((queue, sem, self.cnt[sem]),) if self.cnt[sem] > 0 else ()
        return self.emit(queue, "dma_start", wait=tuple(wait) + chain, inc=16,
                         out=out, in_=in_, **kw)

    def raw_wait(self, eng, tok):
        if tok is None:
            return
        peng, ps, pv = tok
        if peng == eng and not ps.startswith("d"):
            return
        if self.hw[eng].get(ps, -1) >= pv:
            return
        self.hw[eng][ps] = pv
        self.q[eng].append(("__wait__", (), {}, ((ps, pv),), ()))

    def reg(self, name):
        if name not in self.regs:
            self.regs[name] = Reg()
        return self.regs[name]

    @staticmethod
    def wr_deps(r):
        return tuple(r.rds) + tuple(r.wrs)

    @staticmethod
    def rd_deps(r):
        return tuple(r.wrs)

    @staticmethod
    def set_wr(r, tok, reset=True):
        if reset:
            r.wrs = []
            r.rds = []
        if tok is not None:
            r.wrs.append(tok)

    @staticmethod
    def add_rd(r, tok):
        if tok is not None:
            r.rds.append(tok)

    def barrier(self):
        self.nbar += 1
        target = 5 * self.nbar
        for eng in ENGINES:
            w = []
            own = ENG_SEM.get(eng)
            if own is not None and self.cnt[own] > 0:
                w.append((own, self.cnt[own]))
            ds = DMA_SEM.get(eng)
            if ds is not None and self.cnt[ds] > 0:
                w.append((ds, self.cnt[ds]))
            self.cnt["bar"] += 1
            self.q[eng].append(("__barrier__", (), {}, tuple(w), (("bar", 1),)))
            self.q[eng].append(("__wait__", (), {}, (("bar", target),), ()))
            self.hw[eng]["bar"] = target
            if ds is not None:
                self.hw[eng][ds] = max(self.hw[eng].get(ds, -1), self.cnt[ds])
        regsets = list(self.regs.values()) + self.p8_reg + [self.pv_reg]
        regsets += getattr(self, "ptb_reg", []) + getattr(self, "ptsb_reg", [])
        for r in regsets:
            r.wrs = []
            r.rds = []

    def dump(self, name, ap, wait=()):
        if not getattr(self, "debug", False):
            return
        shp = list(ap.shape)
        d = self.nc.dram_tensor("dbg_" + name, shp, ap.dtype, kind="ExternalOutput").ap()
        self.dma("sync", d[:], ap, wait=tuple(wait))

    def sbuf(self, name, shape, dtype, phase=True, scope=None):
        if scope == "sub":
            es = self.sub_es
        elif scope == "global" or (not phase) or self.phase_es is None:
            es = self.es
        else:
            es = self.phase_es
        self._uid = getattr(self, "_uid", 0) + 1
        h = es.enter_context(self.nc.sbuf_tensor(f"{name}__{self._uid}", list(shape), dtype))
        self.sb[name] = h
        return h

    def begin_sub(self):
        assert getattr(self, "sub_es", None) is None
        self.sub_es = ExitStack()

    def end_sub(self):
        assert self.phase_es is None
        self.sub_es.close()
        self.sub_es = None

    def begin_phase(self):
        assert self.phase_es is None
        self.phase_es = ExitStack()

    def end_phase(self):
        self.phase_es.close()
        self.phase_es = None
        self.barrier()

    def finalize(self):
        nc = self.nc
        semh = {s: self.es.enter_context(nc.semaphore(s)) for s in SEMS}
        block = self.es.enter_context(nc.Block())

        def make_body(ops):
            def body(eng):
                for method, args, kwargs, waits, incs in ops:
                    if method == "__wait__":
                        for s, v in waits:
                            eng.wait_ge(semh[s], v)
                        continue
                    for s, v in waits:
                        eng.wait_ge(semh[s], v)
                    if method == "__barrier__":
                        inst = eng.sem_inc(semh["bar"], 1)
                        continue
                    inst = getattr(eng, method)(*args, **kwargs)
                    for s, k in incs:
                        inst.then_inc(semh[s], k)
            return body

        block.tensor(make_body(self.q["tensor"]))
        block.vector(make_body(self.q["vector"]))
        block.scalar(make_body(self.q["scalar"]))
        block.gpsimd(make_body(self.q["gpsimd"]))
        block.sync(make_body(self.q["sync"]))

    def close(self):
        self.es.close()



def _chunks(w):
    K, N = w.shape
    assert K % 128 == 0
    return np.ascontiguousarray(w.reshape(K // 128, 128, N).astype(np.float32))


def _chunksT(w):
    return _chunks(np.ascontiguousarray(w.T))


def _col(b):
    N = b.shape[0]
    assert N % 128 == 0
    return np.ascontiguousarray(b.reshape(N // 128, 128).T.astype(np.float32))


def _row(b, p=128):
    a = np.asarray(b, np.float32).astype(ml_dtypes.bfloat16)
    return np.ascontiguousarray(np.broadcast_to(a[None, :], (p, int(a.shape[0]))))


def _colpad64(b):
    out = np.zeros((128, 16), np.float32)
    for h in range(16):
        out[0:64, h] = b[64 * h:64 * h + 64]
    return out


def _padheads(w):
    K = w.shape[0]
    out = np.zeros((K, 2048), np.float32)
    for h in range(16):
        out[:, 128 * h:128 * h + 64] = w[:, 64 * h:64 * h + 64]
    return out


def host_consts(T):
    c = {}
    i128 = np.arange(128)
    c["ident"] = np.eye(128, dtype=np.float32).astype(ml_dtypes.bfloat16)
    c["m_tril"] = (i128[None, :] <= i128[:, None]).astype(ml_dtypes.bfloat16)
    c["m_tril2"] = (i128[None, :] <= i128[:, None] + 2).astype(ml_dtypes.bfloat16)
    c["m_tril2b"] = (i128[None, :] <= i128[:, None] - 126).astype(ml_dtypes.bfloat16)
    inv = 1.0 / 10000.0 ** (np.arange(0, 32, 2, dtype=np.float64) / 32)
    t = np.arange(T, dtype=np.float64)
    fr = np.outer(t, inv)
    c["cosA"] = np.cos(fr).astype(np.float32)
    c["sinA"] = np.sin(fr).astype(np.float32)
    inv2 = 1.0 / 10000.0 ** (np.arange(0, 32, 2, dtype=np.float64) / 32)
    fr2 = np.outer(t, inv2)
    emb = np.concatenate([fr2, fr2], axis=1)
    c["cosB"] = np.cos(emb).astype(np.float32)
    c["sinB"] = np.sin(emb).astype(np.float32)
    c["crB"] = np.concatenate([np.tile(c["cosB"][:, :16], (1, 16)),
                               np.tile(c["sinB"][:, :16], (1, 16))], axis=1).astype(np.float32)
    return c


def host_inputs(inp, core, T):
    g = lambda k: np.asarray(inp[k], np.float32)
    m = {}
    m["x"] = np.ascontiguousarray(g("L0_attn_a_x")[core][:T])
    m["a_kvd"] = _chunks(g("L0_attn_a_W_kv_down"))
    m["a_kup"] = _chunks(g("L0_attn_a_W_k_up"))
    m["a_vup"] = _chunks(g("L0_attn_a_W_v_up"))
    m["a_wq"] = _chunks(g("L0_attn_a_W_q"))
    m["a_wqr"] = _chunks(g("L0_attn_a_W_q_rope"))
    m["a_wkr"] = _chunks(g("L0_attn_a_W_k_rope"))
    m["a_wo"] = _chunks(g("L0_attn_a_W_o"))
    m["f_kvd"] = _chunksT(g("L0_ffn_kv_d"))
    m["f_qd"] = _chunksT(g("L0_ffn_q_d"))
    m["f_ku"] = _chunksT(g("L0_ffn_k_u"))
    m["f_qu"] = _chunksT(g("L0_ffn_q_u"))
    m["f_vu"] = _chunksT(g("L0_ffn_v_u"))
    m["f_rq"] = _chunksT(g("L0_ffn_rope_q_w"))
    m["f_rk"] = _chunksT(g("L0_ffn_rope_k_w"))
    m["f_ow"] = _chunksT(g("L0_ffn_o_w"))
    m["f_rt"] = _chunksT(g("L0_ffn_router_w"))
    m["f_rb_row"] = _row(g("L0_ffn_routing_bias"))
    m["f_ln1_row"] = _row(g("L0_ffn_ln1_w"))
    m["f_ln2_row"] = _row(g("L0_ffn_ln2_w"))
    m["f_sg"] = _chunksT(g("L0_ffn_shared_gate")[0])
    m["f_su"] = _chunksT(g("L0_ffn_shared_up")[0])
    m["f_sd"] = _chunksT(g("L0_ffn_shared_down")[0])
    m["f_rg"] = np.stack([_chunksT(g("L0_ffn_routed_gate")[e]) for e in range(7)])
    m["f_ru"] = np.stack([_chunksT(g("L0_ffn_routed_up")[e]) for e in range(7)])
    m["f_rd"] = np.stack([_chunksT(g("L0_ffn_routed_down")[e]) for e in range(7)])
    m["b_encT"] = _chunksT(g("L0_attn_b_encoder_hidden_states")[0])
    m["b_logits"] = np.ascontiguousarray(
        np.broadcast_to(g("L0_attn_b_scorer_logits").reshape(1, 2048)[:, None, :], (1, 16, 2048)).reshape(1, 16 * 2048))
    m["b_cattn"] = _chunks(g("L0_attn_b_c_attn_w"))
    cb = g("L0_attn_b_c_attn_b")
    m["b_cbq_col"] = _col(cb[0:1024])
    m["b_cbk_col"] = _col(cb[1024:2048])
    m["b_cbv_row"] = _row(cb[2048:3072])
    m["b_proj"] = _chunks(g("L0_attn_b_attn_proj_w"))
    m["b_projb_row"] = _row(g("L0_attn_b_attn_proj_b"))
    m["b_qattn"] = _chunks(_padheads(g("L0_attn_b_q_attn_w")))
    m["b_qb_colp"] = _colpad64(g("L0_attn_b_q_attn_b"))
    xkv = g("L0_attn_b_x_kv_w")
    m["b_xkvk"] = _chunks(_padheads(xkv[:, 0:1024]))
    xkvb = g("L0_attn_b_x_kv_b")
    m["b_xkvk_colp"] = _colpad64(xkvb[0:1024])
    m["b_xkvv"] = _chunks(xkv[:, 1024:2048])
    m["b_xkvv_row"] = _row(xkvb[1024:2048])
    m["b_xproj"] = _chunks(g("L0_attn_b_x_proj_w"))
    m["b_xpb_row"] = _row(g("L0_attn_b_x_proj_b"))
    m["b_fc"] = _chunks(g("L0_attn_b_fc_w"))
    m["b_fcb_row"] = _row(g("L0_attn_b_fc_b"))
    m["b_mproj"] = _chunks(g("L0_attn_b_mlp_proj_w"))
    m["b_mpb_row"] = _row(g("L0_attn_b_mlp_proj_b"))
    for nm, key in [("b_ln1g", "L0_attn_b_ln1_g"), ("b_ln1b", "L0_attn_b_ln1_b"),
                    ("b_lnxg", "L0_attn_b_lnx_g"), ("b_lnxb", "L0_attn_b_lnx_b"),
                    ("b_ln2g", "L0_attn_b_ln2_g"), ("b_ln2b", "L0_attn_b_ln2_b")]:
        m[nm + "_row"] = _row(g(key))
    m["wa_q"] = _chunksT(g("L1_attn_a_Wq"))
    m["wa_k"] = _chunksT(g("L1_attn_a_Wk"))
    m["wa_v"] = _chunksT(g("L1_attn_a_Wv"))
    m["wa_o"] = _chunksT(g("L1_attn_a_Wo"))
    m["c_wq"] = _chunks(g("L1_attn_b_Wq"))
    m["c_wk"] = _chunks(g("L1_attn_b_Wk"))
    m["c_wv"] = _chunks(g("L1_attn_b_Wv"))
    m["c_wo"] = _chunks(g("L1_attn_b_Wo"))
    m["c_bq_col"] = _col(g("L1_attn_b_bq"))
    m["c_bk_col"] = _col(g("L1_attn_b_bk"))
    m["c_bv_row"] = _row(g("L1_attn_b_bv"))
    m["c_bo_row"] = _row(g("L1_attn_b_bo"))
    m["c_w1"] = _chunks(g("L1_attn_b_W1"))
    m["c_b1_row"] = _row(g("L1_attn_b_b1"))
    m["c_w2"] = _chunks(g("L1_attn_b_W2"))
    m["c_b2_row"] = _row(g("L1_attn_b_b2"))
    for nm, key in [("c_lnqg", "L1_attn_b_lnq_s"), ("c_lnqb", "L1_attn_b_lnq_b"),
                    ("c_lnkg", "L1_attn_b_lnk_s"), ("c_lnkb", "L1_attn_b_lnk_b"),
                    ("c_lnvg", "L1_attn_b_lnv_s"), ("c_lnvb", "L1_attn_b_lnv_b"),
                    ("c_lnmg", "L1_attn_b_lnm_s"), ("c_lnmb", "L1_attn_b_lnm_b")]:
        m[nm + "_row"] = _row(g(key))
    m["g_rt"] = _chunksT(g("L1_ffn_router_w"))
    m["g_cb_row"] = _row(g("L1_ffn_correction_bias"))
    m["g_gw"] = np.stack([_chunks(g("L1_ffn_gate_w")[e]) for e in range(16)])
    m["g_uw"] = np.stack([_chunks(g("L1_ffn_up_w")[e]) for e in range(16)])
    m["g_dw"] = np.stack([_chunks(g("L1_ffn_down_w")[e]) for e in range(16)])
    m["g_sg"] = _chunksT(g("L1_ffn_shared_gate_w"))
    m["g_su"] = _chunksT(g("L1_ffn_shared_up_w"))
    m["g_sd"] = _chunksT(g("L1_ffn_shared_down_w"))
    m.update(host_consts(T))
    return m


PREP_GROUPS = {
    "A": ["a_kvd", "a_kup", "a_vup", "a_wq", "a_wqr", "a_wkr", "a_wo"],
    "F": ["f_kvd", "f_qd", "f_ku", "f_qu", "f_vu", "f_rq", "f_rk", "f_ow", "f_rt",
          "f_sg", "f_su", "f_sd", "f_rg", "f_ru", "f_rd"],
    "B": ["b_cattn", "b_proj", "b_qattn", "b_xkvk", "b_xkvv", "b_xproj", "b_fc",
          "b_mproj", "b_encT"],
    "WA": ["wa_q", "wa_k", "wa_v", "wa_o"],
    "C": ["c_wq", "c_wk", "c_wv", "c_wo", "c_w1", "c_w2"],
    "G": ["g_rt", "g_gw", "g_uw", "g_dw", "g_sg", "g_su", "g_sd"],
}



class WH:
    __slots__ = ("h", "nch", "N", "tok")

    def __init__(self, h, nch, N, tok):
        self.h, self.nch, self.N, self.tok = h, nch, N, tok

    def ck(self, c, o0=0, n=None):
        n = self.N - o0 if n is None else n
        return self.h[:, c * self.N + o0: c * self.N + o0 + n]


def declare_io(mk, shapes_dtypes, T, want_out, rs_init=False):
    nc = mk.nc
    for name, (shape, dt) in shapes_dtypes.items():
        mk.dram[name] = nc.dram_tensor(name, list(shape), dt, kind="ExternalInput").ap()
    if rs_init:
        mk.dram["rs_init"] = nc.dram_tensor("rs_init", [T, D], F32, kind="ExternalInput").ap()
    mk.dram["out"] = nc.dram_tensor("out", [T, D], F32, kind="ExternalOutput").ap()
    mk.dram["RS"] = nc.dram_tensor("RS", [T, D], F32, kind="Internal").ap()
    mk.dram["Y"] = nc.dram_tensor("Y", [T, D], F32, kind="Internal").ap()
    mk.dram["Y1"] = nc.dram_tensor("Y1", [T, D], F32, kind="Internal").ap()
    mk.dram["XNF"] = nc.dram_tensor("XNF", [NCH, 128, T], BF16, kind="Internal").ap()
    mk.dram["QS"] = nc.dram_tensor("QS", [T, 2048], BF16, kind="Internal").ap()
    mk.dram["QR"] = nc.dram_tensor("QR", [T // 128, 128, 128], BF16, kind="Internal").ap()


def declare_bf16(mk, names, shapes):
    for name in names:
        sh = list(shapes[name])
        mk.dram["bf_" + name] = mk.nc.dram_tensor("bf_" + name, sh, BF16, kind="Internal").ap()


def emit_prep(mk, names, shapes):
    stf = [mk.sbuf(f"prep_f{i}", [128, 4096], F32, phase=True) for i in range(2)]
    stb = [mk.sbuf(f"prep_b{i}", [128, 4096], BF16, phase=True) for i in range(2)]
    toks = [None, None]
    sttoks = [None, None]
    it = 0
    for name in names:
        sh = shapes[name]
        src = mk.dram[name]
        dst = mk.dram["bf_" + name]
        lead = int(np.prod(sh[:-2]))
        N = sh[-1]
        srcf = src.rearrange(" ".join(f"a{i}" for i in range(len(sh) - 2)) + " p n -> (" +
                             " ".join(f"a{i}" for i in range(len(sh) - 2)) + ") p n") if len(sh) > 3 else src
        dstf = dst.rearrange(" ".join(f"a{i}" for i in range(len(sh) - 2)) + " p n -> (" +
                             " ".join(f"a{i}" for i in range(len(sh) - 2)) + ") p n") if len(sh) > 3 else dst
        if len(sh) == 2:
            srcf = srcf.rearrange("p n -> 1 p n") if False else None
            raise AssertionError("prep expects chunked >=3d")
        for li in range(lead):
            for o in range(0, N, 4096):
                n = min(4096, N - o)
                sl = it % 2
                ld = mk.dma("sync", stf[sl][:, 0:n], srcf[li, :, o:o + n],
                            wait=(toks[sl],))
                cv = mk.emit("vector", "tensor_copy", stb[sl][:, 0:n], stf[sl][:, 0:n],
                             wait=(ld, sttoks[sl]), inc=1)
                toks[sl] = cv
                sttoks[sl] = mk.dma("sync", dstf[li, :, o:o + n], stb[sl][:, 0:n], wait=(cv,))
                it += 1
    return sttoks


def load_w(mk, name, queue="sync"):
    dst_key = "bf_" + name
    sh = mk.dram[dst_key].tensor.shape if hasattr(mk.dram[dst_key], "tensor") else None
    sh = list(mk.dram[dst_key].shape)
    assert len(sh) == 3, (name, sh)
    nch, _, N = sh
    h = mk.sbuf("w_" + name, [128, nch * N], BF16)
    tok = mk.dma(queue, h[:].rearrange("p (c n) -> p c n", c=nch),
                 mk.dram[dst_key].rearrange("c p n -> p c n"))
    return WH(h[:], nch, N, tok)


def load_w_cols(mk, name, c0, c1, queue="sync"):
    sh = list(mk.dram["bf_" + name].shape)
    nch, _, N = sh
    n = c1 - c0
    h = mk.sbuf(f"w_{name}_{c0}", [128, nch * n], BF16)
    tok = mk.dma(queue, h[:].rearrange("p (c n) -> p c n", c=nch),
                 mk.dram["bf_" + name][:, :, c0:c1].rearrange("c p n -> p c n"))
    return WH(h[:], nch, n, tok)


def load_w_expert(mk, name, e, sbname, queue="sync"):
    sh = list(mk.dram["bf_" + name].shape)
    E, nch, _, N = sh
    h = mk.sb[sbname]
    tok = mk.dma(queue, h[:, 0:nch * N].rearrange("p (c n) -> p c n", c=nch),
                 mk.dram["bf_" + name][e].rearrange("c p n -> p c n"))
    return WH(h[:], nch, N, tok)


def load_f32(mk, name, queue="sync"):
    sh = list(mk.dram[name].shape)
    assert len(sh) == 2 and sh[0] <= 128, (name, sh)
    dt = mk.dram[name].dtype
    h = mk.sbuf("f_" + name, sh, dt)
    tok = mk.dma(queue, h[:], mk.dram[name][:])
    mk.set_wr(mk.reg("f_" + name), tok)
    return h


class Scr:

    def __init__(self, mk, name, n=64, phase=True):
        self.h = mk.sbuf(name, [128, n], F32, phase=phase)
        self.n = n
        self.used = 0

    def col(self, k=1):
        a = self.h[:, self.used:self.used + k]
        self.used += k
        assert self.used <= self.n
        return a


def ln_tile(mk, rt, xn, sc2, gamma=None, beta=None, wait=(), double=False, xin=None):
    v1 = mk.emit("vector", "reduce_sum", sc2["negmean"], rt, axis=AX.X, negate=True, wait=wait)
    v2 = mk.emit("vector", "tensor_scalar_mul", sc2["negmean"], sc2["negmean"], 1.0 / D, inc=1)
    s1 = mk.emit("scalar", "activation", sc2["xc"], rt, AF.Identity, bias=sc2["negmean"],
                 wait=(v2,) + tuple(wait))
    s2 = mk.emit("scalar", "activation", sc2["sq"], sc2["xc"], AF.Square, accum_out=sc2["ssq"])
    s3 = mk.emit("scalar", "activation", sc2["rstd"], sc2["ssq"], AF.Sqrt, scale=1.0 / D,
                 bias=mk.epscol, inc=1)
    v3 = mk.emit("vector", "reciprocal", sc2["rstd"], sc2["rstd"], wait=(s3,), inc=1)
    out = {}
    if xin is not None:
        ya = mk.emit("vector", "tensor_scalar_mul", xin, sc2["xc"], sc2["rstd"],
                     wait=(s3,), inc=1)
        out["ya_tok"] = ya
    if double:
        mk.emit("vector", "tensor_scalar_mul", sc2["vc"], sc2["ssq"], 1.0 / D)
        mk.emit("vector", "tensor_mul", sc2["t2"], sc2["rstd"], sc2["rstd"])
        v5 = mk.emit("vector", "tensor_mul", sc2["t2"], sc2["t2"], sc2["vc"], inc=1)
        s4 = mk.emit("scalar", "activation", sc2["t2"], sc2["t2"], AF.Sqrt,
                     bias=mk.epscol, wait=(v5,), inc=1)
        mk.emit("vector", "reciprocal", sc2["t2"], sc2["t2"], wait=(s4,))
        vt = mk.emit("vector", "tensor_mul", sc2["stot"], sc2["rstd"], sc2["t2"], inc=1)
        scale_ap = sc2["stot"]
        sctok = vt
    else:
        scale_ap = sc2["rstd"]
        sctok = v3
    if gamma is None:
        tok = mk.emit("vector", "tensor_scalar_mul", xn, sc2["xc"], scale_ap,
                      wait=(sctok, s3), inc=1)
    elif beta is None:
        tok = mk.emit("vector", "scalar_tensor_tensor", xn, sc2["xc"], scale_ap, gamma,
                      ALU.mult, ALU.mult, wait=(sctok, s3), inc=1)
    else:
        t1 = mk.emit("vector", "scalar_tensor_tensor", sc2["tmp"], sc2["xc"], scale_ap, gamma,
                     ALU.mult, ALU.mult, wait=(sctok, s3))
        tok = mk.emit("vector", "tensor_add", xn, sc2["tmp"], beta, inc=1)
    out["tok"] = tok
    out["scale_ap"] = scale_ap
    out["xc"] = sc2["xc"]
    out["sctok"] = sctok
    return out


def ln_variant(mk, sc2, scale_tok, xn2, gamma, beta, wait=()):
    if beta is None:
        tok = mk.emit("vector", "scalar_tensor_tensor", xn2, sc2["xc"], sc2["stot"], gamma,
                      ALU.mult, ALU.mult, wait=(scale_tok,) + tuple(wait), inc=1)
    else:
        mk.emit("vector", "scalar_tensor_tensor", sc2["tmp"], sc2["xc"], sc2["stot"], gamma,
                ALU.mult, ALU.mult, wait=(scale_tok,) + tuple(wait))
        tok = mk.emit("vector", "tensor_add", xn2, sc2["tmp"], beta, inc=1)
    return tok


def transposes(mk, pairs, wait=(), evac="vector"):
    PT = mk.PT
    out = []
    idx = 0
    first = True
    while idx < len(pairs):
        grp = pairs[idx:idx + 4]
        X = mk.ptb_idx % 2
        mk.ptb_idx += 1
        breg = mk.ptb_reg[X]
        w = (tuple(wait) if first else ()) + mk.wr_deps(breg)
        first = False
        ttok = None
        for j, (srcap, dstap) in enumerate(grp):
            ttok = mk.emit("tensor", "transpose", PT[:, 1024 * X + 128 * j:1024 * X + 128 * j + 128],
                           srcap, mk.ident, wait=(w if j == 0 else ()),
                           inc=(1 if j == len(grp) - 1 else 0))
        mk.set_wr(breg, ttok)
        for j, (srcap, dstap) in enumerate(grp):
            pt = PT[:, 1024 * X + 128 * j:1024 * X + 128 * j + 128]
            if isinstance(dstap, list):
                for (r0, r1, dap) in dstap:
                    etok = mk.emit(evac, "tensor_copy", dap, pt[r0:r1, :],
                                   wait=(ttok,), inc=1)
                    mk.add_rd(breg, etok)
                    out.append(etok)
            else:
                etok = mk.emit(evac, "tensor_copy", dstap, pt, wait=(ttok,), inc=1)
                mk.add_rd(breg, etok)
                out.append(etok)
        idx += 4
    return out



def attn_core(mk, *, nkeys, scale, q_ap, kf, vf, attw, ps, pts, sc, rope=None,
              pmask=None, pextra=None, heads=NH, scap=None, attw_wait=(),
              evac_engs=("vector", "scalar")):
    P8 = mk.P8
    PVp = mk.PV
    scap = scap or mk.scap
    psw = mk.psw
    pend = {}

    def emit_scores_softmax(h):
        sl = h % 2
        C = sc[sl]
        csz = min(1024, scap)
        chunks = [(k0, min(k0 + csz, nkeys)) for k0 in range(0, nkeys, csz)]
        pairs_list = [chunks[i:i + 2] for i in range(0, len(chunks), 2)]
        ptoks = []
        ps_reg = mk.reg(f"ps{sl}")
        for pi, pcs in enumerate(pairs_list):
            cinfo = []
            for cj, (k0, k1) in enumerate(pcs):
                clen = k1 - k0
                base = 1024 * cj
                nbank = (clen + 511) // 512
                bank_regs = mk.p8_reg[2 * cj:2 * cj + nbank]
                first_w = []
                for br in bank_regs:
                    first_w += list(mk.wr_deps(br))
                if pi == 0 and cj == 0:
                    first_w += list(mk.wr_deps(ps_reg))
                sc_tok = None
                for off in range(0, clen, 512):
                    n = min(512, clen - off)
                    last_blk = off + 512 >= clen
                    pr = [(q_ap(h), kf(h, k0 + off, n))]
                    if rope is not None:
                        qr, krf = rope(h)
                        pr.append((qr, krf(h, k0 + off, n)))
                    for j, (l, r) in enumerate(pr):
                        sc_tok = mk.emit("tensor", "matmul", P8[:, base + off:base + off + n],
                                         l, r, start=(j == 0), stop=(j == len(pr) - 1),
                                         wait=(tuple(first_w) if (off == 0 and j == 0) else ()),
                                         inc=(1 if (last_blk and j == len(pr) - 1) else 0))
                for br in bank_regs:
                    mk.set_wr(br, sc_tok)
                first = pi == 0 and cj == 0
                dst = C["negm"] if first else C["negm2"]
                mx = mk.emit("vector", "reduce_max", dst, P8[:, base:base + clen], axis=AX.X,
                             negate=True, wait=(sc_tok,), inc=1)
                for br in bank_regs:
                    mk.add_rd(br, mx)
                if not first:
                    tgt = C["negm"] if pi == 0 else C["negmT"]
                    if pi > 0 and cj == 0:
                        mk.emit("vector", "tensor_copy", tgt, C["negm2"])
                    else:
                        mk.emit("vector", "tensor_tensor", tgt, tgt, C["negm2"], ALU.min)
                cinfo.append((k0, k1, base, sc_tok, bank_regs))
            if pi == 0:
                bt = mk.emit("vector", "tensor_scalar_mul", C["bias"], C["negm"], scale, inc=1)
            else:
                mk.emit("vector", "tensor_tensor", C["negm"], C["negm"], C["negmT"], ALU.min)
                mk.emit("vector", "tensor_scalar_mul", C["bias2"], C["negm"], scale, inc=1)
                rt = mk.emit("vector", "tensor_sub", C["rd"], C["bias2"], C["bias"], inc=1)
                re = mk.emit("scalar", "activation", C["r"], C["rd"], AF.Exp, wait=(rt,), inc=1)
                kp0 = pcs[0][0]
                rs1 = mk.emit("vector", "tensor_scalar_mul", ps[:, sl * psw: sl * psw + kp0],
                              ps[:, sl * psw: sl * psw + kp0], C["r"],
                              wait=(re,) + tuple(ptoks), inc=1)
                ptoks.append(rs1)
                bt = mk.emit("vector", "tensor_copy", C["bias"], C["bias2"], inc=1)
            for (k0, k1, base, sc_tok, bank_regs) in cinfo:
                et = mk.emit("scalar", "activation", ps[:, sl * psw + k0: sl * psw + k1],
                             P8[:, base:base + k1 - k0], AF.Exp, bias=C["bias"], scale=scale,
                             wait=(bt, sc_tok), inc=1)
                ptoks.append(et)
                for br in bank_regs:
                    mk.add_rd(br, et)
        if pmask is not None:
            for kc in range(nkeys // 128):
                msk = pmask(kc)
                if msk is not None:
                    o = sl * psw + kc * 128
                    mt = mk.emit("gpsimd", "tensor_mul", ps[:, o:o + 128], ps[:, o:o + 128],
                                 msk, wait=tuple(ptoks), inc=1)
                    ptoks.append(mt)
        if pextra is not None:
            tt = pextra(h, sl, ptoks)
            if tt is not None:
                ptoks.append(tt)
        mk.set_wr(ps_reg, None, reset=True)
        pend[h] = (ptoks, None)

    def emit_consume(h):
        sl = h % 2
        C = sc[sl]
        ptoks, _st = pend.pop(h)
        nkc = nkeys // 128
        ps_reg = mk.reg(f"ps{sl}")
        pv_tok = None
        kc = 0
        first = True
        while kc < nkc:
            grp = min(4, nkc - kc)
            X = mk.ptb_idx % 2
            mk.ptb_idx += 1
            breg = mk.ptb_reg[X]
            w = (tuple(ptoks) if first else ()) + mk.wr_deps(breg)
            first = False
            ttok = None
            for g in range(grp):
                ttok = mk.emit("tensor", "transpose",
                               mk.PT[:, 1024 * X + 128 * g:1024 * X + 128 * g + 128],
                               ps[:, sl * psw + (kc + g) * 128: sl * psw + (kc + g) * 128 + 128],
                               mk.ident, wait=(w if g == 0 else ()),
                               inc=(1 if g == grp - 1 else 0))
            mk.set_wr(breg, ttok)
            mk.add_rd(ps_reg, ttok)
            ev_eng = evac_engs[(kc // 4) % len(evac_engs)]
            etok = mk.emit(ev_eng, "tensor_copy", pts[:, 512 * X:512 * X + 128 * grp],
                           mk.PT[:, 1024 * X:1024 * X + 128 * grp],
                           wait=(ttok,) + mk.wr_deps(mk.ptsb_reg[X]), inc=1)
            mk.add_rd(breg, etok)
            mk.set_wr(mk.ptsb_reg[X], etok)
            for g in range(grp):
                wv = (etok,)
                if kc + g == 0:
                    wv = wv + mk.wr_deps(mk.pv_reg)
                pv_tok = mk.emit("tensor", "matmul", PVp[:, 0:65],
                                 pts[:, 512 * X + 128 * g:512 * X + 128 * g + 128],
                                 vf(kc + g, h),
                                 start=(kc + g == 0), stop=(kc + g == nkc - 1), wait=wv,
                                 inc=(1 if kc + g == nkc - 1 else 0))
            mk.add_rd(mk.ptsb_reg[X], pv_tok)
            kc += grp
        mk.set_wr(mk.pv_reg, pv_tok)
        rc = mk.emit("vector", "reciprocal", C["rs"], PVp[:, 64:65], wait=(pv_tok,))
        ot = mk.emit("vector", "tensor_scalar_mul", attw(h), PVp[:, 0:64], C["rs"],
                     wait=(pv_tok,) + tuple(attw_wait), inc=1)
        mk.add_rd(mk.pv_reg, ot)
        return ot

    out_toks = []
    for h in range(heads):
        emit_scores_softmax(h)
        if h > 0:
            out_toks.append(emit_consume(h - 1))
    out_toks.append(emit_consume(heads - 1))
    return out_toks


def mm_chain(mk, psum_slice, pairs, bank_regs, wait=()):
    w = tuple(wait)
    for br in bank_regs:
        w += mk.wr_deps(br)
    tok = None
    n = len(pairs)
    for j, (l, r) in enumerate(pairs):
        tok = mk.emit("tensor", "matmul", psum_slice, l, r, start=(j == 0),
                      stop=(j == n - 1), wait=(w if j == 0 else ()),
                      inc=(1 if j == n - 1 else 0))
    for br in bank_regs:
        mk.set_wr(br, tok)
    return tok


def mm_series(mk, fills, bank_regs, wait=()):
    w = tuple(wait)
    for br in bank_regs:
        w += mk.wr_deps(br)
    n = len(fills)
    tok = None
    for j, (p, l, r) in enumerate(fills):
        tok = mk.emit("tensor", "matmul", p, l, r, start=True, stop=True,
                      wait=(w if j == 0 else ()), inc=(1 if j == n - 1 else 0))
    for br in bank_regs:
        mk.set_wr(br, tok)
    return tok


def load_rope_tabs(mk, names):
    TL = mk.TILES
    out = {}
    for nm in names:
        f = mk.dram[nm].shape[1]
        h = mk.sbuf("rt_" + nm, [128, TL * f], F32)[:]
        mk.dma("sync", h.rearrange("p (a f) -> p a f", f=f),
               mk.dram[nm].rearrange("(a p) f -> p a f", p=128))
        out[nm] = h
    return out


def build_persistent(mk):
    nc = mk.nc
    mk.P8 = mk.es.enter_context(nc.psum_tensor("P8", [128, 2560], F32))[:]
    mk.PT = mk.es.enter_context(nc.psum_tensor("PT", [128, 2048], BF16))[:]
    mk.PV = mk.es.enter_context(nc.psum_tensor("PVp", [128, 512], F32))[:]
    mk.psw = 4096
    mk.scap = SCAP_DEFAULT
    mk.ptb_reg = [Reg(), Reg()]
    mk.ptsb_reg = [Reg(), Reg()]
    mk.ptb_idx = 0
    mk.spair = 0
    cb = mk.sbuf("cb", [128, 128 + 3 * 128], BF16, phase=False)
    mk.ident = cb[:, 0:128]
    mk.m_tril = cb[:, 128:256]
    mk.m_tril2 = cb[:, 256:384]
    mk.m_tril2b = cb[:, 384:512]
    mk.dma("sync", mk.ident, mk.dram["ident"][:])
    mk.dma("sync", mk.m_tril, mk.dram["m_tril"][:])
    mk.dma("sync", mk.m_tril2, mk.dram["m_tril2"][:])
    mk.dma("sync", mk.m_tril2b, mk.dram["m_tril2b"][:])

    ep = mk.sbuf("epscol", [128, 1], F32, phase=False)[:]
    mk.epscol = ep
    mk.emit("vector", "memset", ep, EPS, inc=1)


class TC:

    def __init__(self, mk, name="tc"):
        self.mk = mk
        self.rt = [mk.sbuf(f"{name}_rt{i}", [128, D], F32)[:] for i in range(2)]
        xn1 = mk.sbuf(f"{name}_xn", [128, D], BF16)[:]
        self.xn = [xn1, xn1]
        xnf1 = mk.sbuf(f"{name}_xnf", [128, D], BF16)[:]
        self.xnf = [xnf1, xnf1]
        sh = mk.sbuf(f"{name}_scr", [128, 2 * D + 32], F32)[:]
        self.scr = []
        for i in range(1):
            base = 0
            cb = 2 * D
            self.scr.append({
                "xc": sh[:, base:base + D], "sq": sh[:, base + D:base + 2 * D],
                "negmean": sh[:, cb:cb + 1],
                "ssq": sh[:, cb + 1:cb + 2],
                "rstd": sh[:, cb + 2:cb + 3],
                "vc": sh[:, cb + 3:cb + 4],
                "t2": sh[:, cb + 4:cb + 5],
                "stot": sh[:, cb + 5:cb + 6],
                "tmp": sh[:, base + D:base + 2 * D],
            })
        self.scr.append(self.scr[0])
        self.tmp_wide = [sh[:, 0:0] for _ in range(2)]
        self.rt_reg = [Reg(), Reg()]
        xr = Reg()
        self.xn_reg = [xr, xr]
        xfr = Reg()
        self.xnf_reg = [xfr, xfr]
        self.ln_busy = [None, None]

    def tile_in_sb(self, j, src_ap, src_toks, gamma=None, beta=None, fm=True, tmp=None):
        mk = self.mk
        sl = j % 2
        scr = dict(self.scr[sl])
        if tmp is not None:
            scr["tmp"] = tmp
        w = tuple(src_toks)
        if self.ln_busy[0] is not None:
            w = w + (self.ln_busy[0],)
        w = w + mk.wr_deps(self.xn_reg[sl])
        lt = ln_tile(mk, src_ap, self.xn[sl], scr, gamma=gamma, beta=beta, wait=w)
        mk.set_wr(self.xn_reg[sl], lt)
        self.ln_busy[0] = lt
        out = {"sl": sl, "xn": self.xn[sl], "xn_tok": lt, "xn_reg": self.xn_reg[sl]}
        if fm:
            pairs = [(self.xn[sl][:, 128 * c:128 * c + 128],
                      self.xnf[sl][:, 128 * c:128 * c + 128]) for c in range(NCH)]
            ft = transposes(mk, pairs, wait=(lt,) + mk.wr_deps(self.xnf_reg[sl]))
            for t in ft:
                mk.add_rd(self.xn_reg[sl], t)
            self.xnf_reg[sl].wrs = list(ft)
            self.xnf_reg[sl].rds = []
            out["xnf"] = self.xnf[sl]
            out["fm_toks"] = ft
            out["xnf_reg"] = self.xnf_reg[sl]
        return out

    def note_busy(self, sl, tok):
        self.ln_busy[0] = tok

    def tile_in(self, j, src, gamma=None, beta=None, fm=True, queue="gpsimd", tmp=None,
                double=False, xin=None):
        mk = self.mk
        sl = j % 2
        scr = dict(self.scr[sl])
        if tmp is not None:
            scr["tmp"] = tmp
        ld = mk.dma(queue, self.rt[sl], src[j * 128:(j + 1) * 128, :],
                    wait=mk.wr_deps(self.rt_reg[sl]))
        mk.set_wr(self.rt_reg[sl], ld)
        w = (ld,)
        if self.ln_busy[0] is not None:
            w = w + (self.ln_busy[0],)
        w = w + mk.wr_deps(self.xn_reg[sl])
        lr = ln_tile(mk, self.rt[sl], self.xn[sl], scr, gamma=gamma, beta=beta, wait=w,
                     double=double, xin=xin)
        lt = lr["tok"]
        mk.add_rd(self.rt_reg[sl], lt)
        mk.set_wr(self.xn_reg[sl], lt)
        self.ln_busy[0] = lt
        out = {"sl": sl, "rt": self.rt[sl], "rt_tok": ld, "rt_reg": self.rt_reg[sl],
               "xn": self.xn[sl], "xn_tok": lt, "xn_reg": self.xn_reg[sl],
               "ln": lr, "scr": scr}
        if fm:
            pairs = [(self.xn[sl][:, 128 * c:128 * c + 128],
                      self.xnf[sl][:, 128 * c:128 * c + 128]) for c in range(NCH)]
            ft = transposes(mk, pairs, wait=(lt,) + mk.wr_deps(self.xnf_reg[sl]))
            for t in ft:
                mk.add_rd(self.xn_reg[sl], t)
            mk.set_wr(self.xnf_reg[sl], ft[-1])
            for t in ft[:-1]:
                mk.add_rd(self.xnf_reg[sl], t)
            self.xnf_reg[sl].wrs = list(ft)
            self.xnf_reg[sl].rds = []
            out["xnf"] = self.xnf[sl]
            out["fm_toks"] = ft
            out["xnf_reg"] = self.xnf_reg[sl]
        return out


def attn_scr(mk, name):
    h = mk.sbuf(name, [128, 32], F32)[:]
    keys = ["negm", "negm2", "negmT", "bias", "bias2", "rd", "r", "ssumA", "ssumB",
            "stot", "rs"]
    out = []
    for i in range(2):
        out.append({k: h[:, i * 16 + j:i * 16 + j + 1] for j, k in enumerate(keys)})
    return out



def rope_apply(mk, x_psum, cos, sin, dst, halves=16, blocks=((0,),)):
    x1 = x_psum[:, 0:halves]
    x2 = x_psum[:, halves:2 * halves]
    t1 = mk.sb["ropescr"][:, 0:halves]
    t2 = mk.sb["ropescr"][:, halves:2 * halves]
    t3 = mk.sb["ropescr"][:, 2 * halves:3 * halves]
    t4 = mk.sb["ropescr"][:, 3 * halves:4 * halves]
    mk.emit("vector", "tensor_mul", t1, x1, cos)
    mk.emit("vector", "tensor_mul", t2, x2, sin)
    mk.emit("vector", "tensor_mul", t3, x2, cos)
    mk.emit("vector", "tensor_mul", t4, x1, sin)
    tok = None
    for (off,) in blocks:
        mk.emit("vector", "tensor_sub", dst[:, off:off + halves], t1, t2)
        tok = mk.emit("vector", "tensor_add", dst[:, off + halves:off + 2 * halves],
                      t3, t4, inc=1)
    return tok


def sub_l0_attn_a(mk, src, dst):
    T, TILES = mk.T, mk.TILES
    nc = mk.nc
    B = lambda i: mk.P8[:, 512 * i:512 * (i + 1)]
    BR = mk.p8_reg

    mk.begin_sub()
    KB = mk.sbuf("KBUF", [128, NCH * T], BF16, scope="sub")[:]
    VB = mk.sbuf("VBUF", [128, TILES * 1040], BF16, scope="sub")[:]
    KRP = mk.sbuf("KRP", [128, T], BF16, scope="sub")[:]
    mk.begin_phase()
    W = {n: load_w(mk, n) for n in ["a_kvd", "a_kup", "a_vup", "a_wkr"]}
    rt_tabs = load_rope_tabs(mk, ["cosA", "sinA"])
    cosA, sinA = rt_tabs["cosA"], rt_tabs["sinA"]
    mk.weights_ready()
    tc = TC(mk, "p1")
    mk.sbuf("ropescr", [128, 64], F32)
    ckv = [mk.sbuf(f"ckv{i}", [128, 128], BF16)[:] for i in range(2)]
    ckv_reg = [Reg(), Reg()]
    krp_tm = mk.sbuf("krp_tm", [128, 128], BF16)[:]
    krp_reg = Reg()
    mk.emit("vector", "memset", krp_tm, 0.0, inc=1)
    mk.emit("vector", "memset", VB, 1.0, inc=1)
    wt = tuple(w.tok for w in W.values())
    for j in range(TILES):
        ti = tc.tile_in(j, src)
        sl = ti["sl"]
        xnf = ti["xnf"]
        fmw = tuple(mk.wr_deps(ti["xnf_reg"]))
        ck = mm_chain(mk, B(4)[:, 0:128],
                      [(W["a_kvd"].ck(c), xnf[:, 128 * c:128 * c + 128]) for c in range(NCH)],
                      [BR[4]], wait=fmw + wt)
        for t in [ck]:
            mk.add_rd(ti["xnf_reg"], t)
        ce = mk.emit("vector", "tensor_copy", ckv[sl], B(4)[:, 0:128],
                     wait=(ck,) + mk.wr_deps(ckv_reg[sl]), inc=1)
        mk.add_rd(BR[4], ce)
        mk.set_wr(ckv_reg[sl], ce)
        for nb in range(2):
            fills = [(B(nb)[:, 128 * q:128 * q + 128],
                      W["a_kup"].ck(0, o0=128 * (4 * nb + q), n=128), ckv[sl])
                     for q in range(4)]
            kt = mm_series(mk, fills, [BR[nb]], wait=(ce,))
            dstap = KB[:, 4 * nb * T: (4 * nb + 4) * T].rearrange(
                "p (c t) -> p c t", c=4)[:, :, 128 * j:128 * j + 128]
            ke = mk.emit("vector", "tensor_copy", dstap,
                         B(nb).rearrange("p (c t) -> p c t", c=4), wait=(kt,), inc=1)
            mk.add_rd(BR[nb], ke)
            mk.add_rd(ckv_reg[sl], kt)
        vt = mm_series(mk, [(B(2), ckv[sl], W["a_vup"].ck(0, 0, 512)),
                            (B(3), ckv[sl], W["a_vup"].ck(0, 512, 512))],
                       [BR[2], BR[3]], wait=(ce,))
        ve = mk.emit("vector", "tensor_copy",
                     VB[:, 1040 * j:1040 * (j + 1)].rearrange("p (h d) -> p h d", d=65)[:, :, 0:64],
                     mk.P8[:, 1024:2048].rearrange("p (h d) -> p h d", d=64), wait=(vt,), inc=1)
        mk.add_rd(BR[2], ve)
        mk.add_rd(BR[3], ve)
        mk.add_rd(ckv_reg[sl], vt)
        rt5 = mm_chain(mk, B(4)[:, 128:160], [(ckv[sl], W["a_wkr"].ck(0))], [BR[4]], wait=(ce,))
        mk.add_rd(ckv_reg[sl], rt5)
        rw = tuple(mk.wr_deps(krp_reg))
        mk.raw_wait("vector", rt5)
        for t in rw:
            mk.raw_wait("vector", t)
        rtok = rope_apply(mk, B(4)[:, 128:160], cosA[:, 16 * j:16 * j + 16],
                          sinA[:, 16 * j:16 * j + 16], krp_tm, halves=16,
                          blocks=((0,), (64,)))
        mk.add_rd(BR[4], rtok)
        ktt = transposes(mk, [(krp_tm, KRP[:, 128 * j:128 * j + 128])], wait=(rtok,))
        mk.set_wr(krp_reg, None)
        for t in ktt:
            mk.add_rd(krp_reg, t)
        if j == 0:
            mk.dump("xn0", tc.xn[0], wait=(ti["xn_tok"],))
            mk.dump("xc0", tc.scr[0]["xc"], wait=(ti["xn_tok"],))
            mk.dump("negmean0", tc.scr[0]["negmean"], wait=(ti["xn_tok"],))
            mk.dump("ssq0", tc.scr[0]["ssq"], wait=(ti["xn_tok"],))
            mk.dump("rstd0", tc.scr[0]["rstd"], wait=(ti["xn_tok"],))
            mk.dump("xnf0", ti["xnf"], wait=tuple(mk.wr_deps(ti["xnf_reg"])))
            mk.dump("ckv0", ckv[0], wait=(ce,))
    mk.end_phase()
    mk.dump("kb0", KB[:, 0:512])
    mk.dump("kb4", KB[:, 4 * T:4 * T + 512])
    mk.dump("vb0", VB[:, 0:1040])
    mk.dump("krp0", KRP[:, 0:256])

    mk.begin_phase()
    W2 = {n: load_w(mk, n) for n in ["a_wq", "a_wqr"]}
    rt_tabs2 = load_rope_tabs(mk, ["cosA", "sinA"])
    cosA2, sinA2 = rt_tabs2["cosA"], rt_tabs2["sinA"]
    mk.weights_ready()
    mk.sbuf("ropescr", [128, 64], F32)
    qrp_tm = mk.sbuf("qrp_tm", [128, 128], BF16)[:]
    mk.emit("vector", "memset", qrp_tm, 0.0, inc=1)
    QRS = [mk.sbuf(f"QRS{i}", [128, 128], BF16)[:] for i in range(2)]
    QRS_reg = [Reg(), Reg()]
    B = lambda i: mk.P8[:, 512 * i:512 * (i + 1)]
    BR = mk.p8_reg

    def extra_rope(ti, i):
        sl = ti["sl"]
        xnf = ti["xnf"]
        fmw = tuple(mk.wr_deps(ti["xnf_reg"]))
        qr = mm_chain(mk, B(2)[:, 0:32],
                      [(xnf[:, 128 * c:128 * c + 128], W2["a_wqr"].ck(c)) for c in range(NCH)],
                      [BR[2]], wait=fmw)
        mk.add_rd(ti["xnf_reg"], qr)
        mk.raw_wait("vector", qr)
        rtok = rope_apply(mk, B(2)[:, 0:32], cosA2[:, 16 * i:16 * i + 16],
                          sinA2[:, 16 * i:16 * i + 16], qrp_tm, halves=16,
                          blocks=((0,), (64,)))
        mk.add_rd(BR[2], rtok)
        qtt = transposes(mk, [(qrp_tm, QRS[sl])], wait=(rtok,) + mk.wr_deps(QRS_reg[sl]))
        mk.set_wr(QRS_reg[sl], qtt[-1])
        st = mk.dma("sync", mk.dram["QR"][i], QRS[sl], wait=(qtt[-1],))
        mk.add_rd(QRS_reg[sl], st)

    q_store_pass(mk, src, T, W2["a_wq"], None, None, None, double=False,
                 extra=extra_rope)
    mk.end_phase()
    mk.begin_phase()
    wo = load_w(mk, "a_wo")
    mk.weights_ready()
    attn_only_pass(mk, dst, T, KB, VB, wo, None, 1.0 / math.sqrt(96.0),
                   lambda i: 128 * (i + 1),
                   pmask_fn=lambda i, kc: mk.m_tril if kc == i else None,
                   kf=lambda h, k0, n: KB[(h % 2) * 64:(h % 2) * 64 + 64,
                                          T * (h // 2) + k0:T * (h // 2) + k0 + n],
                   vf=lambda kc, h: VB[:, 1040 * kc + 65 * h:1040 * kc + 65 * h + 65],
                   rope_dram=mk.dram["QR"], krp=KRP, adds=(src,))
    mk.end_phase()
    mk.end_sub()


SUBS = {"a1": sub_l0_attn_a}
SUB_PREP = {"a1": "A", "f1": "F", "f2": "F", "b1": "B", "wa": "WA", "g1": "G",
            "g2": "G", "cb": "C"}
SUB_ORDER = ["a1", "f1", "b1", "f2", "wa", "g1", "cb", "g2"]


def build_program(shapes_dtypes, T, subs, rs_init=False, detect_races=True, debug=False):
    nc = bass.Bass("TRN2", target_bir_lowering=False, detect_race_conditions=detect_races)
    mk = MK(nc, T)
    mk.debug = debug
    declare_io(mk, shapes_dtypes, T, True, rs_init=rs_init)
    prep_names = []
    seen = set()
    for s in subs:
        g = SUB_PREP[s]
        if g in seen:
            continue
        seen.add(g)
        prep_names += PREP_GROUPS[g]
    shapes = {k: v[0] for k, v in shapes_dtypes.items()}
    declare_bf16(mk, prep_names, shapes)
    build_persistent(mk)
    mk.begin_phase()
    emit_prep(mk, prep_names, shapes)
    mk.end_phase()
    src = mk.dram["rs_init"] if rs_init else mk.dram["x"]
    for k, s in enumerate(subs):
        dst = mk.dram["out"] if k == len(subs) - 1 else mk.dram["RS"]
        SUBS[s](mk, src, dst)
        src = dst
    mk.finalize()
    return mk


def _shapes_dtypes(m):
    out = {}
    for k, v in m.items():
        dt = BF16 if v.dtype == ml_dtypes.bfloat16 else F32
        out[k] = (tuple(v.shape), dt)
    return out


def run_subchain(inputs, T, subs, cores, rs_state=None, trace=False):
    maps = [host_inputs(inputs, c, T) for c in cores]
    if rs_state is not None:
        for i in range(len(cores)):
            maps[i]["rs_init"] = np.ascontiguousarray(rs_state[i].astype(np.float32))
    sd = _shapes_dtypes({k: v for k, v in maps[0].items() if k != "rs_init"})
    mk = build_program(sd, T, subs, rs_init=rs_state is not None)
    res = run_bass_kernel_spmd(mk.nc, maps, core_ids=list(range(len(cores))), trace=trace)
    mk.close()
    out = np.stack([r["out"] for r in res.results])
    return out, res


def kernel(**inputs):
    T = 4096
    maps = [host_inputs(inputs, c, T) for c in range(8)]
    sd = _shapes_dtypes(maps[0])
    mk = build_program(sd, T, SUB_ORDER)
    res = run_bass_kernel_spmd(mk.nc, maps, core_ids=list(range(8)))
    mk.close()
    return np.stack([np.asarray(r["out"]).astype(np.float32) for r in res.results])


class WHoff:

    def __init__(self, wh, base):
        self.wh, self.base, self.tok = wh, base, wh.tok
        self.nch = wh.nch

    def ck(self, c, o0=0, n=None):
        return self.wh.ck(c, self.base + o0, n)


def gelu_tanh(mk, psum_ap, brow, out_bf, gs, out_wait=(), in_wait=()):
    mk.emit("vector", "tensor_add", gs["ub"], psum_ap, brow, wait=tuple(in_wait))
    mk.emit("scalar", "activation", gs["x2"], gs["ub"], AF.Square,
            wait=(mk.vtok(),))
    mk.emit("gpsimd", "tensor_mul", gs["x3"], gs["x2"], gs["ub"], wait=(mk.stok(),))
    mk.emit("vector", "scalar_tensor_tensor", gs["x3"], gs["x3"], 0.044715, gs["ub"],
            ALU.mult, ALU.add, wait=(mk.gtok(),))
    mk.emit("scalar", "activation", gs["x2"], gs["x3"], AF.Tanh, scale=0.7978845608028654,
            wait=(mk.vtok(),))
    mk.emit("gpsimd", "tensor_mul", gs["x3"], gs["x2"], gs["ub"], wait=(mk.stok(),))
    mk.emit("gpsimd", "tensor_add", gs["x3"], gs["x3"], gs["ub"])
    tok = mk.emit("vector", "tensor_scalar_mul", out_bf, gs["x3"], 0.5,
                  wait=(mk.gtok(),) + tuple(out_wait), inc=1)
    return tok


def qkv_build_pass(mk, src, T, KB, VB, wk, wv, bk_col, bv_row, lnkg, lnkb,
                   lnvg=None, lnvb=None, double=True, y_store=None, wtoks=(), only=None):
    B = lambda i: mk.P8[:, 512 * i:512 * (i + 1)]
    BR = mk.p8_reg
    tc = TC(mk, "kvp")
    xw = mk.sbuf("kvp_w", [128, D], F32)[:] if y_store is not None else None
    xw_reg = Reg()
    sep_v = lnvg is not None
    if sep_v:
        _xv = mk.sbuf("kvp_xnv", [128, D], BF16)[:]
        _xvf = mk.sbuf("kvp_xnvf", [128, D], BF16)[:]
        xnv, xnvf = [_xv, _xv], [_xvf, _xvf]
        _r1, _r2 = Reg(), Reg()
        xnv_reg, xnvf_reg = [_r1, _r1], [_r2, _r2]
    else:
        xnv = xnvf = None
        xnv_reg = [Reg(), Reg()]
        xnvf_reg = [Reg(), Reg()]
    for j in range(mk.TILES):
        ti = tc.tile_in(j, src, gamma=lnkg, beta=lnkb, fm=True, double=double,
                        xin=(xw if y_store is not None else None))
        sl = ti["sl"]
        if y_store is not None:
            ya = ti["ln"]["ya_tok"]
            yst = mk.dma("sync", y_store[j * 128:(j + 1) * 128, :], xw,
                         wait=(ya,) + mk.wr_deps(xw_reg))
            mk.set_wr(xw_reg, ya)
            mk.add_rd(xw_reg, yst)
        if sep_v:
            vdeps = mk.wr_deps(xnv_reg[sl])
            vt0 = ln_variant(mk, ti["scr"], ti["ln"]["sctok"], xnv[sl], lnvg, lnvb,
                             wait=vdeps)
            mk.set_wr(xnv_reg[sl], vt0)
            tc.note_busy(sl, vt0)
            vft = transposes(mk, [(xnv[sl][:, 128 * c:128 * c + 128],
                                   xnvf[sl][:, 128 * c:128 * c + 128]) for c in range(NCH)],
                             wait=(vt0,) + mk.wr_deps(xnvf_reg[sl]))
            xnvf_reg[sl].wrs = list(vft)
            xnvf_reg[sl].rds = []
            for t in vft:
                mk.add_rd(xnv_reg[sl], t)
            xnfv = xnvf[sl]
            vreg = xnvf_reg[sl]
        else:
            xnfv = ti["xnf"]
            vreg = ti["xnf_reg"]
        xnfk = ti["xnf"]
        fmwk = tuple(mk.wr_deps(ti["xnf_reg"])) + tuple(wtoks)
        fmwv = tuple(mk.wr_deps(vreg)) + tuple(wtoks)
        for nb in (() if only == "V" else range(2)):
            gts = []
            for q in range(4):
                co = 4 * nb + q
                gt = mm_chain(mk, B(nb)[:, 128 * q:128 * q + 128],
                              [(wk.ck(c, o0=128 * co, n=128), xnfk[:, 128 * c:128 * c + 128])
                               for c in range(NCH)],
                              [BR[nb]] if q == 0 else [], wait=(fmwk if q == 0 else ()))
                gts.append(gt)
            mk.set_wr(BR[nb], gts[-1])
            mk.add_rd(ti["xnf_reg"], gts[-1])
            for q in range(4):
                co = 4 * nb + q
                dstap = KB[:, co * T + 128 * j: co * T + 128 * j + 128]
                if bk_col is not None:
                    ke = mk.emit("vector", "tensor_scalar_add", dstap,
                                 B(nb)[:, 128 * q:128 * q + 128], bk_col[:, co:co + 1],
                                 wait=(gts[-1],), inc=1)
                else:
                    ke = mk.emit("vector", "tensor_copy", dstap,
                                 B(nb)[:, 128 * q:128 * q + 128], wait=(gts[-1],), inc=1)
                mk.add_rd(BR[nb], ke)
        for nb in (() if only == "K" else range(2)):
            vt = mm_chain(mk, B(2 + nb), [(xnfv[:, 128 * c:128 * c + 128],
                                           wv.ck(c, o0=512 * nb, n=512)) for c in range(NCH)],
                          [BR[2 + nb]], wait=fmwv)
            mk.add_rd(vreg, vt)
            dv = VB[:, 1040 * j:1040 * (j + 1)].rearrange("p (h d) -> p h d", d=65)
            dv = dv[:, 8 * nb:8 * nb + 8, 0:64]
            sv = B(2 + nb).rearrange("p (h d) -> p h d", d=64)
            if bv_row is not None:
                bv3 = bv_row[:, 512 * nb:512 * nb + 512].rearrange("p (h d) -> p h d", d=64)
                ve = mk.emit("vector", "scalar_tensor_tensor", dv, sv, 1.0,
                             bv3, ALU.mult, ALU.add, wait=(vt,), inc=1)
            else:
                ve = mk.emit("vector", "tensor_copy", dv, sv, wait=(vt,), inc=1)
            mk.add_rd(BR[2 + nb], ve)


def q_store_pass(mk, src, T, wq, bq_col, lng, lnb, double, nco=8, padded=False,
                 y_store=None, wtoks=(), extra=None):
    B = lambda i: mk.P8[:, 512 * i:512 * (i + 1)]
    BR = mk.p8_reg
    QS = mk.dram["QS"]
    tc = TC(mk, "qsp")
    xw = mk.sbuf("qs_w", [128, D], F32)[:] if y_store is not None else None
    xw_reg = Reg()
    QA = [mk.sbuf(f"qsQA{i}", [128, 128 * nco], BF16)[:] for i in range(2)]
    QA_reg = [Reg(), Reg()]
    if padded:
        for i in range(2):
            mk.emit("vector", "memset", QA[i], 0.0, inc=1)
            mk.emit("vector", "memset", QA[i][64:65, :], 8.0, inc=1)
    for i in range(mk.TILES):
        ti = tc.tile_in(i, src, gamma=lng, beta=lnb, fm=True, double=double,
                        xin=(xw if y_store is not None else None))
        sl = ti["sl"]
        if y_store is not None:
            ya = ti["ln"]["ya_tok"]
            yst = mk.dma("sync", y_store[i * 128:(i + 1) * 128, :], xw,
                         wait=(ya,) + mk.wr_deps(xw_reg))
            mk.set_wr(xw_reg, ya)
            mk.add_rd(xw_reg, yst)
        xnf = ti["xnf"]
        fmw = tuple(mk.wr_deps(ti["xnf_reg"])) + tuple(wtoks)
        qdeps = mk.wr_deps(QA_reg[sl])
        QA_reg[sl].wrs = []
        QA_reg[sl].rds = []
        for r in range(nco // 4):
            bk = r % 2
            gts = []
            for q in range(4):
                co = 4 * r + q
                gt = mm_chain(mk, B(bk)[:, 128 * q:128 * q + 128],
                              [(wq.ck(c, o0=128 * co, n=128), xnf[:, 128 * c:128 * c + 128])
                               for c in range(NCH)],
                              [BR[bk]] if q == 0 else [], wait=(fmw if q == 0 else ()))
                gts.append(gt)
            mk.set_wr(BR[bk], gts[-1])
            mk.add_rd(ti["xnf_reg"], gts[-1])
            for q in range(4):
                co = 4 * r + q
                rows = slice(0, 64) if padded else slice(0, 128)
                dstq = QA[sl][rows, 128 * co:128 * co + 128]
                if bq_col is not None:
                    qe = mk.emit("vector", "tensor_scalar_add", dstq,
                                 B(bk)[rows, 128 * q:128 * q + 128],
                                 bq_col[rows, co:co + 1], wait=(gts[-1],) + qdeps, inc=1)
                else:
                    qe = mk.emit("vector", "tensor_copy", dstq,
                                 B(bk)[rows, 128 * q:128 * q + 128],
                                 wait=(gts[-1],) + qdeps, inc=1)
                mk.add_rd(BR[bk], qe)
                QA_reg[sl].wrs.append(qe)
        st = mk.dma("sync", QS[i * 128:(i + 1) * 128, 0:128 * nco], QA[sl],
                    wait=tuple(QA_reg[sl].wrs))
        mk.add_rd(QA_reg[sl], st)
        if extra is not None:
            extra(ti, i)
    return tc


def attn_only_pass(mk, dst, T, KB, VB, wo, wo_brow, scale, nkeys_fn, pmask_fn=None,
                   pextra_fn=None, qwidth=1024, qmode="heads64", kf=None, vf=None,
                   rope_dram=None, krp=None, adds=(), psw=None, wtoks=()):
    B = lambda i: mk.P8[:, 512 * i:512 * (i + 1)]
    BR = mk.p8_reg
    psw = psw or mk.psw
    old_psw = mk.psw
    mk.psw = psw
    QS = mk.dram["QS"]
    QT = [mk.sbuf(f"aoQT{i}", [128, qwidth], BF16)[:] for i in range(2)]
    QT_reg = [Reg(), Reg()]
    QR = [mk.sbuf(f"aoQR{i}", [128, 128], BF16)[:] for i in range(2)] if rope_dram is not None else None
    QR_reg = [Reg(), Reg()]
    AT = [mk.sbuf(f"aoAT{i}", [128, 1024], BF16)[:] for i in range(2)]
    AT_reg = [Reg(), Reg()]
    AF_ = [mk.sbuf(f"aoAF{i}", [128, 1024], BF16)[:] for i in range(2)]
    AF_reg = [Reg(), Reg()]
    PSB = mk.sbuf("aoPSB", [128, 2 * psw], BF16)[:]
    PTS = mk.sbuf("aoPTS", [128, 1024], BF16)[:]
    OT = [mk.sbuf(f"aoOT{i}", [128, 1024], F32)[:] for i in range(2)]
    OT_reg = [Reg(), Reg()]
    AD = [[mk.sbuf(f"aoAD{a}_{i}", [128, 1024], F32)[:] for i in range(2)]
          for a in range(len(adds))]
    AD_reg = [[Reg(), Reg()] for _ in range(len(adds))]
    sc = attn_scr(mk, "aosc")
    for i in range(mk.TILES):
        sl = i % 2
        qld = mk.dma("gpsimd", QT[sl], QS[i * 128:(i + 1) * 128, 0:qwidth],
                     wait=mk.wr_deps(QT_reg[sl]))
        mk.set_wr(QT_reg[sl], qld)
        rtoks = (qld,) + tuple(wtoks)
        if rope_dram is not None:
            rld = mk.dma("gpsimd", QR[sl], rope_dram[i], wait=mk.wr_deps(QR_reg[sl]))
            mk.set_wr(QR_reg[sl], rld)
            rtoks = rtoks + (rld,)
        addl = []
        for a, adram in enumerate(adds):
            ald = mk.dma("gpsimd", AD[a][sl], adram[i * 128:(i + 1) * 128, :],
                         wait=mk.wr_deps(AD_reg[a][sl]))
            mk.set_wr(AD_reg[a][sl], ald)
            addl.append((AD[a][sl], ald, AD_reg[a][sl]))
        nkeys = nkeys_fn(i)
        qa = QT[sl]
        at = AT[sl]
        for t in rtoks:
            mk.raw_wait("tensor", t)
        at_deps = tuple(AT_reg[sl].rds)
        AT_reg[sl].rds = []
        if qmode == "heads64":
            q_ap = lambda h: qa[(h % 2) * 64:(h % 2) * 64 + 64, 128 * (h // 2):128 * (h // 2) + 128]
        else:
            q_ap = lambda h: qa[:, 128 * h:128 * h + 128]
        rope_arg = None
        if rope_dram is not None:
            qr = QR[sl]
            rope_arg = lambda h: (qr[(h % 2) * 64:(h % 2) * 64 + 64, :],
                                  lambda hh, k0, n: krp[(hh % 2) * 64:(hh % 2) * 64 + 64, k0:k0 + n])
        ot_toks = attn_core(
            mk, nkeys=nkeys, scale=scale, attw_wait=at_deps,
            q_ap=q_ap, kf=kf, vf=vf, rope=rope_arg,
            attw=lambda h: at[:, 64 * h:64 * h + 64],
            ps=PSB, pts=PTS, sc=sc,
            pmask=(lambda kc: pmask_fn(i, kc)) if pmask_fn else None,
            pextra=(lambda h, slp, ptoks: pextra_fn(i, h, slp, ptoks, PSB)) if pextra_fn else None)
        for t in ot_toks:
            QT_reg[sl].rds.append(t)
            if rope_dram is not None:
                QR_reg[sl].rds.append(t)
        aft = transposes(mk, [(at[:, 128 * c:128 * c + 128], AF_[sl][:, 128 * c:128 * c + 128])
                              for c in range(NCH)],
                         wait=tuple(ot_toks) + mk.wr_deps(AF_reg[sl]))
        AF_reg[sl].wrs = list(aft)
        AF_reg[sl].rds = []
        for t in aft:
            AT_reg[sl].rds.append(t)
        odeps = mk.wr_deps(OT_reg[sl])
        OT_reg[sl].wrs = []
        OT_reg[sl].rds = []
        for nb in range(2):
            wo_t = mm_chain(mk, B(3 + nb), [(AF_[sl][:, 128 * c:128 * c + 128],
                                             wo.ck(c, 512 * nb, 512)) for c in range(NCH)],
                            [BR[3 + nb]], wait=tuple(aft) + tuple(wtoks))
            mk.add_rd(AF_reg[sl], wo_t)
            colr = slice(512 * nb, 512 * nb + 512)
            if wo_brow is not None:
                oe = mk.emit("vector", "scalar_tensor_tensor", OT[sl][:, colr],
                             B(3 + nb), 1.0, wo_brow[:, colr], ALU.mult, ALU.add,
                             wait=(wo_t,) + odeps, inc=1)
            else:
                oe = mk.emit("vector", "tensor_copy", OT[sl][:, colr], B(3 + nb),
                             wait=(wo_t,) + odeps, inc=1)
            mk.add_rd(BR[3 + nb], oe)
            for (aap, atok, areg) in addl:
                oe = mk.emit("vector", "tensor_add", OT[sl][:, colr], OT[sl][:, colr],
                             aap[:, colr], wait=(atok,), inc=1)
                mk.add_rd(areg, oe)
            OT_reg[sl].wrs.append(oe)
        st = mk.dma("sync", dst[i * 128:(i + 1) * 128, :], OT[sl],
                    wait=tuple(OT_reg[sl].wrs))
        mk.add_rd(OT_reg[sl], st)
    mk.psw = old_psw


def mlp_pass(mk, src, dst, T, w1, b1row, w2, b2row, lng, lnb, inner, adds):
    B = lambda i: mk.P8[:, 512 * i:512 * (i + 1)]
    BR = mk.p8_reg
    tc = TC(mk, "mlp")
    ngb = 1 if inner > 2048 else 2
    GB = [mk.sbuf(f"mlp_gb{i}", [128, inner], BF16)[:] for i in range(ngb)]
    GB = [GB[i % ngb] for i in range(2)]
    GB_reg = [Reg(), Reg()] if ngb == 2 else [Reg()] * 2
    GF = [mk.sbuf(f"mlp_gf{i}", [128, inner], BF16)[:] for i in range(ngb)]
    GF = [GF[i % ngb] for i in range(2)]
    GF_reg = [Reg(), Reg()] if ngb == 2 else [Reg()] * 2
    gs = {k: mk.sbuf("mlp_" + k, [128, 512], F32)[:] for k in ("ub", "x2", "x3")}
    OT = [mk.sbuf(f"mlp_ot{i}", [128, 1024], F32)[:] for i in range(ngb)]
    OT = [OT[i % ngb] for i in range(2)]
    OT_reg = [Reg(), Reg()] if ngb == 2 else [Reg()] * 2
    AD = [mk.sbuf(f"mlp_ad{i}", [128, 1024], F32)[:] for i in range(ngb)]
    AD = [AD[i % ngb] for i in range(2)]
    AD_reg = [Reg(), Reg()] if ngb == 2 else [Reg()] * 2
    nin = inner // 512
    for i in range(mk.TILES):
        ti = tc.tile_in(i, src, gamma=lng, beta=lnb, fm=True)
        sl = i % 2
        xnf = ti["xnf"]
        fmw = tuple(mk.wr_deps(ti["xnf_reg"]))
        gdeps = mk.wr_deps(GB_reg[sl])
        GB_reg[sl].wrs = []
        GB_reg[sl].rds = []
        for ib in range(nin):
            bk = ib % 3
            ft = mm_chain(mk, B(bk), [(xnf[:, 128 * c:128 * c + 128],
                                       w1.ck(c, 512 * ib, 512)) for c in range(NCH)],
                          [BR[bk]], wait=(fmw if ib == 0 else ()))
            mk.add_rd(ti["xnf_reg"], ft)
            gt = gelu_tanh(mk, B(bk), b1row[:, 512 * ib:512 * ib + 512],
                           GB[sl][:, 512 * ib:512 * ib + 512], gs, out_wait=gdeps,
                           in_wait=(ft,))
            mk.add_rd(BR[bk], gt)
            GB_reg[sl].wrs.append(gt)
        gft = transposes(mk, [(GB[sl][:, 128 * c:128 * c + 128],
                               GF[sl][:, 128 * c:128 * c + 128]) for c in range(inner // 128)],
                         wait=tuple(GB_reg[sl].wrs) + mk.wr_deps(GF_reg[sl]))
        GF_reg[sl].wrs = list(gft)
        GF_reg[sl].rds = []
        for t in gft:
            GB_reg[sl].rds.append(t)
        odeps = mk.wr_deps(OT_reg[sl])
        OT_reg[sl].wrs = []
        OT_reg[sl].rds = []
        addl = []
        for ai, a in enumerate(adds):
            if a == "src":
                addl.append((ti["rt"], ti["rt_tok"], ti["rt_reg"]))
            else:
                adeps = mk.wr_deps(AD_reg[sl])
                AD_reg[sl].wrs = []
                AD_reg[sl].rds = []
                lda = mk.dma("gpsimd", AD[sl][:, 0:1024], a[i * 128:(i + 1) * 128, :],
                             wait=adeps)
                AD_reg[sl].wrs.append(lda)
                addl.append((AD[sl][:, 0:1024], lda, AD_reg[sl]))
        for nb in range(2):
            wt = mm_chain(mk, B(3 + nb), [(GF[sl][:, 128 * c:128 * c + 128],
                                           w2.ck(c, 512 * nb, 512)) for c in range(inner // 128)],
                          [BR[3 + nb]], wait=tuple(gft))
            mk.add_rd(GF_reg[sl], wt)
            colr = slice(512 * nb, 512 * nb + 512)
            oe = mk.emit("vector", "scalar_tensor_tensor", OT[sl][:, colr], B(3 + nb),
                         1.0, b2row[:, colr], ALU.mult, ALU.add, wait=(wt,) + odeps, inc=1)
            mk.add_rd(BR[3 + nb], oe)
            for (aap, atok, areg) in addl:
                oe = mk.emit("vector", "tensor_add", OT[sl][:, colr], OT[sl][:, colr],
                             aap[:, colr] if aap.shape[1] > 512 else aap, wait=(atok,), inc=1)
                if areg is not None:
                    mk.add_rd(areg, oe)
            OT_reg[sl].wrs.append(oe)
        st = mk.dma("sync", dst[i * 128:(i + 1) * 128, :], OT[sl],
                    wait=tuple(OT_reg[sl].wrs))
        mk.add_rd(OT_reg[sl], st)



def sub_l1_attn_b(mk, src, dst):
    T = mk.T
    mk.begin_sub()
    KB = mk.sbuf("cKBUF", [128, NCH * T], BF16, scope="sub")[:]
    VB = mk.sbuf("cVBUF", [128, mk.TILES * 1040], BF16, scope="sub")[:]
    mk.begin_phase()
    wk = load_w(mk, "c_wk")
    bkc = load_f32(mk, "c_bk_col")
    lnkg = load_f32(mk, "c_lnkg_row")
    lnkb = load_f32(mk, "c_lnkb_row")
    mk.weights_ready()
    qkv_build_pass(mk, src, T, KB, VB, wk, None, bkc[:], None, lnkg[:], lnkb[:],
                   double=True, y_store=mk.dram["Y"], wtoks=(wk.tok,), only="K")
    mk.end_phase()
    mk.begin_phase()
    wv = load_w(mk, "c_wv")
    bvr = load_f32(mk, "c_bv_row")
    lnvg = load_f32(mk, "c_lnvg_row")
    lnvb = load_f32(mk, "c_lnvb_row")
    mk.weights_ready()
    mk.emit("vector", "memset", VB, 1.0, inc=1)
    qkv_build_pass(mk, src, T, KB, VB, None, wv, None, bvr[:], lnvg[:], lnvb[:],
                   double=True, wtoks=(wv.tok,), only="V")
    mk.end_phase()
    mk.begin_phase()
    wq = load_w(mk, "c_wq")
    bqc = load_f32(mk, "c_bq_col")
    lnqg = load_f32(mk, "c_lnqg_row")
    lnqb = load_f32(mk, "c_lnqb_row")
    mk.weights_ready()
    q_store_pass(mk, src, T, wq, bqc[:], lnqg[:], lnqb[:], double=True)
    mk.end_phase()
    mk.begin_phase()
    wo = load_w(mk, "c_wo")
    bor = load_f32(mk, "c_bo_row")
    mk.weights_ready()

    def pextra(i, h, slp, ptoks, PSB):
        if i != 0:
            return None
        o = slp * mk.psw + (T - 1)
        return mk.emit("vector", "tensor_scalar_mul", PSB[0:1, o:o + 1],
                       PSB[0:1, o:o + 1], 0.0, wait=tuple(ptoks), inc=1)

    attn_only_pass(mk, mk.dram["Y"], T, KB, VB, wo, bor[:], 1.0 / 32.0,
                   lambda i: T, pextra_fn=pextra,
                   kf=lambda h, k0, n: KB[(h % 2) * 64:(h % 2) * 64 + 64,
                                          T * (h // 2) + k0:T * (h // 2) + k0 + n],
                   vf=lambda kc, h: VB[:, 1040 * kc + 65 * h:1040 * kc + 65 * h + 65],
                   adds=(mk.dram["Y"],))
    mk.end_phase()
    mk.end_sub()
    mk.begin_sub()
    mk.begin_phase()
    w1 = load_w(mk, "c_w1")
    w2 = load_w(mk, "c_w2")
    b1r = load_f32(mk, "c_b1_row")
    b2r = load_f32(mk, "c_b2_row")
    lnmg = load_f32(mk, "c_lnmg_row")
    lnmb = load_f32(mk, "c_lnmb_row")
    mk.weights_ready()
    mlp_pass(mk, mk.dram["Y"], dst, T, w1, b1r[:], w2, b2r[:], lnmg[:], lnmb[:],
             2048, adds=["src", src])
    mk.end_phase()
    mk.end_sub()


SUBS["cb"] = sub_l1_attn_b



def sub_l0_attn_b(mk, src, dst):
    T = mk.T
    TE = 2048
    TE_TILES = 16
    mk.begin_sub()
    KB = mk.sbuf("bKBUF", [128, max(NCH * T, 16 * TE)], BF16, scope="sub")[:]
    VB = mk.sbuf("bVBUF", [128, max(mk.TILES, TE_TILES) * 1040], BF16, scope="sub")[:]
    Y = mk.dram["Y"]
    mk.begin_phase()
    wck = load_w_cols(mk, "b_cattn", 1024, 2048)
    bkc = load_f32(mk, "b_cbk_col")
    g1 = load_f32(mk, "b_ln1g_row")
    b1_ = load_f32(mk, "b_ln1b_row")
    mk.weights_ready()
    qkv_build_pass(mk, src, T, KB, VB, wck, None, bkc[:], None, g1[:], b1_[:],
                   double=True, y_store=Y, only="K")
    mk.end_phase()
    mk.begin_phase()
    wcv = load_w_cols(mk, "b_cattn", 2048, 3072)
    bvr = load_f32(mk, "b_cbv_row")
    g1v = load_f32(mk, "b_ln1g_row")
    b1v = load_f32(mk, "b_ln1b_row")
    mk.weights_ready()
    mk.emit("vector", "memset", VB, 1.0, inc=1)
    qkv_build_pass(mk, src, T, KB, VB, None, wcv, None, bvr[:], g1v[:], b1v[:],
                   double=True, only="V")
    mk.end_phase()
    mk.begin_phase()
    wcq = load_w_cols(mk, "b_cattn", 0, 1024)
    bqc = load_f32(mk, "b_cbq_col")
    g1b = load_f32(mk, "b_ln1g_row")
    b1b = load_f32(mk, "b_ln1b_row")
    mk.weights_ready()
    q_store_pass(mk, src, T, wcq, bqc[:], g1b[:], b1b[:], double=True)
    mk.end_phase()
    mk.begin_phase()
    wo = load_w(mk, "b_proj")
    bor = load_f32(mk, "b_projb_row")
    mk.weights_ready()
    attn_only_pass(mk, Y, T, KB, VB, wo, bor[:], 0.125, lambda i: 128 * (i + 1),
                   pmask_fn=lambda i, kc: mk.m_tril if kc == i else None,
                   kf=lambda h, k0, n: KB[(h % 2) * 64:(h % 2) * 64 + 64,
                                          T * (h // 2) + k0:T * (h // 2) + k0 + n],
                   vf=lambda kc, h: VB[:, 1040 * kc + 65 * h:1040 * kc + 65 * h + 65],
                   adds=(Y,))
    mk.end_phase()
    mk.begin_phase()
    enc = load_w(mk, "b_encT")
    wxk = load_w(mk, "b_xkvk")
    bkp = load_f32(mk, "b_xkvk_colp")
    mk.weights_ready()
    B = lambda i: mk.P8[:, 512 * i:512 * (i + 1)]
    BR = mk.p8_reg
    mset = mk.emit("vector", "memset", KB[:, 0:16 * TE], 0.0, inc=1)
    for h in range(16):
        for ks in range(TE // 512):
            bk = (h * 4 + ks) % 2
            kt = mm_chain(mk, B(bk), [(wxk.ck(c, o0=128 * h, n=128),
                                       enc.ck(c, o0=512 * ks, n=512)) for c in range(NCH)],
                          [BR[bk]], wait=(mset,))
            ke = mk.emit("vector", "tensor_scalar_add",
                         KB[0:64, TE * h + 512 * ks: TE * h + 512 * ks + 512],
                         B(bk)[0:64, :], bkp[0:64, h:h + 1], wait=(kt,), inc=1)
            mk.add_rd(BR[bk], ke)
    lg = mk.dma("gpsimd", KB[64:65, 0:16 * TE], mk.dram["b_logits"][:], wait=(mset,))
    mk.raw_wait("tensor", lg)
    mk.raw_wait("vector", lg)
    mk.end_phase()
    mk.begin_phase()
    enc2 = load_w(mk, "b_encT")
    wxv = load_w(mk, "b_xkvv")
    bvr2 = load_f32(mk, "b_xkvv_row")
    mk.weights_ready()
    mk.emit("vector", "memset", VB[:, 0:TE_TILES * 1040], 1.0, inc=1)
    for t in range(TE_TILES):
        for nb in range(2):
            bk = 2 + nb
            vt = mm_chain(mk, B(bk), [(enc2.ck(c, o0=128 * t, n=128),
                                       wxv.ck(c, o0=512 * nb, n=512)) for c in range(NCH)],
                          [BR[bk]], wait=())
            dv = VB[:, 1040 * t:1040 * (t + 1)].rearrange("p (h d) -> p h d", d=65)
            dv = dv[:, 8 * nb:8 * nb + 8, 0:64]
            sv = B(bk).rearrange("p (h d) -> p h d", d=64)
            bv3 = bvr2[:, 512 * nb:512 * nb + 512].rearrange("p (h d) -> p h d", d=64)
            ve = mk.emit("vector", "scalar_tensor_tensor", dv, sv, 1.0, bv3,
                         ALU.mult, ALU.add, wait=(vt,), inc=1)
            mk.add_rd(BR[bk], ve)
    mk.end_phase()
    mk.begin_phase()
    wq2 = load_w(mk, "b_qattn")
    bqp = load_f32(mk, "b_qb_colp")
    gx = load_f32(mk, "b_lnxg_row")
    bx = load_f32(mk, "b_lnxb_row")
    mk.weights_ready()
    q_store_pass(mk, Y, T, wq2, bqp[:], gx[:], bx[:], double=False, nco=16, padded=True)
    mk.end_phase()
    mk.begin_phase()
    wxo = load_w(mk, "b_xproj")
    bxr = load_f32(mk, "b_xpb_row")
    mk.weights_ready()

    def pmask(i, kc):
        if kc == i and i <= TE_TILES - 1:
            return mk.m_tril2
        if kc == i + 1 and i + 1 <= TE_TILES - 1:
            return mk.m_tril2b
        return None

    attn_only_pass(mk, Y, T, KB, VB, wxo, bxr[:], 0.125,
                   lambda i: 128 * min(TE_TILES, i + 2), pmask_fn=pmask,
                   qwidth=2048, qmode="full128",
                   kf=lambda h, k0, n: KB[:, TE * h + k0:TE * h + k0 + n],
                   vf=lambda kc, h: VB[:, 1040 * kc + 65 * h:1040 * kc + 65 * h + 65],
                   adds=(Y,), psw=2048)
    mk.end_phase()
    mk.end_sub()
    mk.begin_sub()
    mk.begin_phase()
    w1 = load_w(mk, "b_fc")
    w2 = load_w(mk, "b_mproj")
    b1r = load_f32(mk, "b_fcb_row")
    b2r = load_f32(mk, "b_mpb_row")
    g2 = load_f32(mk, "b_ln2g_row")
    b2_ = load_f32(mk, "b_ln2b_row")
    mk.weights_ready()
    mlp_pass(mk, Y, dst, T, w1, b1r[:], w2, b2r[:], g2[:], b2_[:], 4096,
             adds=["src", src])
    mk.end_phase()
    mk.end_sub()


SUBS["b1"] = sub_l0_attn_b



def sub_l1_attn_a(mk, src, dst):
    T = mk.T
    B = lambda i: mk.P8[:, 512 * i:512 * (i + 1)]
    BR = mk.p8_reg
    mk.begin_sub()
    mk.begin_phase()
    W = {n: load_w(mk, n) for n in ["wa_q", "wa_k", "wa_v", "wa_o"]}
    mk.weights_ready()
    tc = TC(mk, "wa")
    QW = [mk.sbuf(f"waQ{i}", [128, 1024], BF16)[:] for i in range(2)]
    QW_reg = [Reg(), Reg()]
    KW = [mk.sbuf(f"waK{i}", [128, 1024], BF16)[:] for i in range(2)]
    KW_reg = [Reg(), Reg()]
    VW = [mk.sbuf(f"waV{i}", [128, 1040], BF16)[:] for i in range(2)]
    VW_reg = [Reg(), Reg()]
    KG = mk.sbuf("waKG", [128, 512], BF16)[:]
    VG = mk.sbuf("waVG", [128, 1040], BF16)[:]
    KG_reg = Reg()
    VG_reg = Reg()
    PSW = mk.sbuf("waPS", [128, 256], BF16)[:]
    PSW_reg = Reg()
    PTS = mk.sbuf("waPTS", [128, 1024], BF16)[:]
    AT = [mk.sbuf(f"waAT{i}", [128, 1024], BF16)[:] for i in range(2)]
    AT_reg = [Reg(), Reg()]
    AF_ = [mk.sbuf(f"waAF{i}", [128, 1024], BF16)[:] for i in range(2)]
    AF_reg = [Reg(), Reg()]
    OT = [mk.sbuf(f"waOT{i}", [128, 1024], F32)[:] for i in range(2)]
    OT_reg = [Reg(), Reg()]
    sc = attn_scr(mk, "wasc")
    mk.emit("vector", "memset", PSW, 0.0, inc=1)
    for i in range(2):
        mk.emit("vector", "memset", VW[i], 1.0, inc=1)
    scale = 1.0 / 8.0
    vg_tok = None
    for j in range(mk.TILES):
        ti = tc.tile_in(j, src, fm=True)
        sl = ti["sl"]
        xnf = ti["xnf"]
        fmw = tuple(mk.wr_deps(ti["xnf_reg"]))
        for dest, wname, dreg in ((QW[sl], "wa_q", QW_reg[sl]), (KW[sl], "wa_k", KW_reg[sl])):
            ddeps = mk.wr_deps(dreg)
            dreg.wrs = []
            dreg.rds = []
            for nb in range(2):
                gts = []
                for q in range(4):
                    co = 4 * nb + q
                    gt = mm_chain(mk, B(nb)[:, 128 * q:128 * q + 128],
                                  [(W[wname].ck(c, o0=128 * co, n=128),
                                    xnf[:, 128 * c:128 * c + 128]) for c in range(NCH)],
                                  [BR[nb]] if q == 0 else [], wait=(fmw if q == 0 else ()))
                    gts.append(gt)
                mk.set_wr(BR[nb], gts[-1])
                mk.add_rd(ti["xnf_reg"], gts[-1])
                qe = mk.emit("vector", "tensor_copy", dest[:, 512 * nb:512 * nb + 512],
                             B(nb), wait=(gts[-1],) + ddeps, inc=1)
                mk.add_rd(BR[nb], qe)
                dreg.wrs.append(qe)
        vdeps = mk.wr_deps(VW_reg[sl])
        VW_reg[sl].wrs = []
        VW_reg[sl].rds = []
        for nb in range(2):
            vt = mm_chain(mk, B(2 + nb), [(xnf[:, 128 * c:128 * c + 128],
                                           W["wa_v"].ck(c, o0=512 * nb, n=512))
                                          for c in range(NCH)],
                          [BR[2 + nb]], wait=fmw)
            mk.add_rd(ti["xnf_reg"], vt)
            dv = VW[sl].rearrange("p (h d) -> p h d", d=65)[:, 8 * nb:8 * nb + 8, 0:64]
            sv = B(2 + nb).rearrange("p (h d) -> p h d", d=64)
            ve = mk.emit("vector", "tensor_copy", dv, sv, wait=(vt,) + vdeps, inc=1)
            mk.add_rd(BR[2 + nb], ve)
            VW_reg[sl].wrs.append(ve)
        if j == 0:
            kdeps = mk.wr_deps(KG_reg) + tuple(KW_reg[sl].wrs)
            kg_toks = []
            for c in range(NCH):
                kt = mk.emit("vector", "tensor_copy", KG[:, 64 * c:64 * c + 64],
                             KW[sl][:, 128 * c:128 * c + 64], wait=kdeps, inc=1)
                kg_toks.append(kt)
                mk.add_rd(KW_reg[sl], kt)
            KG_reg.wrs = kg_toks
            vg_tok = mk.emit("vector", "tensor_copy", VG, VW[sl],
                             wait=tuple(VW_reg[sl].wrs) + mk.wr_deps(VG_reg), inc=1)
            mk.set_wr(VG_reg, vg_tok)
            mk.add_rd(VW_reg[sl], vg_tok)
        at = AT[sl]
        at_deps = tuple(AT_reg[sl].rds)
        AT_reg[sl].rds = []
        C = sc[0]
        for h in range(NH):
            r0 = (h % 2) * 64
            cck = 128 * (h // 2)
            sdeps = mk.wr_deps(BR[4]) + tuple(KG_reg.wrs) + tuple(KW_reg[sl].wrs) + \
                tuple(QW_reg[sl].wrs)
            s1 = mk.emit("tensor", "matmul", B(4)[:, 0:64],
                         QW[sl][r0:r0 + 64, cck:cck + 128],
                         KG[r0:r0 + 64, 64 * (h // 2):64 * (h // 2) + 64],
                         start=True, stop=True, wait=sdeps)
            s2 = mk.emit("tensor", "matmul", B(4)[:, 128:256],
                         QW[sl][r0:r0 + 64, cck:cck + 128],
                         KW[sl][r0:r0 + 64, cck:cck + 128],
                         start=True, stop=True, inc=1)
            mk.set_wr(BR[4], s2)
            mk.add_rd(QW_reg[sl], s2)
            mk.add_rd(KW_reg[sl], s2)
            KG_reg.rds.append(s2)
            m1 = mk.emit("vector", "reduce_max", C["negm"], B(4)[:, 0:64], axis=AX.X,
                         negate=True, wait=(s2,))
            m2 = mk.emit("vector", "reduce_max", C["negm2"], B(4)[:, 128:256], axis=AX.X,
                         negate=True)
            mk.emit("vector", "tensor_tensor", C["negmT"], C["negm"], C["negm2"], ALU.min)
            bt = mk.emit("vector", "tensor_scalar_mul", C["bias"], C["negmT"], scale, inc=1)
            pdeps = mk.wr_deps(PSW_reg)
            e1 = mk.emit("scalar", "activation", PSW[:, 0:64], B(4)[:, 0:64], AF.Exp,
                         bias=C["bias"], scale=scale, wait=(bt, s2) + pdeps, inc=1)
            e2 = mk.emit("scalar", "activation", PSW[:, 128:256], B(4)[:, 128:256], AF.Exp,
                         bias=C["bias"], scale=scale, inc=1)
            PSW_reg.wrs = [e1, e2]
            PSW_reg.rds = []
            mk.add_rd(BR[4], m1)
            mk.add_rd(BR[4], m2)
            mk.add_rd(BR[4], e2)
            X = mk.ptb_idx % 2
            mk.ptb_idx += 1
            breg = mk.ptb_reg[X]
            t1 = mk.emit("tensor", "transpose", mk.PT[:, 1024 * X:1024 * X + 128],
                         PSW[:, 0:128], mk.ident, wait=(e1, e2) + mk.wr_deps(breg))
            t2 = mk.emit("tensor", "transpose", mk.PT[:, 1024 * X + 128:1024 * X + 256],
                         PSW[:, 128:256], mk.ident, inc=1)
            mk.set_wr(breg, t2)
            PSW_reg.rds.append(t2)
            et = mk.emit("vector", "tensor_copy", PTS[:, 0:256],
                         mk.PT[:, 1024 * X:1024 * X + 256],
                         wait=(t2,) + mk.wr_deps(mk.ptsb_reg[0]), inc=1)
            mk.add_rd(breg, et)
            mk.set_wr(mk.ptsb_reg[0], et)
            pv1 = mk.emit("tensor", "matmul", mk.PV[:, 0:65], PTS[:, 0:128],
                          VG[:, 65 * h:65 * h + 65], start=True, stop=False,
                          wait=(et, vg_tok) + mk.wr_deps(mk.pv_reg))
            pv2 = mk.emit("tensor", "matmul", mk.PV[:, 0:65], PTS[:, 128:256],
                          VW[sl][:, 65 * h:65 * h + 65], start=False, stop=True,
                          wait=tuple(VW_reg[sl].wrs), inc=1)
            mk.set_wr(mk.pv_reg, pv2)
            mk.add_rd(mk.ptsb_reg[0], pv2)
            VG_reg.rds.append(pv2)
            mk.add_rd(VW_reg[sl], pv2)
            rc = mk.emit("vector", "reciprocal", C["rs"], mk.PV[:, 64:65], wait=(pv2,))
            ot = mk.emit("vector", "tensor_scalar_mul", at[:, 64 * h:64 * h + 64],
                         mk.PV[:, 0:64], C["rs"], wait=(pv2,) + at_deps, inc=1)
            mk.add_rd(mk.pv_reg, ot)
            AT_reg[sl].wrs.append(ot)
        ot_toks = tuple(AT_reg[sl].wrs)
        AT_reg[sl].wrs = []
        aft = transposes(mk, [(at[:, 128 * c:128 * c + 128], AF_[sl][:, 128 * c:128 * c + 128])
                              for c in range(NCH)], wait=ot_toks + mk.wr_deps(AF_reg[sl]))
        AF_reg[sl].wrs = list(aft)
        AF_reg[sl].rds = []
        for t in aft:
            AT_reg[sl].rds.append(t)
        odeps = mk.wr_deps(OT_reg[sl])
        OT_reg[sl].wrs = []
        OT_reg[sl].rds = []
        for nb in range(2):
            wo_t = mm_chain(mk, B(3 + nb), [(AF_[sl][:, 128 * c:128 * c + 128],
                                             W["wa_o"].ck(c, 512 * nb, 512)) for c in range(NCH)],
                            [BR[3 + nb]], wait=tuple(aft))
            mk.add_rd(AF_reg[sl], wo_t)
            colr = slice(512 * nb, 512 * nb + 512)
            oe = mk.emit("vector", "scalar_tensor_tensor", OT[sl][:, colr], B(3 + nb),
                         1.0, ti["rt"][:, colr], ALU.mult, ALU.add,
                         wait=(wo_t, ti["rt_tok"]) + odeps, inc=1)
            mk.add_rd(BR[3 + nb], oe)
            mk.add_rd(ti["rt_reg"], oe)
            OT_reg[sl].wrs.append(oe)
        st = mk.dma("sync", dst[j * 128:(j + 1) * 128, :], OT[sl],
                    wait=tuple(OT_reg[sl].wrs))
        mk.add_rd(OT_reg[sl], st)
    mk.end_phase()
    mk.end_sub()


SUBS["wa"] = sub_l1_attn_a



def load_w_area(mk, area, dram_ap, nch, N, queue="sync", wait=()):
    tok = mk.dma(queue, area[:, 0:nch * N].rearrange("p (c n) -> p c n", c=nch),
                 dram_ap.rearrange("c p n -> p c n"), wait=wait)
    return WH(area, nch, N, tok)


def moe_quarters(mk, src_adds, dst, experts, wgt_ap_fn, I_of, tiles_per_q=None):
    B = lambda i: mk.P8[:, 512 * i:512 * (i + 1)]
    BR = mk.p8_reg
    T = mk.T
    tiles_per_q = tiles_per_q or min(8, mk.TILES)
    nq = mk.TILES // tiles_per_q
    vex = []
    for e in range(len(experts)):
        I = I_of(e)
        nparts = max(1, I // 1024)
        Ip = I // nparts
        for hh in range(nparts):
            vex.append((e, hh, Ip))
    NE = len(vex)
    Imax = max(v[2] for v in vex)
    WGa = [mk.sbuf(f"moe_g{i}", [128, 8 * Imax], BF16)[:] for i in range(2)]
    WUa = [mk.sbuf(f"moe_u{i}", [128, 8 * Imax], BF16)[:] for i in range(2)]
    WDa = [mk.sbuf(f"moe_d{i}", [128, (Imax // 128) * 1024], BF16)[:] for i in range(2)]
    XQ = mk.sbuf("moe_xq", [128, NCH * 128 * tiles_per_q], BF16)[:]
    AC = mk.sbuf("moe_ac", [128, 1024 * tiles_per_q], F32)[:]
    HF = mk.sbuf("moe_hf", [128, (Imax // 128) * 512], BF16)[:]
    GS = [mk.sbuf(f"moe_gs{i}", [128, 512], BF16)[:] for i in range(2)]
    AD = [mk.sbuf(f"moe_ad{i}", [128, 1024], F32)[:] for i in range(2)]
    AD_reg = [Reg(), Reg()]
    OTm = [mk.sbuf(f"moe_ot{i}", [128, 1024], F32)[:] for i in range(2)]
    OT_reg = [Reg(), Reg()]
    XQ_reg = Reg()
    AC_reg = Reg()
    HF_reg = Reg()
    GS_reg = [Reg(), Reg()]
    W_reg = [Reg(), Reg()]
    QTOK = 128 * tiles_per_q
    WH3 = [None, None]

    def load_exp(gi, sl):
        e, hh, Ip = vex[gi % NE]
        gd, ud, dd, kind = experts[e]
        nicp = Ip // 128
        wdeps = mk.wr_deps(W_reg[sl])
        W_reg[sl].wrs = []
        W_reg[sl].rds = []
        wg = load_w_area(mk, WGa[sl], gd[:, :, hh * Ip:(hh + 1) * Ip], NCH, Ip, wait=wdeps)
        wu = load_w_area(mk, WUa[sl], ud[:, :, hh * Ip:(hh + 1) * Ip], NCH, Ip, wait=wdeps)
        wd = load_w_area(mk, WDa[sl], dd[hh * nicp:(hh + 1) * nicp], nicp, 1024, wait=wdeps)
        W_reg[sl].wrs = [wg.tok, wu.tok, wd.tok]
        WH3[sl] = (wg, wu, wd)

    load_exp(0, 0)
    gidx = 0
    for qt in range(nq):
        xql = mk.dma("sync", XQ.rearrange("p (c t) -> p c t", c=NCH),
                     mk.dram["XNF"][:, :, qt * QTOK:(qt + 1) * QTOK],
                     wait=mk.wr_deps(XQ_reg))
        mk.set_wr(XQ_reg, xql)
        acdeps = mk.wr_deps(AC_reg)
        AC_reg.wrs = []
        AC_reg.rds = []
        for ve in range(NE):
            e, hh, Ip = vex[ve]
            sl = gidx % 2
            wg, wu, wd = WH3[sl]
            nic = Ip // 128
            if gidx + 1 < nq * NE:
                load_exp(gidx + 1, (gidx + 1) % 2)
            for tb in range(tiles_per_q // 4):
                hfdeps = mk.wr_deps(HF_reg)
                HF_reg.wrs = []
                HF_reg.rds = []
                for ic in range(nic):
                    bg = (ic % 2) * 2
                    gt = mm_chain(mk, B(bg), [(wg.ck(c, o0=128 * ic, n=128),
                                               XQ[:, c * QTOK + tb * 512: c * QTOK + tb * 512 + 512])
                                              for c in range(NCH)],
                                  [BR[bg]], wait=(wg.tok, xql))
                    ut = mm_chain(mk, B(bg + 1), [(wu.ck(c, o0=128 * ic, n=128),
                                                   XQ[:, c * QTOK + tb * 512: c * QTOK + tb * 512 + 512])
                                                  for c in range(NCH)],
                                  [BR[bg + 1]], wait=(wu.tok, xql))
                    mk.add_rd(XQ_reg, gt)
                    mk.add_rd(XQ_reg, ut)
                    gsl = ic % 2
                    sg = mk.emit("scalar", "activation", GS[gsl], B(bg), AF.Silu,
                                 wait=(gt,) + mk.wr_deps(GS_reg[gsl]), inc=1)
                    mk.add_rd(BR[bg], sg)
                    hm = mk.emit("vector", "tensor_mul", HF[:, 512 * ic:512 * ic + 512],
                                 B(bg + 1), GS[gsl], wait=(ut, sg) + hfdeps, inc=1)
                    mk.add_rd(BR[bg + 1], hm)
                    GS_reg[gsl].wrs = [sg]
                    GS_reg[gsl].rds = [hm]
                    HF_reg.wrs.append(hm)
                for tt in range(4):
                    gtile = qt * tiles_per_q + tb * 4 + tt
                    for nb in range(2):
                        bk = 2 + ((tt * 2 + nb) % 3)
                        dt = mm_chain(mk, B(bk),
                                      [(HF[:, 512 * ic + 128 * tt: 512 * ic + 128 * tt + 128],
                                        wd.ck(ic, o0=512 * nb, n=512)) for ic in range(nic)],
                                      [BR[bk]], wait=tuple(HF_reg.wrs) + (wd.tok,))
                        HF_reg.rds.append(dt)
                        wap = wgt_ap_fn(e, gtile)
                        colr = slice((tb * 4 + tt) * 1024 + 512 * nb,
                                     (tb * 4 + tt) * 1024 + 512 * nb + 512)
                        ev = "vector" if (tt + nb) % 2 == 0 else "scalar"
                        if ve == 0:
                            if wap is None:
                                ae = mk.emit(ev, "tensor_copy", AC[:, colr], B(bk),
                                             wait=(dt,) + acdeps, inc=1)
                            else:
                                ae = mk.emit("vector", "tensor_scalar_mul", AC[:, colr],
                                             B(bk), wap, wait=(dt,) + acdeps, inc=1)
                        else:
                            if wap is None:
                                ae = mk.emit("vector", "tensor_tensor", AC[:, colr], AC[:, colr],
                                             B(bk), ALU.add, wait=(dt,), inc=1)
                            else:
                                ae = mk.emit("vector", "scalar_tensor_tensor", AC[:, colr],
                                             B(bk), wap, AC[:, colr], ALU.mult, ALU.add,
                                             wait=(dt,), inc=1)
                        mk.add_rd(BR[bk], ae)
                        AC_reg.wrs.append(ae)
                        mk.add_rd(W_reg[sl], dt)
            gidx += 1
        for tt in range(tiles_per_q):
            gtile = qt * tiles_per_q + tt
            sl = tt % 2
            otdeps = mk.wr_deps(OT_reg[sl])
            OT_reg[sl].wrs = []
            OT_reg[sl].rds = []
            first = True
            prev = None
            for a, adram in enumerate(src_adds):
                ald = mk.dma("gpsimd", AD[sl], adram[gtile * 128:(gtile + 1) * 128, :],
                             wait=mk.wr_deps(AD_reg[sl]))
                mk.set_wr(AD_reg[sl], ald)
                if first:
                    oe = mk.emit("vector", "tensor_add", OTm[sl], AD[sl],
                                 AC[:, tt * 1024:(tt + 1) * 1024],
                                 wait=(ald,) + tuple(AC_reg.wrs) + otdeps, inc=1)
                    AC_reg.rds.append(oe)
                    first = False
                else:
                    oe = mk.emit("vector", "tensor_add", OTm[sl], OTm[sl], AD[sl],
                                 wait=(ald,), inc=1)
                mk.add_rd(AD_reg[sl], oe)
                prev = oe
            OT_reg[sl].wrs = [prev]
            st = mk.dma("sync", dst[gtile * 128:(gtile + 1) * 128, :], OTm[sl], wait=(prev,))
            mk.add_rd(OT_reg[sl], st)


def route_l1(mk, psum_ap, crow, wgt_dst, rs):
    BIG = 1.0e30
    S = rs
    mk.emit("scalar", "activation", S["sco"], psum_ap, AF.Sigmoid, wait=rs["_wait"])
    mk.emit("vector", "tensor_add", S["sb"], S["sco"], crow, wait=(mk.stok(),))
    sb3 = S["sb"].rearrange("p (g j) -> p g j", g=4)
    mk.emit("vector", "tensor_reduce", S["g4a"], sb3, axis=AX.X, op=ALU.max)
    g4r = S["tmp"].rearrange("p (g j) -> p g j", g=4)
    for j in range(4):
        mk.emit("vector", "tensor_copy", g4r[:, :, j:j + 1],
                S["g4a"].rearrange("p (g x) -> p g x", x=1))
    mk.emit("vector", "tensor_tensor", S["eq"], S["sb"], S["tmp"], ALU.is_ge)
    mk.emit("vector", "scalar_tensor_tensor", S["mb"], S["eq"], -BIG, S["sb"],
            ALU.mult, ALU.add)
    mk.emit("vector", "tensor_reduce", S["g4b"], S["mb"].rearrange("p (g j) -> p g j", g=4),
            axis=AX.X, op=ALU.max)
    mk.emit("vector", "tensor_add", S["g4a"], S["g4a"], S["g4b"])
    mk.emit("vector", "reduce_max", S["c1"], S["g4a"], axis=AX.X)
    mk.emit("vector", "tensor_single_scalar", S["g4b"], S["g4a"], S["c1"], ALU.is_ge)
    mk.emit("vector", "scalar_tensor_tensor", S["g4b"], S["g4b"], -BIG, S["g4a"],
            ALU.mult, ALU.add)
    mk.emit("vector", "reduce_max", S["c1"], S["g4b"], axis=AX.X)
    mk.emit("vector", "tensor_single_scalar", S["g4b"], S["g4a"], S["c1"], ALU.is_ge)
    for j in range(4):
        mk.emit("vector", "tensor_copy", g4r[:, :, j:j + 1],
                S["g4b"].rearrange("p (g x) -> p g x", x=1))
    mk.emit("vector", "tensor_scalar", S["tmp"], S["tmp"], BIG, -BIG, ALU.mult, ALU.add)
    mk.emit("vector", "tensor_add", S["msb"], S["sb"], S["tmp"])
    mk.emit("vector", "tensor_copy", S["msb0"], S["msb"])
    for _ in range(3):
        mk.emit("vector", "reduce_max", S["c1"], S["msb"], axis=AX.X)
        mk.emit("vector", "tensor_single_scalar", S["eq"], S["msb"], S["c1"], ALU.is_ge)
        mk.emit("vector", "scalar_tensor_tensor", S["msb"], S["eq"], -BIG, S["msb"],
                ALU.mult, ALU.add)
    mk.emit("vector", "reduce_max", S["c1"], S["msb"], axis=AX.X)
    mk.emit("vector", "tensor_single_scalar", S["eq"], S["msb0"], S["c1"], ALU.is_ge)
    mk.emit("vector", "tensor_mul", S["eq"], S["eq"], S["sco"])
    mk.emit("vector", "reduce_sum", S["c1"], S["eq"], axis=AX.X)
    mk.emit("vector", "tensor_scalar_add", S["c1"], S["c1"], 1.0e-20)
    mk.emit("vector", "reciprocal", S["c1"], S["c1"])
    tok = mk.emit("vector", "tensor_scalar_mul", wgt_dst, S["eq"], S["c1"], inc=1)
    return tok


def route_scr(mk, name):
    h = mk.sbuf(name, [128, 128], F32)[:]
    return {"sco": h[:, 0:16], "sb": h[:, 16:32], "eq": h[:, 32:48], "mb": h[:, 48:64],
            "msb": h[:, 64:80], "msb0": h[:, 80:96], "tmp": h[:, 96:112],
            "g4a": h[:, 112:116], "g4b": h[:, 116:120], "c1": h[:, 120:121], "_wait": ()}


def route_l0(mk, psum_ap, brow, wgt_dst, rs):
    BIG = 1.0e30
    S = rs
    mk.emit("vector", "scalar_tensor_tensor", S["sco"][:, 0:7], psum_ap, 1.0, brow,
            ALU.mult, ALU.add, wait=rs["_wait"])
    mk.emit("scalar", "activation", S["sb"][:, 0:7], S["sco"][:, 0:7], AF.Sigmoid,
            wait=(mk.vtok(),))
    prb = S["sb"][:, 0:7]
    mk.emit("vector", "reduce_max", S["c1"], prb, axis=AX.X, wait=(mk.stok(),))
    mk.emit("vector", "tensor_single_scalar", S["eq"][:, 0:7], prb, S["c1"], ALU.is_ge)
    mk.emit("vector", "scalar_tensor_tensor", S["mb"][:, 0:7], S["eq"][:, 0:7], -BIG, prb,
            ALU.mult, ALU.add)
    mk.emit("vector", "reduce_max", S["c1"], S["mb"][:, 0:7], axis=AX.X)
    mk.emit("vector", "tensor_single_scalar", S["eq"][:, 0:7], prb, S["c1"], ALU.is_ge)
    mk.emit("vector", "tensor_mul", S["eq"][:, 0:7], S["eq"][:, 0:7], prb)
    mk.emit("vector", "reduce_sum", S["c1"], S["eq"][:, 0:7], axis=AX.X)
    mk.emit("vector", "reciprocal", S["c1"], S["c1"])
    tok = mk.emit("vector", "tensor_scalar_mul", wgt_dst, S["eq"][:, 0:7], S["c1"], inc=1)
    return tok


def rope_heads(mk, x_psum, crb_tile, dst):
    x3 = x_psum.rearrange("p (h d) -> p h d", d=32)
    x1 = x3[:, :, 0:16]
    x2 = x3[:, :, 16:32]
    cos3 = crb_tile[:, 0:256].rearrange("p (h d) -> p h d", d=16)
    sin3 = crb_tile[:, 256:512].rearrange("p (h d) -> p h d", d=16)
    sc = mk.sb["ropescr"]
    m1 = sc[:, 0:256].rearrange("p (h d) -> p h d", d=16)
    m2 = sc[:, 256:512].rearrange("p (h d) -> p h d", d=16)
    d3 = dst.rearrange("p (h d) -> p h d", d=64)
    o1 = d3[:, :, 0:16]
    o2 = d3[:, :, 16:32]
    mk.emit("vector", "tensor_mul", m1, x1, cos3)
    mk.emit("vector", "tensor_mul", m2, x2, sin3)
    mk.emit("vector", "tensor_sub", o1, m1, m2)
    mk.emit("vector", "tensor_mul", m1, x2, cos3)
    mk.emit("vector", "tensor_mul", m2, x1, sin3)
    tok = mk.emit("vector", "tensor_add", o2, m1, m2, inc=1)
    return tok



def sub_l1_ffn(mk, src, dst):
    T = mk.T
    B = lambda i: mk.P8[:, 512 * i:512 * (i + 1)]
    BR = mk.p8_reg
    mk.begin_sub()
    WGT = mk.sbuf("gWGT", [128, mk.TILES * 16], F32, scope="sub")[:]
    mk.begin_phase()
    wrt = load_w(mk, "g_rt")
    crow = load_f32(mk, "g_cb_row")
    mk.weights_ready()
    tc = TC(mk, "gx")
    rs = route_scr(mk, "grs")
    for j in range(mk.TILES):
        ti = tc.tile_in(j, src, fm=True)
        sl = ti["sl"]
        xnf = ti["xnf"]
        fmw = tuple(mk.wr_deps(ti["xnf_reg"]))
        st = mk.dma("sync", mk.dram["XNF"][:, :, j * 128:(j + 1) * 128],
                    xnf.rearrange("p (c t) -> p c t", c=NCH), wait=fmw)
        mk.add_rd(ti["xnf_reg"], st)
        lt = mm_chain(mk, B(4)[:, 0:16], [(xnf[:, 128 * c:128 * c + 128], wrt.ck(c))
                                          for c in range(NCH)],
                      [BR[4]], wait=fmw + (wrt.tok,))
        mk.add_rd(ti["xnf_reg"], lt)
        rs["_wait"] = (lt,)
        rt = route_l1(mk, B(4)[:, 0:16], crow[:], WGT[:, 16 * j:16 * j + 16], rs)
        mk.add_rd(BR[4], rt)
    mk.end_phase()
    mk.begin_phase()
    mk.weights_ready()
    experts = [(mk.dram["bf_g_gw"][e], mk.dram["bf_g_uw"][e], mk.dram["bf_g_dw"][e],
                "routed") for e in range(16)]
    experts.append((mk.dram["bf_g_sg"], mk.dram["bf_g_su"], mk.dram["bf_g_sd"], "shared"))

    def wgt_fn(e, gtile):
        if e == 16:
            return None
        return WGT[:, 16 * gtile + e:16 * gtile + e + 1]

    moe_quarters(mk, [src], dst, experts, wgt_fn, lambda e: 256 if e < 16 else 2048)
    mk.end_phase()
    mk.end_sub()


SUBS["g1"] = sub_l1_ffn
SUBS["g2"] = sub_l1_ffn



def sub_l0_ffn(mk, src, dst):
    T = mk.T
    B = lambda i: mk.P8[:, 512 * i:512 * (i + 1)]
    BR = mk.p8_reg
    Y = mk.dram["Y"]
    Y1 = mk.dram["Y1"]
    mk.begin_sub()
    KB = mk.sbuf("fKBUF", [128, NCH * T], BF16, scope="sub")[:]
    VB = mk.sbuf("fVBUF", [128, mk.TILES * 1040], BF16, scope="sub")[:]
    mk.begin_phase()
    Wp = {n: load_w(mk, n) for n in ["f_kvd", "f_ku", "f_vu", "f_rk"]}
    ln1 = load_f32(mk, "f_ln1_row")
    mk.weights_ready()
    tc = TC(mk, "fp1")
    xw = mk.sbuf("f_xw", [128, D], F32)[:]
    xw_reg = Reg()
    mk.sbuf("ropescr", [128, 512], F32)
    ropek = mk.sbuf("f_ropek", [128, 1024], BF16)[:]
    ropek_reg = Reg()
    crb = [mk.sbuf(f"f_crb{i}", [128, 512], F32)[:] for i in range(2)]
    crb_reg = [Reg(), Reg()]
    kvl = [mk.sbuf(f"f_kvl{i}", [128, 256], BF16)[:] for i in range(2)]
    kvl_reg = [Reg(), Reg()]
    mk.emit("vector", "memset", VB, 1.0, inc=1)
    for j in range(mk.TILES):
        ti = tc.tile_in(j, src, gamma=ln1[:], fm=True, double=True, xin=xw)
        sl = ti["sl"]
        xnf = ti["xnf"]
        ya = ti["ln"]["ya_tok"]
        yst = mk.dma("sync", Y[j * 128:(j + 1) * 128, :], xw, wait=(ya,) + mk.wr_deps(xw_reg))
        mk.set_wr(xw_reg, ya)
        mk.add_rd(xw_reg, yst)
        fmw = tuple(mk.wr_deps(ti["xnf_reg"]))
        cld = mk.dma("gpsimd", crb[sl], mk.dram["crB"][j * 128:(j + 1) * 128, :],
                     wait=mk.wr_deps(crb_reg[sl]))
        mk.set_wr(crb_reg[sl], cld)
        kdeps = mk.wr_deps(kvl_reg[sl])
        kvl_reg[sl].wrs = []
        kvl_reg[sl].rds = []
        kts = []
        for lc in range(2):
            kt = mm_chain(mk, B(4)[:, 128 * lc:128 * lc + 128],
                          [(Wp["f_kvd"].ck(c, o0=128 * lc, n=128),
                            xnf[:, 128 * c:128 * c + 128]) for c in range(NCH)],
                          [BR[4]] if lc == 0 else [], wait=(fmw if lc == 0 else ()))
            kts.append(kt)
            mk.add_rd(ti["xnf_reg"], kt)
        mk.set_wr(BR[4], kts[-1])
        ke = mk.emit("vector", "tensor_copy", kvl[sl], B(4)[:, 0:256],
                     wait=(kts[-1],) + kdeps, inc=1)
        mk.add_rd(BR[4], ke)
        kvl_reg[sl].wrs = [ke]
        for nb in range(2):
            gts = []
            for q in range(4):
                co = 4 * nb + q
                gt = mm_chain(mk, B(nb)[:, 128 * q:128 * q + 128],
                              [(Wp["f_ku"].ck(lc, o0=128 * co, n=128),
                                kvl[sl][:, 128 * lc:128 * lc + 128]) for lc in range(2)],
                              [BR[nb]] if q == 0 else [], wait=((ke,) if q == 0 else ()))
                gts.append(gt)
            mk.set_wr(BR[nb], gts[-1])
            kvl_reg[sl].rds.append(gts[-1])
            for q in range(4):
                co = 4 * nb + q
                ket = mk.emit("vector", "tensor_copy",
                              KB[:, co * T + 128 * j: co * T + 128 * j + 128],
                              B(nb)[:, 128 * q:128 * q + 128], wait=(gts[-1],), inc=1)
                mk.add_rd(BR[nb], ket)
        for nb in range(2):
            vt = mm_chain(mk, B(2 + nb), [(kvl[sl][:, 128 * lc:128 * lc + 128],
                                           Wp["f_vu"].ck(lc, o0=512 * nb, n=512))
                                          for lc in range(2)],
                          [BR[2 + nb]], wait=(ke,))
            kvl_reg[sl].rds.append(vt)
            dv = VB[:, 1040 * j:1040 * (j + 1)].rearrange("p (h d) -> p h d", d=65)
            dv = dv[:, 8 * nb:8 * nb + 8, 0:64]
            sv = B(2 + nb).rearrange("p (h d) -> p h d", d=64)
            ve = mk.emit("vector", "tensor_copy", dv, sv, wait=(vt,), inc=1)
            mk.add_rd(BR[2 + nb], ve)
        krt = mm_chain(mk, B(0), [(xnf[:, 128 * c:128 * c + 128], Wp["f_rk"].ck(c))
                                  for c in range(NCH)],
                       [BR[0]], wait=fmw)
        mk.add_rd(ti["xnf_reg"], krt)
        mk.raw_wait("vector", krt)
        mk.raw_wait("vector", cld)
        for t in mk.wr_deps(ropek_reg):
            mk.raw_wait("vector", t)
        rtok = rope_heads(mk, B(0), crb[sl], ropek)
        mk.add_rd(BR[0], rtok)
        crb_reg[sl].rds = [rtok]
        pairs = []
        for c in range(NCH):
            pairs.append((ropek[:, 128 * c:128 * c + 128],
                          [(0, 32, KB[0:32, c * T + 128 * j: c * T + 128 * j + 128]),
                           (64, 96, KB[64:96, c * T + 128 * j: c * T + 128 * j + 128])]))
        ktt = transposes(mk, pairs, wait=(rtok,))
        ropek_reg.wrs = []
        ropek_reg.rds = list(ktt)
    mk.end_phase()
    mk.begin_phase()
    Wq = {n: load_w(mk, n) for n in ["f_qd", "f_qu", "f_rq"]}
    ln1b = load_f32(mk, "f_ln1_row")
    mk.weights_ready()
    tc2 = TC(mk, "fp2")
    mk.sbuf("ropescr", [128, 512], F32)
    ropeq = mk.sbuf("f_ropeq", [128, 1024], BF16)[:]
    ropeq_reg = Reg()
    crb2 = [mk.sbuf(f"f_crb2{i}", [128, 512], F32)[:] for i in range(2)]
    crb2_reg = [Reg(), Reg()]
    qvl = [mk.sbuf(f"f_qvl{i}", [128, 256], BF16)[:] for i in range(2)]
    qvl_reg = [Reg(), Reg()]
    QA = [mk.sbuf(f"fQA{i}", [128, 1024], BF16)[:] for i in range(2)]
    QA_reg = [Reg(), Reg()]
    for i in range(mk.TILES):
        ti = tc2.tile_in(i, src, gamma=ln1b[:], fm=True, double=True)
        sl = ti["sl"]
        xnf = ti["xnf"]
        fmw = tuple(mk.wr_deps(ti["xnf_reg"]))
        cld = mk.dma("gpsimd", crb2[sl], mk.dram["crB"][i * 128:(i + 1) * 128, :],
                     wait=mk.wr_deps(crb2_reg[sl]))
        mk.set_wr(crb2_reg[sl], cld)
        qdeps0 = mk.wr_deps(qvl_reg[sl])
        qvl_reg[sl].wrs = []
        qvl_reg[sl].rds = []
        kts = []
        for lc in range(2):
            kt = mm_chain(mk, B(4)[:, 128 * lc:128 * lc + 128],
                          [(Wq["f_qd"].ck(c, o0=128 * lc, n=128),
                            xnf[:, 128 * c:128 * c + 128]) for c in range(NCH)],
                          [BR[4]] if lc == 0 else [], wait=(fmw if lc == 0 else ()))
            kts.append(kt)
            mk.add_rd(ti["xnf_reg"], kt)
        mk.set_wr(BR[4], kts[-1])
        ke = mk.emit("vector", "tensor_copy", qvl[sl], B(4)[:, 0:256],
                     wait=(kts[-1],) + qdeps0, inc=1)
        mk.add_rd(BR[4], ke)
        qvl_reg[sl].wrs = [ke]
        qadeps = mk.wr_deps(QA_reg[sl])
        QA_reg[sl].wrs = []
        QA_reg[sl].rds = []
        for nb in range(2):
            gts = []
            for q in range(4):
                co = 4 * nb + q
                gt = mm_chain(mk, B(nb)[:, 128 * q:128 * q + 128],
                              [(Wq["f_qu"].ck(lc, o0=128 * co, n=128),
                                qvl[sl][:, 128 * lc:128 * lc + 128]) for lc in range(2)],
                              [BR[nb]] if q == 0 else [], wait=((ke,) if q == 0 else ()))
                gts.append(gt)
            mk.set_wr(BR[nb], gts[-1])
            qvl_reg[sl].rds.append(gts[-1])
            qe = mk.emit("vector", "tensor_copy", QA[sl][:, 512 * nb:512 * nb + 512],
                         B(nb), wait=(gts[-1],) + qadeps, inc=1)
            mk.add_rd(BR[nb], qe)
            QA_reg[sl].wrs.append(qe)
        qrt = mm_chain(mk, B(0), [(qvl[sl][:, 128 * lc:128 * lc + 128], Wq["f_rq"].ck(lc))
                                  for lc in range(2)],
                       [BR[0]], wait=(ke,))
        qvl_reg[sl].rds.append(qrt)
        mk.raw_wait("vector", qrt)
        mk.raw_wait("vector", cld)
        for t in mk.wr_deps(ropeq_reg):
            mk.raw_wait("vector", t)
        rtok = rope_heads(mk, B(0), crb2[sl], ropeq)
        mk.add_rd(BR[0], rtok)
        crb2_reg[sl].rds = [rtok]
        pairs = []
        for c in range(NCH):
            pairs.append((ropeq[:, 128 * c:128 * c + 128],
                          [(0, 32, QA[sl][0:32, 128 * c:128 * c + 128]),
                           (64, 96, QA[sl][64:96, 128 * c:128 * c + 128])]))
        qtt = transposes(mk, pairs, wait=(rtok,) + tuple(QA_reg[sl].wrs))
        ropeq_reg.wrs = []
        ropeq_reg.rds = list(qtt)
        QA_reg[sl].wrs += list(qtt)
        st = mk.dma("sync", mk.dram["QS"][i * 128:(i + 1) * 128, 0:1024], QA[sl],
                    wait=tuple(QA_reg[sl].wrs))
        mk.add_rd(QA_reg[sl], st)
    mk.end_phase()
    mk.begin_phase()
    wo = load_w(mk, "f_ow")
    mk.weights_ready()
    attn_only_pass(mk, Y1, T, KB, VB, wo, None, 0.125, lambda i: 128 * (i + 1),
                   pmask_fn=lambda i, kc: mk.m_tril if kc == i else None,
                   kf=lambda h, k0, n: KB[(h % 2) * 64:(h % 2) * 64 + 64,
                                          T * (h // 2) + k0:T * (h // 2) + k0 + n],
                   vf=lambda kc, h: VB[:, 1040 * kc + 65 * h:1040 * kc + 65 * h + 65],
                   adds=(Y,))
    mk.end_phase()
    mk.end_sub()
    mk.begin_sub()
    WGT = mk.sbuf("fWGT", [128, mk.TILES * 8], F32, scope="sub")[:]
    mk.begin_phase()
    wrt = load_w(mk, "f_rt")
    brow = load_f32(mk, "f_rb_row")
    ln2 = load_f32(mk, "f_ln2_row")
    mk.weights_ready()
    tc3 = TC(mk, "fp3")
    rs = route_scr(mk, "frs")
    for j in range(mk.TILES):
        ti = tc3.tile_in(j, Y1, gamma=ln2[:], fm=True)
        sl = ti["sl"]
        xnf = ti["xnf"]
        fmw = tuple(mk.wr_deps(ti["xnf_reg"]))
        st = mk.dma("sync", mk.dram["XNF"][:, :, j * 128:(j + 1) * 128],
                    xnf.rearrange("p (c t) -> p c t", c=NCH), wait=fmw)
        mk.add_rd(ti["xnf_reg"], st)
        lt = mm_chain(mk, B(4)[:, 0:7], [(xnf[:, 128 * c:128 * c + 128], wrt.ck(c))
                                         for c in range(NCH)],
                      [BR[4]], wait=fmw + (wrt.tok,))
        mk.add_rd(ti["xnf_reg"], lt)
        rs["_wait"] = (lt,)
        rt = route_l0(mk, B(4)[:, 0:7], brow[:], WGT[:, 8 * j:8 * j + 7], rs)
        mk.add_rd(BR[4], rt)
    mk.end_phase()
    mk.begin_phase()
    mk.weights_ready()
    experts = [(mk.dram["bf_f_rg"][e], mk.dram["bf_f_ru"][e], mk.dram["bf_f_rd"][e],
                "routed") for e in range(7)]
    experts.append((mk.dram["bf_f_sg"], mk.dram["bf_f_su"], mk.dram["bf_f_sd"], "shared"))

    def wgt_fn(e, gtile):
        if e == 7:
            return None
        return WGT[:, 8 * gtile + e:8 * gtile + e + 1]

    moe_quarters(mk, [src, Y1], dst, experts, wgt_fn, lambda e: 2048)
    mk.end_phase()
    mk.end_sub()


SUBS["f1"] = sub_l0_ffn
SUBS["f2"] = sub_l0_ffn


# version 06 (final)
# speedup vs baseline: 1.0852x; 1.0852x over previous
import math
import numpy as np
import ml_dtypes
from contextlib import ExitStack

import concourse.bass as bass
import concourse.mybir as mybir
from concourse.bass_utils import run_bass_kernel_spmd

F32 = mybir.dt.float32
BF16 = mybir.dt.bfloat16
AF = mybir.ActivationFunctionType
ALU = mybir.AluOpType
AX = mybir.AxisListType

D = 1024
NCH = 8
NH = 16
HD = 64
EPS = 1e-5
NEG = -1.0e30

SCAP_DEFAULT = 2560


ENGINES = ("tensor", "vector", "scalar", "gpsimd", "sync")
SEMS = ("t", "v", "s", "g", "dsy", "dgp", "dsc", "bar")
ENG_SEM = {"tensor": "t", "vector": "v", "scalar": "s", "gpsimd": "g"}
DMA_SEM = {"sync": "dsy", "gpsimd": "dgp", "scalar": "dsc"}


class Reg:
    __slots__ = ("wrs", "rds")

    def __init__(self):
        self.wrs = []
        self.rds = []


class MK:
    def __init__(self, nc, T):
        self.nc = nc
        self.T = T
        self.TILES = T // 128
        self.q = {e: [] for e in ENGINES}
        self.cnt = {s: 0 for s in SEMS}
        self.hw = {e: {} for e in ENGINES}
        self.nbar = 0
        self.es = ExitStack()
        self.phase_es = None
        self.dram = {}
        self.sub_es = None
        self.sb = {}
        self.regs = {}
        self.p8_reg = [Reg() for _ in range(5)]
        self.pv_reg = Reg()

    def emit(self, eng, method, *args, wait=(), inc=0, **kwargs):
        if eng == "scalar" and method == "tensor_copy":
            method = "copy"
        is_dma = method == "dma_start"
        semname = DMA_SEM[eng] if is_dma else ENG_SEM.get(eng)
        wait = tuple(wait)
        if not is_dma and eng in ("vector", "scalar", "gpsimd"):
            if self.cnt[semname] > 0:
                wait = wait + ((eng, semname, self.cnt[semname]),)
            inc = max(inc, 1)
        waits = []
        for tok in wait:
            if tok is None:
                continue
            peng, ps, pv = tok
            if peng == eng and not ps.startswith("d") and peng == "tensor":
                continue
            if self.hw[eng].get(ps, -1) >= pv:
                continue
            self.hw[eng][ps] = pv
            waits.append((ps, pv))
        incs = []
        tok = None
        if inc:
            self.cnt[semname] += inc
            incs.append((semname, inc))
            tok = (eng, semname, self.cnt[semname])
        self.q[eng].append((method, args, kwargs, tuple(waits), tuple(incs)))
        return tok

    def weights_ready(self):
        for eng in ENGINES:
            self.raw_wait(eng, ("sync", "dsy", self.cnt["dsy"]))

    def vtok(self):
        return ("vector", "v", self.cnt["v"])

    def stok(self):
        return ("scalar", "s", self.cnt["s"])

    def gtok(self):
        return ("gpsimd", "g", self.cnt["g"])

    def ttok(self):
        return ("tensor", "t", self.cnt["t"])

    def dma(self, queue, out, in_, wait=(), **kw):
        sem = DMA_SEM[queue]
        chain = ((queue, sem, self.cnt[sem]),) if self.cnt[sem] > 0 else ()
        return self.emit(queue, "dma_start", wait=tuple(wait) + chain, inc=16,
                         out=out, in_=in_, **kw)

    def raw_wait(self, eng, tok):
        if tok is None:
            return
        peng, ps, pv = tok
        if peng == eng and not ps.startswith("d"):
            return
        if self.hw[eng].get(ps, -1) >= pv:
            return
        self.hw[eng][ps] = pv
        self.q[eng].append(("__wait__", (), {}, ((ps, pv),), ()))

    def reg(self, name):
        if name not in self.regs:
            self.regs[name] = Reg()
        return self.regs[name]

    @staticmethod
    def wr_deps(r):
        return tuple(r.rds) + tuple(r.wrs)

    @staticmethod
    def rd_deps(r):
        return tuple(r.wrs)

    @staticmethod
    def set_wr(r, tok, reset=True):
        if reset:
            r.wrs = []
            r.rds = []
        if tok is not None:
            r.wrs.append(tok)

    @staticmethod
    def add_rd(r, tok):
        if tok is not None:
            r.rds.append(tok)

    def barrier(self):
        self.nbar += 1
        target = 5 * self.nbar
        for eng in ENGINES:
            w = []
            own = ENG_SEM.get(eng)
            if own is not None and self.cnt[own] > 0:
                w.append((own, self.cnt[own]))
            ds = DMA_SEM.get(eng)
            if ds is not None and self.cnt[ds] > 0:
                w.append((ds, self.cnt[ds]))
            self.cnt["bar"] += 1
            self.q[eng].append(("__barrier__", (), {}, tuple(w), (("bar", 1),)))
            self.q[eng].append(("__wait__", (), {}, (("bar", target),), ()))
            self.hw[eng]["bar"] = target
            if ds is not None:
                self.hw[eng][ds] = max(self.hw[eng].get(ds, -1), self.cnt[ds])
        regsets = list(self.regs.values()) + self.p8_reg + [self.pv_reg]
        regsets += getattr(self, "ptb_reg", []) + getattr(self, "ptsb_reg", [])
        for r in regsets:
            r.wrs = []
            r.rds = []

    def dump(self, name, ap, wait=()):
        if not getattr(self, "debug", False):
            return
        shp = list(ap.shape)
        d = self.nc.dram_tensor("dbg_" + name, shp, ap.dtype, kind="ExternalOutput").ap()
        self.dma("sync", d[:], ap, wait=tuple(wait))

    def sbuf(self, name, shape, dtype, phase=True, scope=None):
        if scope == "sub":
            es = self.sub_es
        elif scope == "global" or (not phase) or self.phase_es is None:
            es = self.es
        else:
            es = self.phase_es
        self._uid = getattr(self, "_uid", 0) + 1
        h = es.enter_context(self.nc.sbuf_tensor(f"{name}__{self._uid}", list(shape), dtype))
        self.sb[name] = h
        return h

    def begin_sub(self):
        assert getattr(self, "sub_es", None) is None
        self.sub_es = ExitStack()

    def end_sub(self):
        assert self.phase_es is None
        self.sub_es.close()
        self.sub_es = None

    def begin_phase(self):
        assert self.phase_es is None
        self.phase_es = ExitStack()

    def end_phase(self):
        self.phase_es.close()
        self.phase_es = None
        self.barrier()

    def finalize(self):
        nc = self.nc
        semh = {s: self.es.enter_context(nc.semaphore(s)) for s in SEMS}
        block = self.es.enter_context(nc.Block())

        def make_body(ops):
            def body(eng):
                for method, args, kwargs, waits, incs in ops:
                    if method == "__wait__":
                        for s, v in waits:
                            eng.wait_ge(semh[s], v)
                        continue
                    for s, v in waits:
                        eng.wait_ge(semh[s], v)
                    if method == "__barrier__":
                        inst = eng.sem_inc(semh["bar"], 1)
                        continue
                    inst = getattr(eng, method)(*args, **kwargs)
                    for s, k in incs:
                        inst.then_inc(semh[s], k)
            return body

        block.tensor(make_body(self.q["tensor"]))
        block.vector(make_body(self.q["vector"]))
        block.scalar(make_body(self.q["scalar"]))
        block.gpsimd(make_body(self.q["gpsimd"]))
        block.sync(make_body(self.q["sync"]))

    def close(self):
        self.es.close()



def _chunks(w):
    K, N = w.shape
    assert K % 128 == 0
    return np.ascontiguousarray(w.reshape(K // 128, 128, N).astype(np.float32))


def _chunksT(w):
    return _chunks(np.ascontiguousarray(w.T))


def _col(b):
    N = b.shape[0]
    assert N % 128 == 0
    return np.ascontiguousarray(b.reshape(N // 128, 128).T.astype(np.float32))


def _row(b, p=128):
    a = np.asarray(b, np.float32).astype(ml_dtypes.bfloat16)
    return np.ascontiguousarray(np.broadcast_to(a[None, :], (p, int(a.shape[0]))))


def _colpad64(b):
    out = np.zeros((128, 16), np.float32)
    for h in range(16):
        out[0:64, h] = b[64 * h:64 * h + 64]
    return out


def _padheads(w):
    K = w.shape[0]
    out = np.zeros((K, 2048), np.float32)
    for h in range(16):
        out[:, 128 * h:128 * h + 64] = w[:, 64 * h:64 * h + 64]
    return out


def host_consts(T):
    c = {}
    i128 = np.arange(128)
    c["ident"] = np.eye(128, dtype=np.float32).astype(ml_dtypes.bfloat16)
    c["m_tril"] = (i128[None, :] <= i128[:, None]).astype(ml_dtypes.bfloat16)
    c["m_tril2"] = (i128[None, :] <= i128[:, None] + 2).astype(ml_dtypes.bfloat16)
    c["m_tril2b"] = (i128[None, :] <= i128[:, None] - 126).astype(ml_dtypes.bfloat16)
    inv = 1.0 / 10000.0 ** (np.arange(0, 32, 2, dtype=np.float64) / 32)
    t = np.arange(T, dtype=np.float64)
    fr = np.outer(t, inv)
    c["cosA"] = np.cos(fr).astype(np.float32)
    c["sinA"] = np.sin(fr).astype(np.float32)
    inv2 = 1.0 / 10000.0 ** (np.arange(0, 32, 2, dtype=np.float64) / 32)
    fr2 = np.outer(t, inv2)
    emb = np.concatenate([fr2, fr2], axis=1)
    c["cosB"] = np.cos(emb).astype(np.float32)
    c["sinB"] = np.sin(emb).astype(np.float32)
    c["crB"] = np.concatenate([np.tile(c["cosB"][:, :16], (1, 16)),
                               np.tile(c["sinB"][:, :16], (1, 16))], axis=1).astype(np.float32)
    return c


def host_inputs(inp, core, T):
    g = lambda k: np.asarray(inp[k], np.float32)
    m = {}
    m["x"] = np.ascontiguousarray(g("L0_attn_a_x")[core][:T])
    m["a_kvd"] = _chunks(g("L0_attn_a_W_kv_down"))
    m["a_kup"] = _chunks(g("L0_attn_a_W_k_up"))
    m["a_vup"] = _chunks(g("L0_attn_a_W_v_up"))
    m["a_wq"] = _chunks(g("L0_attn_a_W_q"))
    m["a_wqr"] = _chunks(g("L0_attn_a_W_q_rope"))
    m["a_wkr"] = _chunks(g("L0_attn_a_W_k_rope"))
    m["a_wo"] = _chunks(g("L0_attn_a_W_o"))
    m["f_kvd"] = _chunksT(g("L0_ffn_kv_d"))
    m["f_qd"] = _chunksT(g("L0_ffn_q_d"))
    m["f_ku"] = _chunksT(g("L0_ffn_k_u"))
    m["f_qu"] = _chunksT(g("L0_ffn_q_u"))
    m["f_vu"] = _chunksT(g("L0_ffn_v_u"))
    m["f_rq"] = _chunksT(g("L0_ffn_rope_q_w"))
    m["f_rk"] = _chunksT(g("L0_ffn_rope_k_w"))
    m["f_ow"] = _chunksT(g("L0_ffn_o_w"))
    m["f_rt"] = _chunksT(g("L0_ffn_router_w"))
    m["f_rb_row"] = _row(g("L0_ffn_routing_bias"))
    m["f_ln1_row"] = _row(g("L0_ffn_ln1_w"))
    m["f_ln2_row"] = _row(g("L0_ffn_ln2_w"))
    m["f_sg"] = _chunksT(g("L0_ffn_shared_gate")[0])
    m["f_su"] = _chunksT(g("L0_ffn_shared_up")[0])
    m["f_sd"] = _chunksT(g("L0_ffn_shared_down")[0])
    m["f_rg"] = np.stack([_chunksT(g("L0_ffn_routed_gate")[e]) for e in range(7)])
    m["f_ru"] = np.stack([_chunksT(g("L0_ffn_routed_up")[e]) for e in range(7)])
    m["f_rd"] = np.stack([_chunksT(g("L0_ffn_routed_down")[e]) for e in range(7)])
    m["b_encT"] = _chunksT(g("L0_attn_b_encoder_hidden_states")[0])
    m["b_logits"] = np.ascontiguousarray(
        np.broadcast_to(g("L0_attn_b_scorer_logits").reshape(1, 2048)[:, None, :], (1, 16, 2048)).reshape(1, 16 * 2048))
    m["b_cattn"] = _chunks(g("L0_attn_b_c_attn_w"))
    cb = g("L0_attn_b_c_attn_b")
    m["b_cbq_col"] = _col(cb[0:1024])
    m["b_cbk_col"] = _col(cb[1024:2048])
    m["b_cbv_row"] = _row(cb[2048:3072])
    m["b_proj"] = _chunks(g("L0_attn_b_attn_proj_w"))
    m["b_projb_row"] = _row(g("L0_attn_b_attn_proj_b"))
    m["b_qattn"] = _chunks(_padheads(g("L0_attn_b_q_attn_w")))
    m["b_qb_colp"] = _colpad64(g("L0_attn_b_q_attn_b"))
    xkv = g("L0_attn_b_x_kv_w")
    m["b_xkvk"] = _chunks(_padheads(xkv[:, 0:1024]))
    xkvb = g("L0_attn_b_x_kv_b")
    m["b_xkvk_colp"] = _colpad64(xkvb[0:1024])
    m["b_xkvv"] = _chunks(xkv[:, 1024:2048])
    m["b_xkvv_row"] = _row(xkvb[1024:2048])
    m["b_xproj"] = _chunks(g("L0_attn_b_x_proj_w"))
    m["b_xpb_row"] = _row(g("L0_attn_b_x_proj_b"))
    m["b_fc"] = _chunks(g("L0_attn_b_fc_w"))
    m["b_fcb_row"] = _row(g("L0_attn_b_fc_b"))
    m["b_mproj"] = _chunks(g("L0_attn_b_mlp_proj_w"))
    m["b_mpb_row"] = _row(g("L0_attn_b_mlp_proj_b"))
    for nm, key in [("b_ln1g", "L0_attn_b_ln1_g"), ("b_ln1b", "L0_attn_b_ln1_b"),
                    ("b_lnxg", "L0_attn_b_lnx_g"), ("b_lnxb", "L0_attn_b_lnx_b"),
                    ("b_ln2g", "L0_attn_b_ln2_g"), ("b_ln2b", "L0_attn_b_ln2_b")]:
        m[nm + "_row"] = _row(g(key))
    m["wa_q"] = _chunksT(g("L1_attn_a_Wq"))
    m["wa_k"] = _chunksT(g("L1_attn_a_Wk"))
    m["wa_v"] = _chunksT(g("L1_attn_a_Wv"))
    m["wa_o"] = _chunksT(g("L1_attn_a_Wo"))
    m["c_wq"] = _chunks(g("L1_attn_b_Wq"))
    m["c_wk"] = _chunks(g("L1_attn_b_Wk"))
    m["c_wv"] = _chunks(g("L1_attn_b_Wv"))
    m["c_wo"] = _chunks(g("L1_attn_b_Wo"))
    m["c_bq_col"] = _col(g("L1_attn_b_bq"))
    m["c_bk_col"] = _col(g("L1_attn_b_bk"))
    m["c_bv_row"] = _row(g("L1_attn_b_bv"))
    m["c_bo_row"] = _row(g("L1_attn_b_bo"))
    m["c_w1"] = _chunks(g("L1_attn_b_W1"))
    m["c_b1_row"] = _row(g("L1_attn_b_b1"))
    m["c_w2"] = _chunks(g("L1_attn_b_W2"))
    m["c_b2_row"] = _row(g("L1_attn_b_b2"))
    for nm, key in [("c_lnqg", "L1_attn_b_lnq_s"), ("c_lnqb", "L1_attn_b_lnq_b"),
                    ("c_lnkg", "L1_attn_b_lnk_s"), ("c_lnkb", "L1_attn_b_lnk_b"),
                    ("c_lnvg", "L1_attn_b_lnv_s"), ("c_lnvb", "L1_attn_b_lnv_b"),
                    ("c_lnmg", "L1_attn_b_lnm_s"), ("c_lnmb", "L1_attn_b_lnm_b")]:
        m[nm + "_row"] = _row(g(key))
    m["g_rt"] = _chunksT(g("L1_ffn_router_w"))
    m["g_cb_row"] = _row(g("L1_ffn_correction_bias"))
    m["g_gw"] = np.stack([_chunks(g("L1_ffn_gate_w")[e]) for e in range(16)])
    m["g_uw"] = np.stack([_chunks(g("L1_ffn_up_w")[e]) for e in range(16)])
    m["g_dw"] = np.stack([_chunks(g("L1_ffn_down_w")[e]) for e in range(16)])
    m["g_sg"] = _chunksT(g("L1_ffn_shared_gate_w"))
    m["g_su"] = _chunksT(g("L1_ffn_shared_up_w"))
    m["g_sd"] = _chunksT(g("L1_ffn_shared_down_w"))
    m.update(host_consts(T))
    return m


PREP_GROUPS = {
    "A": ["a_kvd", "a_kup", "a_vup", "a_wq", "a_wqr", "a_wkr", "a_wo"],
    "F": ["f_kvd", "f_qd", "f_ku", "f_qu", "f_vu", "f_rq", "f_rk", "f_ow", "f_rt",
          "f_sg", "f_su", "f_sd", "f_rg", "f_ru", "f_rd"],
    "B": ["b_cattn", "b_proj", "b_qattn", "b_xkvk", "b_xkvv", "b_xproj", "b_fc",
          "b_mproj", "b_encT"],
    "WA": ["wa_q", "wa_k", "wa_v", "wa_o"],
    "C": ["c_wq", "c_wk", "c_wv", "c_wo", "c_w1", "c_w2"],
    "G": ["g_rt", "g_gw", "g_uw", "g_dw", "g_sg", "g_su", "g_sd"],
}



class WH:
    __slots__ = ("h", "nch", "N", "tok")

    def __init__(self, h, nch, N, tok):
        self.h, self.nch, self.N, self.tok = h, nch, N, tok

    def ck(self, c, o0=0, n=None):
        n = self.N - o0 if n is None else n
        return self.h[:, c * self.N + o0: c * self.N + o0 + n]


def declare_io(mk, shapes_dtypes, T, want_out, rs_init=False):
    nc = mk.nc
    for name, (shape, dt) in shapes_dtypes.items():
        mk.dram[name] = nc.dram_tensor(name, list(shape), dt, kind="ExternalInput").ap()
    if rs_init:
        mk.dram["rs_init"] = nc.dram_tensor("rs_init", [T, D], F32, kind="ExternalInput").ap()
    mk.dram["out"] = nc.dram_tensor("out", [T, D], F32, kind="ExternalOutput").ap()
    mk.dram["RS"] = nc.dram_tensor("RS", [T, D], F32, kind="Internal").ap()
    mk.dram["Y"] = nc.dram_tensor("Y", [T, D], F32, kind="Internal").ap()
    mk.dram["Y1"] = nc.dram_tensor("Y1", [T, D], F32, kind="Internal").ap()
    mk.dram["XNF"] = nc.dram_tensor("XNF", [NCH, 128, T], BF16, kind="Internal").ap()
    mk.dram["QS"] = nc.dram_tensor("QS", [T, 2048], BF16, kind="Internal").ap()
    mk.dram["QR"] = nc.dram_tensor("QR", [T // 128, 128, 128], BF16, kind="Internal").ap()


def declare_bf16(mk, names, shapes):
    for name in names:
        sh = list(shapes[name])
        mk.dram["bf_" + name] = mk.nc.dram_tensor("bf_" + name, sh, BF16, kind="Internal").ap()


def emit_prep(mk, names, shapes):
    stf = [mk.sbuf(f"prep_f{i}", [128, 4096], F32, phase=True) for i in range(2)]
    stb = [mk.sbuf(f"prep_b{i}", [128, 4096], BF16, phase=True) for i in range(2)]
    toks = [None, None]
    sttoks = [None, None]
    it = 0
    for name in names:
        sh = shapes[name]
        src = mk.dram[name]
        dst = mk.dram["bf_" + name]
        lead = int(np.prod(sh[:-2]))
        N = sh[-1]
        srcf = src.rearrange(" ".join(f"a{i}" for i in range(len(sh) - 2)) + " p n -> (" +
                             " ".join(f"a{i}" for i in range(len(sh) - 2)) + ") p n") if len(sh) > 3 else src
        dstf = dst.rearrange(" ".join(f"a{i}" for i in range(len(sh) - 2)) + " p n -> (" +
                             " ".join(f"a{i}" for i in range(len(sh) - 2)) + ") p n") if len(sh) > 3 else dst
        if len(sh) == 2:
            srcf = srcf.rearrange("p n -> 1 p n") if False else None
            raise AssertionError("prep expects chunked >=3d")
        for li in range(lead):
            for o in range(0, N, 4096):
                n = min(4096, N - o)
                sl = it % 2
                ld = mk.dma("sync", stf[sl][:, 0:n], srcf[li, :, o:o + n],
                            wait=(toks[sl],))
                cv = mk.emit("vector", "tensor_copy", stb[sl][:, 0:n], stf[sl][:, 0:n],
                             wait=(ld, sttoks[sl]), inc=1)
                toks[sl] = cv
                sttoks[sl] = mk.dma("sync", dstf[li, :, o:o + n], stb[sl][:, 0:n], wait=(cv,))
                it += 1
    return sttoks


def load_w(mk, name, queue="sync"):
    dst_key = "bf_" + name
    sh = mk.dram[dst_key].tensor.shape if hasattr(mk.dram[dst_key], "tensor") else None
    sh = list(mk.dram[dst_key].shape)
    assert len(sh) == 3, (name, sh)
    nch, _, N = sh
    h = mk.sbuf("w_" + name, [128, nch * N], BF16)
    tok = mk.dma(queue, h[:].rearrange("p (c n) -> p c n", c=nch),
                 mk.dram[dst_key].rearrange("c p n -> p c n"))
    return WH(h[:], nch, N, tok)


def load_w_cols(mk, name, c0, c1, queue="sync"):
    sh = list(mk.dram["bf_" + name].shape)
    nch, _, N = sh
    n = c1 - c0
    h = mk.sbuf(f"w_{name}_{c0}", [128, nch * n], BF16)
    tok = mk.dma(queue, h[:].rearrange("p (c n) -> p c n", c=nch),
                 mk.dram["bf_" + name][:, :, c0:c1].rearrange("c p n -> p c n"))
    return WH(h[:], nch, n, tok)


def load_w_expert(mk, name, e, sbname, queue="sync"):
    sh = list(mk.dram["bf_" + name].shape)
    E, nch, _, N = sh
    h = mk.sb[sbname]
    tok = mk.dma(queue, h[:, 0:nch * N].rearrange("p (c n) -> p c n", c=nch),
                 mk.dram["bf_" + name][e].rearrange("c p n -> p c n"))
    return WH(h[:], nch, N, tok)


def load_f32(mk, name, queue="sync"):
    sh = list(mk.dram[name].shape)
    assert len(sh) == 2 and sh[0] <= 128, (name, sh)
    dt = mk.dram[name].dtype
    h = mk.sbuf("f_" + name, sh, dt)
    tok = mk.dma(queue, h[:], mk.dram[name][:])
    mk.set_wr(mk.reg("f_" + name), tok)
    return h


class Scr:

    def __init__(self, mk, name, n=64, phase=True):
        self.h = mk.sbuf(name, [128, n], F32, phase=phase)
        self.n = n
        self.used = 0

    def col(self, k=1):
        a = self.h[:, self.used:self.used + k]
        self.used += k
        assert self.used <= self.n
        return a


def ln_tile(mk, rt, xn, sc2, gamma=None, beta=None, wait=(), double=False, xin=None):
    v1 = mk.emit("vector", "reduce_sum", sc2["negmean"], rt, axis=AX.X, negate=True, wait=wait)
    v2 = mk.emit("vector", "tensor_scalar_mul", sc2["negmean"], sc2["negmean"], 1.0 / D, inc=1)
    s1 = mk.emit("scalar", "activation", sc2["xc"], rt, AF.Identity, bias=sc2["negmean"],
                 wait=(v2,) + tuple(wait))
    s2 = mk.emit("scalar", "activation", sc2["sq"], sc2["xc"], AF.Square, accum_out=sc2["ssq"])
    s3 = mk.emit("scalar", "activation", sc2["rstd"], sc2["ssq"], AF.Sqrt, scale=1.0 / D,
                 bias=mk.epscol, inc=1)
    v3 = mk.emit("vector", "reciprocal", sc2["rstd"], sc2["rstd"], wait=(s3,), inc=1)
    out = {}
    if xin is not None:
        ya = mk.emit("vector", "tensor_scalar_mul", xin, sc2["xc"], sc2["rstd"],
                     wait=(s3,), inc=1)
        out["ya_tok"] = ya
    if double:
        mk.emit("vector", "tensor_scalar_mul", sc2["vc"], sc2["ssq"], 1.0 / D)
        mk.emit("vector", "tensor_mul", sc2["t2"], sc2["rstd"], sc2["rstd"])
        v5 = mk.emit("vector", "tensor_mul", sc2["t2"], sc2["t2"], sc2["vc"], inc=1)
        s4 = mk.emit("scalar", "activation", sc2["t2"], sc2["t2"], AF.Sqrt,
                     bias=mk.epscol, wait=(v5,), inc=1)
        mk.emit("vector", "reciprocal", sc2["t2"], sc2["t2"], wait=(s4,))
        vt = mk.emit("vector", "tensor_mul", sc2["stot"], sc2["rstd"], sc2["t2"], inc=1)
        scale_ap = sc2["stot"]
        sctok = vt
    else:
        scale_ap = sc2["rstd"]
        sctok = v3
    if gamma is None:
        tok = mk.emit("vector", "tensor_scalar_mul", xn, sc2["xc"], scale_ap,
                      wait=(sctok, s3), inc=1)
    elif beta is None:
        tok = mk.emit("vector", "scalar_tensor_tensor", xn, sc2["xc"], scale_ap, gamma,
                      ALU.mult, ALU.mult, wait=(sctok, s3), inc=1)
    else:
        t1 = mk.emit("vector", "scalar_tensor_tensor", sc2["tmp"], sc2["xc"], scale_ap, gamma,
                     ALU.mult, ALU.mult, wait=(sctok, s3))
        tok = mk.emit("vector", "tensor_add", xn, sc2["tmp"], beta, inc=1)
    out["tok"] = tok
    out["scale_ap"] = scale_ap
    out["xc"] = sc2["xc"]
    out["sctok"] = sctok
    return out


def ln_variant(mk, sc2, scale_tok, xn2, gamma, beta, wait=()):
    if beta is None:
        tok = mk.emit("vector", "scalar_tensor_tensor", xn2, sc2["xc"], sc2["stot"], gamma,
                      ALU.mult, ALU.mult, wait=(scale_tok,) + tuple(wait), inc=1)
    else:
        mk.emit("vector", "scalar_tensor_tensor", sc2["tmp"], sc2["xc"], sc2["stot"], gamma,
                ALU.mult, ALU.mult, wait=(scale_tok,) + tuple(wait))
        tok = mk.emit("vector", "tensor_add", xn2, sc2["tmp"], beta, inc=1)
    return tok


def transposes(mk, pairs, wait=(), evac="vector"):
    PT = mk.PT
    out = []
    idx = 0
    first = True
    while idx < len(pairs):
        grp = pairs[idx:idx + 4]
        X = mk.ptb_idx % 2
        mk.ptb_idx += 1
        breg = mk.ptb_reg[X]
        w = (tuple(wait) if first else ()) + mk.wr_deps(breg)
        first = False
        ttok = None
        for j, (srcap, dstap) in enumerate(grp):
            ttok = mk.emit("tensor", "transpose", PT[:, 1024 * X + 128 * j:1024 * X + 128 * j + 128],
                           srcap, mk.ident, wait=(w if j == 0 else ()),
                           inc=(1 if j == len(grp) - 1 else 0))
        mk.set_wr(breg, ttok)
        for j, (srcap, dstap) in enumerate(grp):
            pt = PT[:, 1024 * X + 128 * j:1024 * X + 128 * j + 128]
            if isinstance(dstap, list):
                for (r0, r1, dap) in dstap:
                    etok = mk.emit(evac, "tensor_copy", dap, pt[r0:r1, :],
                                   wait=(ttok,), inc=1)
                    mk.add_rd(breg, etok)
                    out.append(etok)
            else:
                etok = mk.emit(evac, "tensor_copy", dstap, pt, wait=(ttok,), inc=1)
                mk.add_rd(breg, etok)
                out.append(etok)
        idx += 4
    return out



def attn_core(mk, *, nkeys, scale, q_ap, kf, vf, attw, ps, pts, sc, rope=None,
              pmask=None, pextra=None, heads=NH, scap=None, attw_wait=(),
              evac_engs=("vector", "scalar")):
    P8 = mk.P8
    PVp = mk.PV
    scap = scap or mk.scap
    psw = mk.psw
    pend = {}

    def emit_scores_softmax(h):
        sl = h % 2
        C = sc[sl]
        csz = min(1024, scap)
        chunks = [(k0, min(k0 + csz, nkeys)) for k0 in range(0, nkeys, csz)]
        pairs_list = [chunks[i:i + 2] for i in range(0, len(chunks), 2)]
        ptoks = []
        ps_reg = mk.reg(f"ps{sl}")
        for pi, pcs in enumerate(pairs_list):
            cinfo = []
            for cj, (k0, k1) in enumerate(pcs):
                clen = k1 - k0
                base = 1024 * cj
                nbank = (clen + 511) // 512
                bank_regs = mk.p8_reg[2 * cj:2 * cj + nbank]
                first_w = []
                for br in bank_regs:
                    first_w += list(mk.wr_deps(br))
                if pi == 0 and cj == 0:
                    first_w += list(mk.wr_deps(ps_reg))
                sc_tok = None
                for off in range(0, clen, 512):
                    n = min(512, clen - off)
                    last_blk = off + 512 >= clen
                    pr = [(q_ap(h), kf(h, k0 + off, n))]
                    if rope is not None:
                        qr, krf = rope(h)
                        pr.append((qr, krf(h, k0 + off, n)))
                    for j, (l, r) in enumerate(pr):
                        sc_tok = mk.emit("tensor", "matmul", P8[:, base + off:base + off + n],
                                         l, r, start=(j == 0), stop=(j == len(pr) - 1),
                                         wait=(tuple(first_w) if (off == 0 and j == 0) else ()),
                                         inc=(1 if (last_blk and j == len(pr) - 1) else 0))
                for br in bank_regs:
                    mk.set_wr(br, sc_tok)
                first = pi == 0 and cj == 0
                dst = C["negm"] if first else C["negm2"]
                mx = mk.emit("vector", "reduce_max", dst, P8[:, base:base + clen], axis=AX.X,
                             negate=True, wait=(sc_tok,), inc=1)
                for br in bank_regs:
                    mk.add_rd(br, mx)
                if not first:
                    tgt = C["negm"] if pi == 0 else C["negmT"]
                    if pi > 0 and cj == 0:
                        mk.emit("vector", "tensor_copy", tgt, C["negm2"])
                    else:
                        mk.emit("vector", "tensor_tensor", tgt, tgt, C["negm2"], ALU.min)
                cinfo.append((k0, k1, base, sc_tok, bank_regs))
            if pi == 0:
                bt = mk.emit("vector", "tensor_scalar_mul", C["bias"], C["negm"], scale, inc=1)
            else:
                mk.emit("vector", "tensor_tensor", C["negm"], C["negm"], C["negmT"], ALU.min)
                mk.emit("vector", "tensor_scalar_mul", C["bias2"], C["negm"], scale, inc=1)
                rt = mk.emit("vector", "tensor_sub", C["rd"], C["bias2"], C["bias"], inc=1)
                re = mk.emit("scalar", "activation", C["r"], C["rd"], AF.Exp, wait=(rt,), inc=1)
                kp0 = pcs[0][0]
                rs1 = mk.emit("vector", "tensor_scalar_mul", ps[:, sl * psw: sl * psw + kp0],
                              ps[:, sl * psw: sl * psw + kp0], C["r"],
                              wait=(re,) + tuple(ptoks), inc=1)
                ptoks.append(rs1)
                bt = mk.emit("vector", "tensor_copy", C["bias"], C["bias2"], inc=1)
            for (k0, k1, base, sc_tok, bank_regs) in cinfo:
                et = mk.emit("scalar", "activation", ps[:, sl * psw + k0: sl * psw + k1],
                             P8[:, base:base + k1 - k0], AF.Exp, bias=C["bias"], scale=scale,
                             wait=(bt, sc_tok), inc=1)
                ptoks.append(et)
                for br in bank_regs:
                    mk.add_rd(br, et)
        if pmask is not None:
            for kc in range(nkeys // 128):
                msk = pmask(kc)
                if msk is not None:
                    o = sl * psw + kc * 128
                    mt = mk.emit("gpsimd", "tensor_mul", ps[:, o:o + 128], ps[:, o:o + 128],
                                 msk, wait=tuple(ptoks), inc=1)
                    ptoks.append(mt)
        if pextra is not None:
            tt = pextra(h, sl, ptoks)
            if tt is not None:
                ptoks.append(tt)
        mk.set_wr(ps_reg, None, reset=True)
        pend[h] = (ptoks, None)

    def emit_consume(h):
        sl = h % 2
        C = sc[sl]
        ptoks, _st = pend.pop(h)
        nkc = nkeys // 128
        ps_reg = mk.reg(f"ps{sl}")
        groups = []
        kc = 0
        while kc < nkc:
            grp = min(4, nkc - kc)
            X = mk.ptb_idx % 2
            mk.ptb_idx += 1
            groups.append((kc, grp, X))
            kc += grp
        ttoks = [None] * len(groups)
        etoks = [None] * len(groups)

        def emit_T(gi):
            kc0, grp, X = groups[gi]
            breg = mk.ptb_reg[X]
            w = (tuple(ptoks) if gi == 0 else ()) + mk.wr_deps(breg)
            ttok = None
            for g in range(grp):
                ttok = mk.emit("tensor", "transpose",
                               mk.PT[:, 1024 * X + 128 * g:1024 * X + 128 * g + 128],
                               ps[:, sl * psw + (kc0 + g) * 128: sl * psw + (kc0 + g) * 128 + 128],
                               mk.ident, wait=(w if g == 0 else ()),
                               inc=(1 if g == grp - 1 else 0))
            mk.set_wr(breg, ttok)
            mk.add_rd(ps_reg, ttok)
            ttoks[gi] = ttok

        pv_tok = None
        emit_T(0)
        for gi, (kc0, grp, X) in enumerate(groups):
            breg = mk.ptb_reg[X]
            ev_eng = evac_engs[gi % len(evac_engs)]
            etok = mk.emit(ev_eng, "tensor_copy", pts[:, 512 * X:512 * X + 128 * grp],
                           mk.PT[:, 1024 * X:1024 * X + 128 * grp],
                           wait=(ttoks[gi],) + mk.wr_deps(mk.ptsb_reg[X]), inc=1)
            mk.add_rd(breg, etok)
            mk.set_wr(mk.ptsb_reg[X], etok)
            etoks[gi] = etok
            if gi + 1 < len(groups):
                emit_T(gi + 1)
            for g in range(grp):
                wv = (etok,)
                if kc0 + g == 0:
                    wv = wv + mk.wr_deps(mk.pv_reg)
                pv_tok = mk.emit("tensor", "matmul", PVp[:, 0:65],
                                 pts[:, 512 * X + 128 * g:512 * X + 128 * g + 128],
                                 vf(kc0 + g, h),
                                 start=(kc0 + g == 0), stop=(kc0 + g == nkc - 1), wait=wv,
                                 inc=(1 if kc0 + g == nkc - 1 else 0))
            mk.add_rd(mk.ptsb_reg[X], pv_tok)
        mk.set_wr(mk.pv_reg, pv_tok)
        rc = mk.emit("vector", "reciprocal", C["rs"], PVp[:, 64:65], wait=(pv_tok,))
        ot = mk.emit("vector", "tensor_scalar_mul", attw(h), PVp[:, 0:64], C["rs"],
                     wait=(pv_tok,) + tuple(attw_wait), inc=1)
        mk.add_rd(mk.pv_reg, ot)
        return ot

    out_toks = []
    for h in range(heads):
        emit_scores_softmax(h)
        if h > 0:
            out_toks.append(emit_consume(h - 1))
    out_toks.append(emit_consume(heads - 1))
    return out_toks


def mm_chain(mk, psum_slice, pairs, bank_regs, wait=()):
    w = tuple(wait)
    for br in bank_regs:
        w += mk.wr_deps(br)
    tok = None
    n = len(pairs)
    for j, (l, r) in enumerate(pairs):
        tok = mk.emit("tensor", "matmul", psum_slice, l, r, start=(j == 0),
                      stop=(j == n - 1), wait=(w if j == 0 else ()),
                      inc=(1 if j == n - 1 else 0))
    for br in bank_regs:
        mk.set_wr(br, tok)
    return tok


def mm_series(mk, fills, bank_regs, wait=()):
    w = tuple(wait)
    for br in bank_regs:
        w += mk.wr_deps(br)
    n = len(fills)
    tok = None
    for j, (p, l, r) in enumerate(fills):
        tok = mk.emit("tensor", "matmul", p, l, r, start=True, stop=True,
                      wait=(w if j == 0 else ()), inc=(1 if j == n - 1 else 0))
    for br in bank_regs:
        mk.set_wr(br, tok)
    return tok


def load_rope_tabs(mk, names):
    TL = mk.TILES
    out = {}
    for nm in names:
        f = mk.dram[nm].shape[1]
        h = mk.sbuf("rt_" + nm, [128, TL * f], F32)[:]
        mk.dma("sync", h.rearrange("p (a f) -> p a f", f=f),
               mk.dram[nm].rearrange("(a p) f -> p a f", p=128))
        out[nm] = h
    return out


def build_persistent(mk):
    nc = mk.nc
    mk.P8 = mk.es.enter_context(nc.psum_tensor("P8", [128, 2560], F32))[:]
    mk.PT = mk.es.enter_context(nc.psum_tensor("PT", [128, 2048], BF16))[:]
    mk.PV = mk.es.enter_context(nc.psum_tensor("PVp", [128, 512], F32))[:]
    mk.psw = 4096
    mk.scap = SCAP_DEFAULT
    mk.ptb_reg = [Reg(), Reg()]
    mk.ptsb_reg = [Reg(), Reg()]
    mk.ptb_idx = 0
    mk.spair = 0
    cb = mk.sbuf("cb", [128, 128 + 3 * 128], BF16, phase=False)
    mk.ident = cb[:, 0:128]
    mk.m_tril = cb[:, 128:256]
    mk.m_tril2 = cb[:, 256:384]
    mk.m_tril2b = cb[:, 384:512]
    mk.dma("sync", mk.ident, mk.dram["ident"][:])
    mk.dma("sync", mk.m_tril, mk.dram["m_tril"][:])
    mk.dma("sync", mk.m_tril2, mk.dram["m_tril2"][:])
    mk.dma("sync", mk.m_tril2b, mk.dram["m_tril2b"][:])

    ep = mk.sbuf("epscol", [128, 1], F32, phase=False)[:]
    mk.epscol = ep
    mk.emit("vector", "memset", ep, EPS, inc=1)


class TC:

    def __init__(self, mk, name="tc"):
        self.mk = mk
        self.rt = [mk.sbuf(f"{name}_rt{i}", [128, D], F32)[:] for i in range(2)]
        xn1 = mk.sbuf(f"{name}_xn", [128, D], BF16)[:]
        self.xn = [xn1, xn1]
        xnf1 = mk.sbuf(f"{name}_xnf", [128, D], BF16)[:]
        self.xnf = [xnf1, xnf1]
        sh = mk.sbuf(f"{name}_scr", [128, 2 * D + 32], F32)[:]
        self.scr = []
        for i in range(1):
            base = 0
            cb = 2 * D
            self.scr.append({
                "xc": sh[:, base:base + D], "sq": sh[:, base + D:base + 2 * D],
                "negmean": sh[:, cb:cb + 1],
                "ssq": sh[:, cb + 1:cb + 2],
                "rstd": sh[:, cb + 2:cb + 3],
                "vc": sh[:, cb + 3:cb + 4],
                "t2": sh[:, cb + 4:cb + 5],
                "stot": sh[:, cb + 5:cb + 6],
                "tmp": sh[:, base + D:base + 2 * D],
            })
        self.scr.append(self.scr[0])
        self.tmp_wide = [sh[:, 0:0] for _ in range(2)]
        self.rt_reg = [Reg(), Reg()]
        xr = Reg()
        self.xn_reg = [xr, xr]
        xfr = Reg()
        self.xnf_reg = [xfr, xfr]
        self.ln_busy = [None, None]

    def tile_in_sb(self, j, src_ap, src_toks, gamma=None, beta=None, fm=True, tmp=None):
        mk = self.mk
        sl = j % 2
        scr = dict(self.scr[sl])
        if tmp is not None:
            scr["tmp"] = tmp
        w = tuple(src_toks)
        if self.ln_busy[0] is not None:
            w = w + (self.ln_busy[0],)
        w = w + mk.wr_deps(self.xn_reg[sl])
        lt = ln_tile(mk, src_ap, self.xn[sl], scr, gamma=gamma, beta=beta, wait=w)
        mk.set_wr(self.xn_reg[sl], lt)
        self.ln_busy[0] = lt
        out = {"sl": sl, "xn": self.xn[sl], "xn_tok": lt, "xn_reg": self.xn_reg[sl]}
        if fm:
            pairs = [(self.xn[sl][:, 128 * c:128 * c + 128],
                      self.xnf[sl][:, 128 * c:128 * c + 128]) for c in range(NCH)]
            ft = transposes(mk, pairs, wait=(lt,) + mk.wr_deps(self.xnf_reg[sl]))
            for t in ft:
                mk.add_rd(self.xn_reg[sl], t)
            self.xnf_reg[sl].wrs = list(ft)
            self.xnf_reg[sl].rds = []
            out["xnf"] = self.xnf[sl]
            out["fm_toks"] = ft
            out["xnf_reg"] = self.xnf_reg[sl]
        return out

    def note_busy(self, sl, tok):
        self.ln_busy[0] = tok

    def tile_in(self, j, src, gamma=None, beta=None, fm=True, queue="gpsimd", tmp=None,
                double=False, xin=None):
        mk = self.mk
        sl = j % 2
        scr = dict(self.scr[sl])
        if tmp is not None:
            scr["tmp"] = tmp
        ld = mk.dma(queue, self.rt[sl], src[j * 128:(j + 1) * 128, :],
                    wait=mk.wr_deps(self.rt_reg[sl]))
        mk.set_wr(self.rt_reg[sl], ld)
        w = (ld,)
        if self.ln_busy[0] is not None:
            w = w + (self.ln_busy[0],)
        w = w + mk.wr_deps(self.xn_reg[sl])
        lr = ln_tile(mk, self.rt[sl], self.xn[sl], scr, gamma=gamma, beta=beta, wait=w,
                     double=double, xin=xin)
        lt = lr["tok"]
        mk.add_rd(self.rt_reg[sl], lt)
        mk.set_wr(self.xn_reg[sl], lt)
        self.ln_busy[0] = lt
        out = {"sl": sl, "rt": self.rt[sl], "rt_tok": ld, "rt_reg": self.rt_reg[sl],
               "xn": self.xn[sl], "xn_tok": lt, "xn_reg": self.xn_reg[sl],
               "ln": lr, "scr": scr}
        if fm:
            pairs = [(self.xn[sl][:, 128 * c:128 * c + 128],
                      self.xnf[sl][:, 128 * c:128 * c + 128]) for c in range(NCH)]
            ft = transposes(mk, pairs, wait=(lt,) + mk.wr_deps(self.xnf_reg[sl]))
            for t in ft:
                mk.add_rd(self.xn_reg[sl], t)
            mk.set_wr(self.xnf_reg[sl], ft[-1])
            for t in ft[:-1]:
                mk.add_rd(self.xnf_reg[sl], t)
            self.xnf_reg[sl].wrs = list(ft)
            self.xnf_reg[sl].rds = []
            out["xnf"] = self.xnf[sl]
            out["fm_toks"] = ft
            out["xnf_reg"] = self.xnf_reg[sl]
        return out


def attn_scr(mk, name):
    h = mk.sbuf(name, [128, 32], F32)[:]
    keys = ["negm", "negm2", "negmT", "bias", "bias2", "rd", "r", "ssumA", "ssumB",
            "stot", "rs"]
    out = []
    for i in range(2):
        out.append({k: h[:, i * 16 + j:i * 16 + j + 1] for j, k in enumerate(keys)})
    return out



def rope_apply(mk, x_psum, cos, sin, dst, halves=16, blocks=((0,),)):
    x1 = x_psum[:, 0:halves]
    x2 = x_psum[:, halves:2 * halves]
    t1 = mk.sb["ropescr"][:, 0:halves]
    t2 = mk.sb["ropescr"][:, halves:2 * halves]
    t3 = mk.sb["ropescr"][:, 2 * halves:3 * halves]
    t4 = mk.sb["ropescr"][:, 3 * halves:4 * halves]
    mk.emit("vector", "tensor_mul", t1, x1, cos)
    mk.emit("vector", "tensor_mul", t2, x2, sin)
    mk.emit("vector", "tensor_mul", t3, x2, cos)
    mk.emit("vector", "tensor_mul", t4, x1, sin)
    tok = None
    for (off,) in blocks:
        mk.emit("vector", "tensor_sub", dst[:, off:off + halves], t1, t2)
        tok = mk.emit("vector", "tensor_add", dst[:, off + halves:off + 2 * halves],
                      t3, t4, inc=1)
    return tok


def sub_l0_attn_a(mk, src, dst):
    T, TILES = mk.T, mk.TILES
    nc = mk.nc
    B = lambda i: mk.P8[:, 512 * i:512 * (i + 1)]
    BR = mk.p8_reg

    mk.begin_sub()
    KB = mk.sbuf("KBUF", [128, NCH * T], BF16, scope="sub")[:]
    VB = mk.sbuf("VBUF", [128, TILES * 1040], BF16, scope="sub")[:]
    KRP = mk.sbuf("KRP", [128, T], BF16, scope="sub")[:]
    mk.begin_phase()
    W = {n: load_w(mk, n) for n in ["a_kvd", "a_kup", "a_vup", "a_wkr"]}
    rt_tabs = load_rope_tabs(mk, ["cosA", "sinA"])
    cosA, sinA = rt_tabs["cosA"], rt_tabs["sinA"]
    mk.weights_ready()
    tc = TC(mk, "p1")
    mk.sbuf("ropescr", [128, 64], F32)
    ckv = [mk.sbuf(f"ckv{i}", [128, 128], BF16)[:] for i in range(2)]
    ckv_reg = [Reg(), Reg()]
    krp_tm = mk.sbuf("krp_tm", [128, 128], BF16)[:]
    krp_reg = Reg()
    mk.emit("vector", "memset", krp_tm, 0.0, inc=1)
    mk.emit("vector", "memset", VB, 1.0, inc=1)
    wt = tuple(w.tok for w in W.values())
    for j in range(TILES):
        ti = tc.tile_in(j, src)
        sl = ti["sl"]
        xnf = ti["xnf"]
        fmw = tuple(mk.wr_deps(ti["xnf_reg"]))
        ck = mm_chain(mk, B(4)[:, 0:128],
                      [(W["a_kvd"].ck(c), xnf[:, 128 * c:128 * c + 128]) for c in range(NCH)],
                      [BR[4]], wait=fmw + wt)
        for t in [ck]:
            mk.add_rd(ti["xnf_reg"], t)
        ce = mk.emit("vector", "tensor_copy", ckv[sl], B(4)[:, 0:128],
                     wait=(ck,) + mk.wr_deps(ckv_reg[sl]), inc=1)
        mk.add_rd(BR[4], ce)
        mk.set_wr(ckv_reg[sl], ce)
        for nb in range(2):
            fills = [(B(nb)[:, 128 * q:128 * q + 128],
                      W["a_kup"].ck(0, o0=128 * (4 * nb + q), n=128), ckv[sl])
                     for q in range(4)]
            kt = mm_series(mk, fills, [BR[nb]], wait=(ce,))
            dstap = KB[:, 4 * nb * T: (4 * nb + 4) * T].rearrange(
                "p (c t) -> p c t", c=4)[:, :, 128 * j:128 * j + 128]
            ke = mk.emit("vector", "tensor_copy", dstap,
                         B(nb).rearrange("p (c t) -> p c t", c=4), wait=(kt,), inc=1)
            mk.add_rd(BR[nb], ke)
            mk.add_rd(ckv_reg[sl], kt)
        vt = mm_series(mk, [(B(2), ckv[sl], W["a_vup"].ck(0, 0, 512)),
                            (B(3), ckv[sl], W["a_vup"].ck(0, 512, 512))],
                       [BR[2], BR[3]], wait=(ce,))
        ve = mk.emit("vector", "tensor_copy",
                     VB[:, 1040 * j:1040 * (j + 1)].rearrange("p (h d) -> p h d", d=65)[:, :, 0:64],
                     mk.P8[:, 1024:2048].rearrange("p (h d) -> p h d", d=64), wait=(vt,), inc=1)
        mk.add_rd(BR[2], ve)
        mk.add_rd(BR[3], ve)
        mk.add_rd(ckv_reg[sl], vt)
        rt5 = mm_chain(mk, B(4)[:, 128:160], [(ckv[sl], W["a_wkr"].ck(0))], [BR[4]], wait=(ce,))
        mk.add_rd(ckv_reg[sl], rt5)
        rw = tuple(mk.wr_deps(krp_reg))
        mk.raw_wait("vector", rt5)
        for t in rw:
            mk.raw_wait("vector", t)
        rtok = rope_apply(mk, B(4)[:, 128:160], cosA[:, 16 * j:16 * j + 16],
                          sinA[:, 16 * j:16 * j + 16], krp_tm, halves=16,
                          blocks=((0,), (64,)))
        mk.add_rd(BR[4], rtok)
        ktt = transposes(mk, [(krp_tm, KRP[:, 128 * j:128 * j + 128])], wait=(rtok,))
        mk.set_wr(krp_reg, None)
        for t in ktt:
            mk.add_rd(krp_reg, t)
        if j == 0:
            mk.dump("xn0", tc.xn[0], wait=(ti["xn_tok"],))
            mk.dump("xc0", tc.scr[0]["xc"], wait=(ti["xn_tok"],))
            mk.dump("negmean0", tc.scr[0]["negmean"], wait=(ti["xn_tok"],))
            mk.dump("ssq0", tc.scr[0]["ssq"], wait=(ti["xn_tok"],))
            mk.dump("rstd0", tc.scr[0]["rstd"], wait=(ti["xn_tok"],))
            mk.dump("xnf0", ti["xnf"], wait=tuple(mk.wr_deps(ti["xnf_reg"])))
            mk.dump("ckv0", ckv[0], wait=(ce,))
    mk.end_phase()
    mk.dump("kb0", KB[:, 0:512])
    mk.dump("kb4", KB[:, 4 * T:4 * T + 512])
    mk.dump("vb0", VB[:, 0:1040])
    mk.dump("krp0", KRP[:, 0:256])

    mk.begin_phase()
    W2 = {n: load_w(mk, n) for n in ["a_wq", "a_wqr"]}
    rt_tabs2 = load_rope_tabs(mk, ["cosA", "sinA"])
    cosA2, sinA2 = rt_tabs2["cosA"], rt_tabs2["sinA"]
    mk.weights_ready()
    mk.sbuf("ropescr", [128, 64], F32)
    qrp_tm = mk.sbuf("qrp_tm", [128, 128], BF16)[:]
    mk.emit("vector", "memset", qrp_tm, 0.0, inc=1)
    QRS = [mk.sbuf(f"QRS{i}", [128, 128], BF16)[:] for i in range(2)]
    QRS_reg = [Reg(), Reg()]
    B = lambda i: mk.P8[:, 512 * i:512 * (i + 1)]
    BR = mk.p8_reg

    def extra_rope(ti, i):
        sl = ti["sl"]
        xnf = ti["xnf"]
        fmw = tuple(mk.wr_deps(ti["xnf_reg"]))
        qr = mm_chain(mk, B(2)[:, 0:32],
                      [(xnf[:, 128 * c:128 * c + 128], W2["a_wqr"].ck(c)) for c in range(NCH)],
                      [BR[2]], wait=fmw)
        mk.add_rd(ti["xnf_reg"], qr)
        mk.raw_wait("vector", qr)
        rtok = rope_apply(mk, B(2)[:, 0:32], cosA2[:, 16 * i:16 * i + 16],
                          sinA2[:, 16 * i:16 * i + 16], qrp_tm, halves=16,
                          blocks=((0,), (64,)))
        mk.add_rd(BR[2], rtok)
        qtt = transposes(mk, [(qrp_tm, QRS[sl])], wait=(rtok,) + mk.wr_deps(QRS_reg[sl]))
        mk.set_wr(QRS_reg[sl], qtt[-1])
        st = mk.dma("sync", mk.dram["QR"][i], QRS[sl], wait=(qtt[-1],))
        mk.add_rd(QRS_reg[sl], st)

    q_store_pass(mk, src, T, W2["a_wq"], None, None, None, double=False,
                 extra=extra_rope)
    mk.end_phase()
    mk.begin_phase()
    wo = load_w(mk, "a_wo")
    mk.weights_ready()
    attn_only_pass(mk, dst, T, KB, VB, wo, None, 1.0 / math.sqrt(96.0),
                   lambda i: 128 * (i + 1),
                   pmask_fn=lambda i, kc: mk.m_tril if kc == i else None,
                   kf=lambda h, k0, n: KB[(h % 2) * 64:(h % 2) * 64 + 64,
                                          T * (h // 2) + k0:T * (h // 2) + k0 + n],
                   vf=lambda kc, h: VB[:, 1040 * kc + 65 * h:1040 * kc + 65 * h + 65],
                   rope_dram=mk.dram["QR"], krp=KRP, adds=(src,))
    mk.end_phase()
    mk.end_sub()


SUBS = {"a1": sub_l0_attn_a}
SUB_PREP = {"a1": "A", "f1": "F", "f2": "F", "b1": "B", "wa": "WA", "g1": "G",
            "g2": "G", "cb": "C"}
SUB_ORDER = ["a1", "f1", "b1", "f2", "wa", "g1", "cb", "g2"]


def build_program(shapes_dtypes, T, subs, rs_init=False, detect_races=True, debug=False):
    nc = bass.Bass("TRN2", target_bir_lowering=False, detect_race_conditions=detect_races)
    mk = MK(nc, T)
    mk.debug = debug
    declare_io(mk, shapes_dtypes, T, True, rs_init=rs_init)
    prep_names = []
    seen = set()
    for s in subs:
        g = SUB_PREP[s]
        if g in seen:
            continue
        seen.add(g)
        prep_names += PREP_GROUPS[g]
    shapes = {k: v[0] for k, v in shapes_dtypes.items()}
    declare_bf16(mk, prep_names, shapes)
    build_persistent(mk)
    mk.begin_phase()
    emit_prep(mk, prep_names, shapes)
    mk.end_phase()
    src = mk.dram["rs_init"] if rs_init else mk.dram["x"]
    for k, s in enumerate(subs):
        dst = mk.dram["out"] if k == len(subs) - 1 else mk.dram["RS"]
        SUBS[s](mk, src, dst)
        src = dst
    mk.finalize()
    return mk


def _shapes_dtypes(m):
    out = {}
    for k, v in m.items():
        dt = BF16 if v.dtype == ml_dtypes.bfloat16 else F32
        out[k] = (tuple(v.shape), dt)
    return out


def run_subchain(inputs, T, subs, cores, rs_state=None, trace=False):
    maps = [host_inputs(inputs, c, T) for c in cores]
    if rs_state is not None:
        for i in range(len(cores)):
            maps[i]["rs_init"] = np.ascontiguousarray(rs_state[i].astype(np.float32))
    sd = _shapes_dtypes({k: v for k, v in maps[0].items() if k != "rs_init"})
    mk = build_program(sd, T, subs, rs_init=rs_state is not None)
    res = run_bass_kernel_spmd(mk.nc, maps, core_ids=list(range(len(cores))), trace=trace)
    mk.close()
    out = np.stack([r["out"] for r in res.results])
    return out, res


def kernel(**inputs):
    T = 4096
    maps = [host_inputs(inputs, c, T) for c in range(8)]
    sd = _shapes_dtypes(maps[0])
    mk = build_program(sd, T, SUB_ORDER)
    res = run_bass_kernel_spmd(mk.nc, maps, core_ids=list(range(8)))
    mk.close()
    return np.stack([np.asarray(r["out"]).astype(np.float32) for r in res.results])


class WHoff:

    def __init__(self, wh, base):
        self.wh, self.base, self.tok = wh, base, wh.tok
        self.nch = wh.nch

    def ck(self, c, o0=0, n=None):
        return self.wh.ck(c, self.base + o0, n)


def gelu_tanh(mk, psum_ap, brow, out_bf, gs, out_wait=(), in_wait=()):
    mk.emit("vector", "tensor_add", gs["ub"], psum_ap, brow, wait=tuple(in_wait))
    mk.emit("scalar", "activation", gs["x2"], gs["ub"], AF.Square,
            wait=(mk.vtok(),))
    mk.emit("gpsimd", "tensor_mul", gs["x3"], gs["x2"], gs["ub"], wait=(mk.stok(),))
    mk.emit("vector", "scalar_tensor_tensor", gs["x3"], gs["x3"], 0.044715, gs["ub"],
            ALU.mult, ALU.add, wait=(mk.gtok(),))
    mk.emit("scalar", "activation", gs["x2"], gs["x3"], AF.Tanh, scale=0.7978845608028654,
            wait=(mk.vtok(),))
    mk.emit("gpsimd", "tensor_mul", gs["x3"], gs["x2"], gs["ub"], wait=(mk.stok(),))
    mk.emit("gpsimd", "tensor_add", gs["x3"], gs["x3"], gs["ub"])
    tok = mk.emit("vector", "tensor_scalar_mul", out_bf, gs["x3"], 0.5,
                  wait=(mk.gtok(),) + tuple(out_wait), inc=1)
    return tok


def qkv_build_pass(mk, src, T, KB, VB, wk, wv, bk_col, bv_row, lnkg, lnkb,
                   lnvg=None, lnvb=None, double=True, y_store=None, wtoks=(), only=None):
    B = lambda i: mk.P8[:, 512 * i:512 * (i + 1)]
    BR = mk.p8_reg
    tc = TC(mk, "kvp")
    xw = mk.sbuf("kvp_w", [128, D], F32)[:] if y_store is not None else None
    xw_reg = Reg()
    sep_v = lnvg is not None
    if sep_v:
        _xv = mk.sbuf("kvp_xnv", [128, D], BF16)[:]
        _xvf = mk.sbuf("kvp_xnvf", [128, D], BF16)[:]
        xnv, xnvf = [_xv, _xv], [_xvf, _xvf]
        _r1, _r2 = Reg(), Reg()
        xnv_reg, xnvf_reg = [_r1, _r1], [_r2, _r2]
    else:
        xnv = xnvf = None
        xnv_reg = [Reg(), Reg()]
        xnvf_reg = [Reg(), Reg()]
    for j in range(mk.TILES):
        ti = tc.tile_in(j, src, gamma=lnkg, beta=lnkb, fm=True, double=double,
                        xin=(xw if y_store is not None else None))
        sl = ti["sl"]
        if y_store is not None:
            ya = ti["ln"]["ya_tok"]
            yst = mk.dma("sync", y_store[j * 128:(j + 1) * 128, :], xw,
                         wait=(ya,) + mk.wr_deps(xw_reg))
            mk.set_wr(xw_reg, ya)
            mk.add_rd(xw_reg, yst)
        if sep_v:
            vdeps = mk.wr_deps(xnv_reg[sl])
            vt0 = ln_variant(mk, ti["scr"], ti["ln"]["sctok"], xnv[sl], lnvg, lnvb,
                             wait=vdeps)
            mk.set_wr(xnv_reg[sl], vt0)
            tc.note_busy(sl, vt0)
            vft = transposes(mk, [(xnv[sl][:, 128 * c:128 * c + 128],
                                   xnvf[sl][:, 128 * c:128 * c + 128]) for c in range(NCH)],
                             wait=(vt0,) + mk.wr_deps(xnvf_reg[sl]))
            xnvf_reg[sl].wrs = list(vft)
            xnvf_reg[sl].rds = []
            for t in vft:
                mk.add_rd(xnv_reg[sl], t)
            xnfv = xnvf[sl]
            vreg = xnvf_reg[sl]
        else:
            xnfv = ti["xnf"]
            vreg = ti["xnf_reg"]
        xnfk = ti["xnf"]
        fmwk = tuple(mk.wr_deps(ti["xnf_reg"])) + tuple(wtoks)
        fmwv = tuple(mk.wr_deps(vreg)) + tuple(wtoks)
        for nb in (() if only == "V" else range(2)):
            gts = []
            for q in range(4):
                co = 4 * nb + q
                gt = mm_chain(mk, B(nb)[:, 128 * q:128 * q + 128],
                              [(wk.ck(c, o0=128 * co, n=128), xnfk[:, 128 * c:128 * c + 128])
                               for c in range(NCH)],
                              [BR[nb]] if q == 0 else [], wait=(fmwk if q == 0 else ()))
                gts.append(gt)
            mk.set_wr(BR[nb], gts[-1])
            mk.add_rd(ti["xnf_reg"], gts[-1])
            for q in range(4):
                co = 4 * nb + q
                dstap = KB[:, co * T + 128 * j: co * T + 128 * j + 128]
                if bk_col is not None:
                    ke = mk.emit("vector", "tensor_scalar_add", dstap,
                                 B(nb)[:, 128 * q:128 * q + 128], bk_col[:, co:co + 1],
                                 wait=(gts[-1],), inc=1)
                else:
                    ke = mk.emit("vector", "tensor_copy", dstap,
                                 B(nb)[:, 128 * q:128 * q + 128], wait=(gts[-1],), inc=1)
                mk.add_rd(BR[nb], ke)
        for nb in (() if only == "K" else range(2)):
            vt = mm_chain(mk, B(2 + nb), [(xnfv[:, 128 * c:128 * c + 128],
                                           wv.ck(c, o0=512 * nb, n=512)) for c in range(NCH)],
                          [BR[2 + nb]], wait=fmwv)
            mk.add_rd(vreg, vt)
            dv = VB[:, 1040 * j:1040 * (j + 1)].rearrange("p (h d) -> p h d", d=65)
            dv = dv[:, 8 * nb:8 * nb + 8, 0:64]
            sv = B(2 + nb).rearrange("p (h d) -> p h d", d=64)
            if bv_row is not None:
                bv3 = bv_row[:, 512 * nb:512 * nb + 512].rearrange("p (h d) -> p h d", d=64)
                ve = mk.emit("vector", "scalar_tensor_tensor", dv, sv, 1.0,
                             bv3, ALU.mult, ALU.add, wait=(vt,), inc=1)
            else:
                ve = mk.emit("vector", "tensor_copy", dv, sv, wait=(vt,), inc=1)
            mk.add_rd(BR[2 + nb], ve)


def q_store_pass(mk, src, T, wq, bq_col, lng, lnb, double, nco=8, padded=False,
                 y_store=None, wtoks=(), extra=None):
    B = lambda i: mk.P8[:, 512 * i:512 * (i + 1)]
    BR = mk.p8_reg
    QS = mk.dram["QS"]
    tc = TC(mk, "qsp")
    xw = mk.sbuf("qs_w", [128, D], F32)[:] if y_store is not None else None
    xw_reg = Reg()
    QA = [mk.sbuf(f"qsQA{i}", [128, 128 * nco], BF16)[:] for i in range(2)]
    QA_reg = [Reg(), Reg()]
    if padded:
        for i in range(2):
            mk.emit("vector", "memset", QA[i], 0.0, inc=1)
            mk.emit("vector", "memset", QA[i][64:65, :], 8.0, inc=1)
    for i in range(mk.TILES):
        ti = tc.tile_in(i, src, gamma=lng, beta=lnb, fm=True, double=double,
                        xin=(xw if y_store is not None else None))
        sl = ti["sl"]
        if y_store is not None:
            ya = ti["ln"]["ya_tok"]
            yst = mk.dma("sync", y_store[i * 128:(i + 1) * 128, :], xw,
                         wait=(ya,) + mk.wr_deps(xw_reg))
            mk.set_wr(xw_reg, ya)
            mk.add_rd(xw_reg, yst)
        xnf = ti["xnf"]
        fmw = tuple(mk.wr_deps(ti["xnf_reg"])) + tuple(wtoks)
        qdeps = mk.wr_deps(QA_reg[sl])
        QA_reg[sl].wrs = []
        QA_reg[sl].rds = []
        for r in range(nco // 4):
            bk = r % 2
            gts = []
            for q in range(4):
                co = 4 * r + q
                gt = mm_chain(mk, B(bk)[:, 128 * q:128 * q + 128],
                              [(wq.ck(c, o0=128 * co, n=128), xnf[:, 128 * c:128 * c + 128])
                               for c in range(NCH)],
                              [BR[bk]] if q == 0 else [], wait=(fmw if q == 0 else ()))
                gts.append(gt)
            mk.set_wr(BR[bk], gts[-1])
            mk.add_rd(ti["xnf_reg"], gts[-1])
            for q in range(4):
                co = 4 * r + q
                rows = slice(0, 64) if padded else slice(0, 128)
                dstq = QA[sl][rows, 128 * co:128 * co + 128]
                if bq_col is not None:
                    qe = mk.emit("vector", "tensor_scalar_add", dstq,
                                 B(bk)[rows, 128 * q:128 * q + 128],
                                 bq_col[rows, co:co + 1], wait=(gts[-1],) + qdeps, inc=1)
                else:
                    qe = mk.emit("vector", "tensor_copy", dstq,
                                 B(bk)[rows, 128 * q:128 * q + 128],
                                 wait=(gts[-1],) + qdeps, inc=1)
                mk.add_rd(BR[bk], qe)
                QA_reg[sl].wrs.append(qe)
        st = mk.dma("sync", QS[i * 128:(i + 1) * 128, 0:128 * nco], QA[sl],
                    wait=tuple(QA_reg[sl].wrs))
        mk.add_rd(QA_reg[sl], st)
        if extra is not None:
            extra(ti, i)
    return tc


def attn_only_pass(mk, dst, T, KB, VB, wo, wo_brow, scale, nkeys_fn, pmask_fn=None,
                   pextra_fn=None, qwidth=1024, qmode="heads64", kf=None, vf=None,
                   rope_dram=None, krp=None, adds=(), psw=None, wtoks=()):
    B = lambda i: mk.P8[:, 512 * i:512 * (i + 1)]
    BR = mk.p8_reg
    psw = psw or mk.psw
    old_psw = mk.psw
    mk.psw = psw
    QS = mk.dram["QS"]
    QT = [mk.sbuf(f"aoQT{i}", [128, qwidth], BF16)[:] for i in range(2)]
    QT_reg = [Reg(), Reg()]
    QR = [mk.sbuf(f"aoQR{i}", [128, 128], BF16)[:] for i in range(2)] if rope_dram is not None else None
    QR_reg = [Reg(), Reg()]
    AT = [mk.sbuf(f"aoAT{i}", [128, 1024], BF16)[:] for i in range(2)]
    AT_reg = [Reg(), Reg()]
    AF_ = [mk.sbuf(f"aoAF{i}", [128, 1024], BF16)[:] for i in range(2)]
    AF_reg = [Reg(), Reg()]
    PSB = mk.sbuf("aoPSB", [128, 2 * psw], BF16)[:]
    PTS = mk.sbuf("aoPTS", [128, 1024], BF16)[:]
    OT = [mk.sbuf(f"aoOT{i}", [128, 1024], F32)[:] for i in range(2)]
    OT_reg = [Reg(), Reg()]
    AD = [[mk.sbuf(f"aoAD{a}_{i}", [128, 1024], F32)[:] for i in range(2)]
          for a in range(len(adds))]
    AD_reg = [[Reg(), Reg()] for _ in range(len(adds))]
    sc = attn_scr(mk, "aosc")
    for i in range(mk.TILES):
        sl = i % 2
        qld = mk.dma("gpsimd", QT[sl], QS[i * 128:(i + 1) * 128, 0:qwidth],
                     wait=mk.wr_deps(QT_reg[sl]))
        mk.set_wr(QT_reg[sl], qld)
        rtoks = (qld,) + tuple(wtoks)
        if rope_dram is not None:
            rld = mk.dma("gpsimd", QR[sl], rope_dram[i], wait=mk.wr_deps(QR_reg[sl]))
            mk.set_wr(QR_reg[sl], rld)
            rtoks = rtoks + (rld,)
        addl = []
        for a, adram in enumerate(adds):
            ald = mk.dma("gpsimd", AD[a][sl], adram[i * 128:(i + 1) * 128, :],
                         wait=mk.wr_deps(AD_reg[a][sl]))
            mk.set_wr(AD_reg[a][sl], ald)
            addl.append((AD[a][sl], ald, AD_reg[a][sl]))
        nkeys = nkeys_fn(i)
        qa = QT[sl]
        at = AT[sl]
        for t in rtoks:
            mk.raw_wait("tensor", t)
        at_deps = tuple(AT_reg[sl].rds)
        AT_reg[sl].rds = []
        if qmode == "heads64":
            q_ap = lambda h: qa[(h % 2) * 64:(h % 2) * 64 + 64, 128 * (h // 2):128 * (h // 2) + 128]
        else:
            q_ap = lambda h: qa[:, 128 * h:128 * h + 128]
        rope_arg = None
        if rope_dram is not None:
            qr = QR[sl]
            rope_arg = lambda h: (qr[(h % 2) * 64:(h % 2) * 64 + 64, :],
                                  lambda hh, k0, n: krp[(hh % 2) * 64:(hh % 2) * 64 + 64, k0:k0 + n])
        ot_toks = attn_core(
            mk, nkeys=nkeys, scale=scale, attw_wait=at_deps,
            q_ap=q_ap, kf=kf, vf=vf, rope=rope_arg,
            attw=lambda h: at[:, 64 * h:64 * h + 64],
            ps=PSB, pts=PTS, sc=sc,
            pmask=(lambda kc: pmask_fn(i, kc)) if pmask_fn else None,
            pextra=(lambda h, slp, ptoks: pextra_fn(i, h, slp, ptoks, PSB)) if pextra_fn else None)
        for t in ot_toks:
            QT_reg[sl].rds.append(t)
            if rope_dram is not None:
                QR_reg[sl].rds.append(t)
        aft = transposes(mk, [(at[:, 128 * c:128 * c + 128], AF_[sl][:, 128 * c:128 * c + 128])
                              for c in range(NCH)],
                         wait=tuple(ot_toks) + mk.wr_deps(AF_reg[sl]))
        AF_reg[sl].wrs = list(aft)
        AF_reg[sl].rds = []
        for t in aft:
            AT_reg[sl].rds.append(t)
        odeps = mk.wr_deps(OT_reg[sl])
        OT_reg[sl].wrs = []
        OT_reg[sl].rds = []
        for nb in range(2):
            wo_t = mm_chain(mk, B(3 + nb), [(AF_[sl][:, 128 * c:128 * c + 128],
                                             wo.ck(c, 512 * nb, 512)) for c in range(NCH)],
                            [BR[3 + nb]], wait=tuple(aft) + tuple(wtoks))
            mk.add_rd(AF_reg[sl], wo_t)
            colr = slice(512 * nb, 512 * nb + 512)
            if wo_brow is not None:
                oe = mk.emit("vector", "scalar_tensor_tensor", OT[sl][:, colr],
                             B(3 + nb), 1.0, wo_brow[:, colr], ALU.mult, ALU.add,
                             wait=(wo_t,) + odeps, inc=1)
            else:
                oe = mk.emit("vector", "tensor_copy", OT[sl][:, colr], B(3 + nb),
                             wait=(wo_t,) + odeps, inc=1)
            mk.add_rd(BR[3 + nb], oe)
            for (aap, atok, areg) in addl:
                oe = mk.emit("vector", "tensor_add", OT[sl][:, colr], OT[sl][:, colr],
                             aap[:, colr], wait=(atok,), inc=1)
                mk.add_rd(areg, oe)
            OT_reg[sl].wrs.append(oe)
        st = mk.dma("sync", dst[i * 128:(i + 1) * 128, :], OT[sl],
                    wait=tuple(OT_reg[sl].wrs))
        mk.add_rd(OT_reg[sl], st)
    mk.psw = old_psw


def mlp_pass(mk, src, dst, T, w1, b1row, w2, b2row, lng, lnb, inner, adds):
    B = lambda i: mk.P8[:, 512 * i:512 * (i + 1)]
    BR = mk.p8_reg
    tc = TC(mk, "mlp")
    ngb = 1 if inner > 2048 else 2
    GB = [mk.sbuf(f"mlp_gb{i}", [128, inner], BF16)[:] for i in range(ngb)]
    GB = [GB[i % ngb] for i in range(2)]
    GB_reg = [Reg(), Reg()] if ngb == 2 else [Reg()] * 2
    GF = [mk.sbuf(f"mlp_gf{i}", [128, inner], BF16)[:] for i in range(ngb)]
    GF = [GF[i % ngb] for i in range(2)]
    GF_reg = [Reg(), Reg()] if ngb == 2 else [Reg()] * 2
    gs = {k: mk.sbuf("mlp_" + k, [128, 512], F32)[:] for k in ("ub", "x2", "x3")}
    OT = [mk.sbuf(f"mlp_ot{i}", [128, 1024], F32)[:] for i in range(ngb)]
    OT = [OT[i % ngb] for i in range(2)]
    OT_reg = [Reg(), Reg()] if ngb == 2 else [Reg()] * 2
    AD = [mk.sbuf(f"mlp_ad{i}", [128, 1024], F32)[:] for i in range(ngb)]
    AD = [AD[i % ngb] for i in range(2)]
    AD_reg = [Reg(), Reg()] if ngb == 2 else [Reg()] * 2
    nin = inner // 512
    for i in range(mk.TILES):
        ti = tc.tile_in(i, src, gamma=lng, beta=lnb, fm=True)
        sl = i % 2
        xnf = ti["xnf"]
        fmw = tuple(mk.wr_deps(ti["xnf_reg"]))
        gdeps = mk.wr_deps(GB_reg[sl])
        GB_reg[sl].wrs = []
        GB_reg[sl].rds = []
        for ib in range(nin):
            bk = ib % 3
            ft = mm_chain(mk, B(bk), [(xnf[:, 128 * c:128 * c + 128],
                                       w1.ck(c, 512 * ib, 512)) for c in range(NCH)],
                          [BR[bk]], wait=(fmw if ib == 0 else ()))
            mk.add_rd(ti["xnf_reg"], ft)
            gt = gelu_tanh(mk, B(bk), b1row[:, 512 * ib:512 * ib + 512],
                           GB[sl][:, 512 * ib:512 * ib + 512], gs, out_wait=gdeps,
                           in_wait=(ft,))
            mk.add_rd(BR[bk], gt)
            GB_reg[sl].wrs.append(gt)
        gft = transposes(mk, [(GB[sl][:, 128 * c:128 * c + 128],
                               GF[sl][:, 128 * c:128 * c + 128]) for c in range(inner // 128)],
                         wait=tuple(GB_reg[sl].wrs) + mk.wr_deps(GF_reg[sl]))
        GF_reg[sl].wrs = list(gft)
        GF_reg[sl].rds = []
        for t in gft:
            GB_reg[sl].rds.append(t)
        odeps = mk.wr_deps(OT_reg[sl])
        OT_reg[sl].wrs = []
        OT_reg[sl].rds = []
        addl = []
        for ai, a in enumerate(adds):
            if a == "src":
                addl.append((ti["rt"], ti["rt_tok"], ti["rt_reg"]))
            else:
                adeps = mk.wr_deps(AD_reg[sl])
                AD_reg[sl].wrs = []
                AD_reg[sl].rds = []
                lda = mk.dma("gpsimd", AD[sl][:, 0:1024], a[i * 128:(i + 1) * 128, :],
                             wait=adeps)
                AD_reg[sl].wrs.append(lda)
                addl.append((AD[sl][:, 0:1024], lda, AD_reg[sl]))
        for nb in range(2):
            wt = mm_chain(mk, B(3 + nb), [(GF[sl][:, 128 * c:128 * c + 128],
                                           w2.ck(c, 512 * nb, 512)) for c in range(inner // 128)],
                          [BR[3 + nb]], wait=tuple(gft))
            mk.add_rd(GF_reg[sl], wt)
            colr = slice(512 * nb, 512 * nb + 512)
            oe = mk.emit("vector", "scalar_tensor_tensor", OT[sl][:, colr], B(3 + nb),
                         1.0, b2row[:, colr], ALU.mult, ALU.add, wait=(wt,) + odeps, inc=1)
            mk.add_rd(BR[3 + nb], oe)
            for (aap, atok, areg) in addl:
                oe = mk.emit("vector", "tensor_add", OT[sl][:, colr], OT[sl][:, colr],
                             aap[:, colr] if aap.shape[1] > 512 else aap, wait=(atok,), inc=1)
                if areg is not None:
                    mk.add_rd(areg, oe)
            OT_reg[sl].wrs.append(oe)
        st = mk.dma("sync", dst[i * 128:(i + 1) * 128, :], OT[sl],
                    wait=tuple(OT_reg[sl].wrs))
        mk.add_rd(OT_reg[sl], st)



def sub_l1_attn_b(mk, src, dst):
    T = mk.T
    mk.begin_sub()
    KB = mk.sbuf("cKBUF", [128, NCH * T], BF16, scope="sub")[:]
    VB = mk.sbuf("cVBUF", [128, mk.TILES * 1040], BF16, scope="sub")[:]
    mk.begin_phase()
    wk = load_w(mk, "c_wk")
    bkc = load_f32(mk, "c_bk_col")
    lnkg = load_f32(mk, "c_lnkg_row")
    lnkb = load_f32(mk, "c_lnkb_row")
    mk.weights_ready()
    qkv_build_pass(mk, src, T, KB, VB, wk, None, bkc[:], None, lnkg[:], lnkb[:],
                   double=True, y_store=mk.dram["Y"], wtoks=(wk.tok,), only="K")
    mk.end_phase()
    mk.begin_phase()
    wv = load_w(mk, "c_wv")
    bvr = load_f32(mk, "c_bv_row")
    lnvg = load_f32(mk, "c_lnvg_row")
    lnvb = load_f32(mk, "c_lnvb_row")
    mk.weights_ready()
    mk.emit("vector", "memset", VB, 1.0, inc=1)
    qkv_build_pass(mk, src, T, KB, VB, None, wv, None, bvr[:], lnvg[:], lnvb[:],
                   double=True, wtoks=(wv.tok,), only="V")
    mk.end_phase()
    mk.begin_phase()
    wq = load_w(mk, "c_wq")
    bqc = load_f32(mk, "c_bq_col")
    lnqg = load_f32(mk, "c_lnqg_row")
    lnqb = load_f32(mk, "c_lnqb_row")
    mk.weights_ready()
    q_store_pass(mk, src, T, wq, bqc[:], lnqg[:], lnqb[:], double=True)
    mk.end_phase()
    mk.begin_phase()
    wo = load_w(mk, "c_wo")
    bor = load_f32(mk, "c_bo_row")
    mk.weights_ready()

    def pextra(i, h, slp, ptoks, PSB):
        if i != 0:
            return None
        o = slp * mk.psw + (T - 1)
        return mk.emit("vector", "tensor_scalar_mul", PSB[0:1, o:o + 1],
                       PSB[0:1, o:o + 1], 0.0, wait=tuple(ptoks), inc=1)

    attn_only_pass(mk, mk.dram["Y"], T, KB, VB, wo, bor[:], 1.0 / 32.0,
                   lambda i: T, pextra_fn=pextra,
                   kf=lambda h, k0, n: KB[(h % 2) * 64:(h % 2) * 64 + 64,
                                          T * (h // 2) + k0:T * (h // 2) + k0 + n],
                   vf=lambda kc, h: VB[:, 1040 * kc + 65 * h:1040 * kc + 65 * h + 65],
                   adds=(mk.dram["Y"],))
    mk.end_phase()
    mk.end_sub()
    mk.begin_sub()
    mk.begin_phase()
    w1 = load_w(mk, "c_w1")
    w2 = load_w(mk, "c_w2")
    b1r = load_f32(mk, "c_b1_row")
    b2r = load_f32(mk, "c_b2_row")
    lnmg = load_f32(mk, "c_lnmg_row")
    lnmb = load_f32(mk, "c_lnmb_row")
    mk.weights_ready()
    mlp_pass(mk, mk.dram["Y"], dst, T, w1, b1r[:], w2, b2r[:], lnmg[:], lnmb[:],
             2048, adds=["src", src])
    mk.end_phase()
    mk.end_sub()


SUBS["cb"] = sub_l1_attn_b



def sub_l0_attn_b(mk, src, dst):
    T = mk.T
    TE = 2048
    TE_TILES = 16
    mk.begin_sub()
    KB = mk.sbuf("bKBUF", [128, max(NCH * T, 16 * TE)], BF16, scope="sub")[:]
    VB = mk.sbuf("bVBUF", [128, max(mk.TILES, TE_TILES) * 1040], BF16, scope="sub")[:]
    Y = mk.dram["Y"]
    mk.begin_phase()
    wck = load_w_cols(mk, "b_cattn", 1024, 2048)
    bkc = load_f32(mk, "b_cbk_col")
    g1 = load_f32(mk, "b_ln1g_row")
    b1_ = load_f32(mk, "b_ln1b_row")
    mk.weights_ready()
    qkv_build_pass(mk, src, T, KB, VB, wck, None, bkc[:], None, g1[:], b1_[:],
                   double=True, y_store=Y, only="K")
    mk.end_phase()
    mk.begin_phase()
    wcv = load_w_cols(mk, "b_cattn", 2048, 3072)
    bvr = load_f32(mk, "b_cbv_row")
    g1v = load_f32(mk, "b_ln1g_row")
    b1v = load_f32(mk, "b_ln1b_row")
    mk.weights_ready()
    mk.emit("vector", "memset", VB, 1.0, inc=1)
    qkv_build_pass(mk, src, T, KB, VB, None, wcv, None, bvr[:], g1v[:], b1v[:],
                   double=True, only="V")
    mk.end_phase()
    mk.begin_phase()
    wcq = load_w_cols(mk, "b_cattn", 0, 1024)
    bqc = load_f32(mk, "b_cbq_col")
    g1b = load_f32(mk, "b_ln1g_row")
    b1b = load_f32(mk, "b_ln1b_row")
    mk.weights_ready()
    q_store_pass(mk, src, T, wcq, bqc[:], g1b[:], b1b[:], double=True)
    mk.end_phase()
    mk.begin_phase()
    wo = load_w(mk, "b_proj")
    bor = load_f32(mk, "b_projb_row")
    mk.weights_ready()
    attn_only_pass(mk, Y, T, KB, VB, wo, bor[:], 0.125, lambda i: 128 * (i + 1),
                   pmask_fn=lambda i, kc: mk.m_tril if kc == i else None,
                   kf=lambda h, k0, n: KB[(h % 2) * 64:(h % 2) * 64 + 64,
                                          T * (h // 2) + k0:T * (h // 2) + k0 + n],
                   vf=lambda kc, h: VB[:, 1040 * kc + 65 * h:1040 * kc + 65 * h + 65],
                   adds=(Y,))
    mk.end_phase()
    mk.begin_phase()
    enc = load_w(mk, "b_encT")
    wxk = load_w(mk, "b_xkvk")
    bkp = load_f32(mk, "b_xkvk_colp")
    mk.weights_ready()
    B = lambda i: mk.P8[:, 512 * i:512 * (i + 1)]
    BR = mk.p8_reg
    mset = mk.emit("vector", "memset", KB[:, 0:16 * TE], 0.0, inc=1)
    for h in range(16):
        for ks in range(TE // 512):
            bk = (h * 4 + ks) % 2
            kt = mm_chain(mk, B(bk), [(wxk.ck(c, o0=128 * h, n=128),
                                       enc.ck(c, o0=512 * ks, n=512)) for c in range(NCH)],
                          [BR[bk]], wait=(mset,))
            ke = mk.emit("vector", "tensor_scalar_add",
                         KB[0:64, TE * h + 512 * ks: TE * h + 512 * ks + 512],
                         B(bk)[0:64, :], bkp[0:64, h:h + 1], wait=(kt,), inc=1)
            mk.add_rd(BR[bk], ke)
    lg = mk.dma("gpsimd", KB[64:65, 0:16 * TE], mk.dram["b_logits"][:], wait=(mset,))
    mk.raw_wait("tensor", lg)
    mk.raw_wait("vector", lg)
    mk.end_phase()
    mk.begin_phase()
    enc2 = load_w(mk, "b_encT")
    wxv = load_w(mk, "b_xkvv")
    bvr2 = load_f32(mk, "b_xkvv_row")
    mk.weights_ready()
    mk.emit("vector", "memset", VB[:, 0:TE_TILES * 1040], 1.0, inc=1)
    for t in range(TE_TILES):
        for nb in range(2):
            bk = 2 + nb
            vt = mm_chain(mk, B(bk), [(enc2.ck(c, o0=128 * t, n=128),
                                       wxv.ck(c, o0=512 * nb, n=512)) for c in range(NCH)],
                          [BR[bk]], wait=())
            dv = VB[:, 1040 * t:1040 * (t + 1)].rearrange("p (h d) -> p h d", d=65)
            dv = dv[:, 8 * nb:8 * nb + 8, 0:64]
            sv = B(bk).rearrange("p (h d) -> p h d", d=64)
            bv3 = bvr2[:, 512 * nb:512 * nb + 512].rearrange("p (h d) -> p h d", d=64)
            ve = mk.emit("vector", "scalar_tensor_tensor", dv, sv, 1.0, bv3,
                         ALU.mult, ALU.add, wait=(vt,), inc=1)
            mk.add_rd(BR[bk], ve)
    mk.end_phase()
    mk.begin_phase()
    wq2 = load_w(mk, "b_qattn")
    bqp = load_f32(mk, "b_qb_colp")
    gx = load_f32(mk, "b_lnxg_row")
    bx = load_f32(mk, "b_lnxb_row")
    mk.weights_ready()
    q_store_pass(mk, Y, T, wq2, bqp[:], gx[:], bx[:], double=False, nco=16, padded=True)
    mk.end_phase()
    mk.begin_phase()
    wxo = load_w(mk, "b_xproj")
    bxr = load_f32(mk, "b_xpb_row")
    mk.weights_ready()

    def pmask(i, kc):
        if kc == i and i <= TE_TILES - 1:
            return mk.m_tril2
        if kc == i + 1 and i + 1 <= TE_TILES - 1:
            return mk.m_tril2b
        return None

    attn_only_pass(mk, Y, T, KB, VB, wxo, bxr[:], 0.125,
                   lambda i: 128 * min(TE_TILES, i + 2), pmask_fn=pmask,
                   qwidth=2048, qmode="full128",
                   kf=lambda h, k0, n: KB[:, TE * h + k0:TE * h + k0 + n],
                   vf=lambda kc, h: VB[:, 1040 * kc + 65 * h:1040 * kc + 65 * h + 65],
                   adds=(Y,), psw=2048)
    mk.end_phase()
    mk.end_sub()
    mk.begin_sub()
    mk.begin_phase()
    w1 = load_w(mk, "b_fc")
    w2 = load_w(mk, "b_mproj")
    b1r = load_f32(mk, "b_fcb_row")
    b2r = load_f32(mk, "b_mpb_row")
    g2 = load_f32(mk, "b_ln2g_row")
    b2_ = load_f32(mk, "b_ln2b_row")
    mk.weights_ready()
    mlp_pass(mk, Y, dst, T, w1, b1r[:], w2, b2r[:], g2[:], b2_[:], 4096,
             adds=["src", src])
    mk.end_phase()
    mk.end_sub()


SUBS["b1"] = sub_l0_attn_b



def sub_l1_attn_a(mk, src, dst):
    T = mk.T
    B = lambda i: mk.P8[:, 512 * i:512 * (i + 1)]
    BR = mk.p8_reg
    mk.begin_sub()
    mk.begin_phase()
    W = {n: load_w(mk, n) for n in ["wa_q", "wa_k", "wa_v", "wa_o"]}
    mk.weights_ready()
    tc = TC(mk, "wa")
    QW = [mk.sbuf(f"waQ{i}", [128, 1024], BF16)[:] for i in range(2)]
    QW_reg = [Reg(), Reg()]
    KW = [mk.sbuf(f"waK{i}", [128, 1024], BF16)[:] for i in range(2)]
    KW_reg = [Reg(), Reg()]
    VW = [mk.sbuf(f"waV{i}", [128, 1040], BF16)[:] for i in range(2)]
    VW_reg = [Reg(), Reg()]
    KG = mk.sbuf("waKG", [128, 512], BF16)[:]
    VG = mk.sbuf("waVG", [128, 1040], BF16)[:]
    KG_reg = Reg()
    VG_reg = Reg()
    PSW = mk.sbuf("waPS", [128, 256], BF16)[:]
    PSW_reg = Reg()
    PTS = mk.sbuf("waPTS", [128, 1024], BF16)[:]
    AT = [mk.sbuf(f"waAT{i}", [128, 1024], BF16)[:] for i in range(2)]
    AT_reg = [Reg(), Reg()]
    AF_ = [mk.sbuf(f"waAF{i}", [128, 1024], BF16)[:] for i in range(2)]
    AF_reg = [Reg(), Reg()]
    OT = [mk.sbuf(f"waOT{i}", [128, 1024], F32)[:] for i in range(2)]
    OT_reg = [Reg(), Reg()]
    sc = attn_scr(mk, "wasc")
    mk.emit("vector", "memset", PSW, 0.0, inc=1)
    for i in range(2):
        mk.emit("vector", "memset", VW[i], 1.0, inc=1)
    scale = 1.0 / 8.0
    vg_tok = None
    for j in range(mk.TILES):
        ti = tc.tile_in(j, src, fm=True)
        sl = ti["sl"]
        xnf = ti["xnf"]
        fmw = tuple(mk.wr_deps(ti["xnf_reg"]))
        for dest, wname, dreg in ((QW[sl], "wa_q", QW_reg[sl]), (KW[sl], "wa_k", KW_reg[sl])):
            ddeps = mk.wr_deps(dreg)
            dreg.wrs = []
            dreg.rds = []
            for nb in range(2):
                gts = []
                for q in range(4):
                    co = 4 * nb + q
                    gt = mm_chain(mk, B(nb)[:, 128 * q:128 * q + 128],
                                  [(W[wname].ck(c, o0=128 * co, n=128),
                                    xnf[:, 128 * c:128 * c + 128]) for c in range(NCH)],
                                  [BR[nb]] if q == 0 else [], wait=(fmw if q == 0 else ()))
                    gts.append(gt)
                mk.set_wr(BR[nb], gts[-1])
                mk.add_rd(ti["xnf_reg"], gts[-1])
                qe = mk.emit("vector", "tensor_copy", dest[:, 512 * nb:512 * nb + 512],
                             B(nb), wait=(gts[-1],) + ddeps, inc=1)
                mk.add_rd(BR[nb], qe)
                dreg.wrs.append(qe)
        vdeps = mk.wr_deps(VW_reg[sl])
        VW_reg[sl].wrs = []
        VW_reg[sl].rds = []
        for nb in range(2):
            vt = mm_chain(mk, B(2 + nb), [(xnf[:, 128 * c:128 * c + 128],
                                           W["wa_v"].ck(c, o0=512 * nb, n=512))
                                          for c in range(NCH)],
                          [BR[2 + nb]], wait=fmw)
            mk.add_rd(ti["xnf_reg"], vt)
            dv = VW[sl].rearrange("p (h d) -> p h d", d=65)[:, 8 * nb:8 * nb + 8, 0:64]
            sv = B(2 + nb).rearrange("p (h d) -> p h d", d=64)
            ve = mk.emit("vector", "tensor_copy", dv, sv, wait=(vt,) + vdeps, inc=1)
            mk.add_rd(BR[2 + nb], ve)
            VW_reg[sl].wrs.append(ve)
        if j == 0:
            kdeps = mk.wr_deps(KG_reg) + tuple(KW_reg[sl].wrs)
            kg_toks = []
            for c in range(NCH):
                kt = mk.emit("vector", "tensor_copy", KG[:, 64 * c:64 * c + 64],
                             KW[sl][:, 128 * c:128 * c + 64], wait=kdeps, inc=1)
                kg_toks.append(kt)
                mk.add_rd(KW_reg[sl], kt)
            KG_reg.wrs = kg_toks
            vg_tok = mk.emit("vector", "tensor_copy", VG, VW[sl],
                             wait=tuple(VW_reg[sl].wrs) + mk.wr_deps(VG_reg), inc=1)
            mk.set_wr(VG_reg, vg_tok)
            mk.add_rd(VW_reg[sl], vg_tok)
        at = AT[sl]
        at_deps = tuple(AT_reg[sl].rds)
        AT_reg[sl].rds = []
        C = sc[0]
        for h in range(NH):
            r0 = (h % 2) * 64
            cck = 128 * (h // 2)
            sdeps = mk.wr_deps(BR[4]) + tuple(KG_reg.wrs) + tuple(KW_reg[sl].wrs) + \
                tuple(QW_reg[sl].wrs)
            s1 = mk.emit("tensor", "matmul", B(4)[:, 0:64],
                         QW[sl][r0:r0 + 64, cck:cck + 128],
                         KG[r0:r0 + 64, 64 * (h // 2):64 * (h // 2) + 64],
                         start=True, stop=True, wait=sdeps)
            s2 = mk.emit("tensor", "matmul", B(4)[:, 128:256],
                         QW[sl][r0:r0 + 64, cck:cck + 128],
                         KW[sl][r0:r0 + 64, cck:cck + 128],
                         start=True, stop=True, inc=1)
            mk.set_wr(BR[4], s2)
            mk.add_rd(QW_reg[sl], s2)
            mk.add_rd(KW_reg[sl], s2)
            KG_reg.rds.append(s2)
            m1 = mk.emit("vector", "reduce_max", C["negm"], B(4)[:, 0:64], axis=AX.X,
                         negate=True, wait=(s2,))
            m2 = mk.emit("vector", "reduce_max", C["negm2"], B(4)[:, 128:256], axis=AX.X,
                         negate=True)
            mk.emit("vector", "tensor_tensor", C["negmT"], C["negm"], C["negm2"], ALU.min)
            bt = mk.emit("vector", "tensor_scalar_mul", C["bias"], C["negmT"], scale, inc=1)
            pdeps = mk.wr_deps(PSW_reg)
            e1 = mk.emit("scalar", "activation", PSW[:, 0:64], B(4)[:, 0:64], AF.Exp,
                         bias=C["bias"], scale=scale, wait=(bt, s2) + pdeps, inc=1)
            e2 = mk.emit("scalar", "activation", PSW[:, 128:256], B(4)[:, 128:256], AF.Exp,
                         bias=C["bias"], scale=scale, inc=1)
            PSW_reg.wrs = [e1, e2]
            PSW_reg.rds = []
            mk.add_rd(BR[4], m1)
            mk.add_rd(BR[4], m2)
            mk.add_rd(BR[4], e2)
            X = mk.ptb_idx % 2
            mk.ptb_idx += 1
            breg = mk.ptb_reg[X]
            t1 = mk.emit("tensor", "transpose", mk.PT[:, 1024 * X:1024 * X + 128],
                         PSW[:, 0:128], mk.ident, wait=(e1, e2) + mk.wr_deps(breg))
            t2 = mk.emit("tensor", "transpose", mk.PT[:, 1024 * X + 128:1024 * X + 256],
                         PSW[:, 128:256], mk.ident, inc=1)
            mk.set_wr(breg, t2)
            PSW_reg.rds.append(t2)
            et = mk.emit("vector", "tensor_copy", PTS[:, 0:256],
                         mk.PT[:, 1024 * X:1024 * X + 256],
                         wait=(t2,) + mk.wr_deps(mk.ptsb_reg[0]), inc=1)
            mk.add_rd(breg, et)
            mk.set_wr(mk.ptsb_reg[0], et)
            pv1 = mk.emit("tensor", "matmul", mk.PV[:, 0:65], PTS[:, 0:128],
                          VG[:, 65 * h:65 * h + 65], start=True, stop=False,
                          wait=(et, vg_tok) + mk.wr_deps(mk.pv_reg))
            pv2 = mk.emit("tensor", "matmul", mk.PV[:, 0:65], PTS[:, 128:256],
                          VW[sl][:, 65 * h:65 * h + 65], start=False, stop=True,
                          wait=tuple(VW_reg[sl].wrs), inc=1)
            mk.set_wr(mk.pv_reg, pv2)
            mk.add_rd(mk.ptsb_reg[0], pv2)
            VG_reg.rds.append(pv2)
            mk.add_rd(VW_reg[sl], pv2)
            rc = mk.emit("vector", "reciprocal", C["rs"], mk.PV[:, 64:65], wait=(pv2,))
            ot = mk.emit("vector", "tensor_scalar_mul", at[:, 64 * h:64 * h + 64],
                         mk.PV[:, 0:64], C["rs"], wait=(pv2,) + at_deps, inc=1)
            mk.add_rd(mk.pv_reg, ot)
            AT_reg[sl].wrs.append(ot)
        ot_toks = tuple(AT_reg[sl].wrs)
        AT_reg[sl].wrs = []
        aft = transposes(mk, [(at[:, 128 * c:128 * c + 128], AF_[sl][:, 128 * c:128 * c + 128])
                              for c in range(NCH)], wait=ot_toks + mk.wr_deps(AF_reg[sl]))
        AF_reg[sl].wrs = list(aft)
        AF_reg[sl].rds = []
        for t in aft:
            AT_reg[sl].rds.append(t)
        odeps = mk.wr_deps(OT_reg[sl])
        OT_reg[sl].wrs = []
        OT_reg[sl].rds = []
        for nb in range(2):
            wo_t = mm_chain(mk, B(3 + nb), [(AF_[sl][:, 128 * c:128 * c + 128],
                                             W["wa_o"].ck(c, 512 * nb, 512)) for c in range(NCH)],
                            [BR[3 + nb]], wait=tuple(aft))
            mk.add_rd(AF_reg[sl], wo_t)
            colr = slice(512 * nb, 512 * nb + 512)
            oe = mk.emit("vector", "scalar_tensor_tensor", OT[sl][:, colr], B(3 + nb),
                         1.0, ti["rt"][:, colr], ALU.mult, ALU.add,
                         wait=(wo_t, ti["rt_tok"]) + odeps, inc=1)
            mk.add_rd(BR[3 + nb], oe)
            mk.add_rd(ti["rt_reg"], oe)
            OT_reg[sl].wrs.append(oe)
        st = mk.dma("sync", dst[j * 128:(j + 1) * 128, :], OT[sl],
                    wait=tuple(OT_reg[sl].wrs))
        mk.add_rd(OT_reg[sl], st)
    mk.end_phase()
    mk.end_sub()


SUBS["wa"] = sub_l1_attn_a



def load_w_area(mk, area, dram_ap, nch, N, queue="sync", wait=()):
    tok = mk.dma(queue, area[:, 0:nch * N].rearrange("p (c n) -> p c n", c=nch),
                 dram_ap.rearrange("c p n -> p c n"), wait=wait)
    return WH(area, nch, N, tok)


def moe_quarters(mk, src_adds, dst, experts, wgt_ap_fn, I_of, tiles_per_q=None):
    B = lambda i: mk.P8[:, 512 * i:512 * (i + 1)]
    BR = mk.p8_reg
    T = mk.T
    tiles_per_q = tiles_per_q or min(8, mk.TILES)
    nq = mk.TILES // tiles_per_q
    vex = []
    for e in range(len(experts)):
        I = I_of(e)
        nparts = max(1, I // 1024)
        Ip = I // nparts
        for hh in range(nparts):
            vex.append((e, hh, Ip))
    NE = len(vex)
    Imax = max(v[2] for v in vex)
    WGa = [mk.sbuf(f"moe_g{i}", [128, 8 * Imax], BF16)[:] for i in range(2)]
    WUa = [mk.sbuf(f"moe_u{i}", [128, 8 * Imax], BF16)[:] for i in range(2)]
    WDa = [mk.sbuf(f"moe_d{i}", [128, (Imax // 128) * 1024], BF16)[:] for i in range(2)]
    XQ = mk.sbuf("moe_xq", [128, NCH * 128 * tiles_per_q], BF16)[:]
    AC = mk.sbuf("moe_ac", [128, 1024 * tiles_per_q], F32)[:]
    HF = mk.sbuf("moe_hf", [128, (Imax // 128) * 512], BF16)[:]
    GS = [mk.sbuf(f"moe_gs{i}", [128, 512], BF16)[:] for i in range(2)]
    AD = [mk.sbuf(f"moe_ad{i}", [128, 1024], F32)[:] for i in range(2)]
    AD_reg = [Reg(), Reg()]
    OTm = [mk.sbuf(f"moe_ot{i}", [128, 1024], F32)[:] for i in range(2)]
    OT_reg = [Reg(), Reg()]
    XQ_reg = Reg()
    AC_reg = Reg()
    HF_reg = Reg()
    GS_reg = [Reg(), Reg()]
    W_reg = [Reg(), Reg()]
    QTOK = 128 * tiles_per_q
    WH3 = [None, None]

    def load_exp(gi, sl):
        e, hh, Ip = vex[gi % NE]
        gd, ud, dd, kind = experts[e]
        nicp = Ip // 128
        wdeps = mk.wr_deps(W_reg[sl])
        W_reg[sl].wrs = []
        W_reg[sl].rds = []
        wg = load_w_area(mk, WGa[sl], gd[:, :, hh * Ip:(hh + 1) * Ip], NCH, Ip, wait=wdeps)
        wu = load_w_area(mk, WUa[sl], ud[:, :, hh * Ip:(hh + 1) * Ip], NCH, Ip, wait=wdeps)
        wd = load_w_area(mk, WDa[sl], dd[hh * nicp:(hh + 1) * nicp], nicp, 1024, wait=wdeps)
        W_reg[sl].wrs = [wg.tok, wu.tok, wd.tok]
        WH3[sl] = (wg, wu, wd)

    load_exp(0, 0)
    gidx = 0
    for qt in range(nq):
        xql = mk.dma("sync", XQ.rearrange("p (c t) -> p c t", c=NCH),
                     mk.dram["XNF"][:, :, qt * QTOK:(qt + 1) * QTOK],
                     wait=mk.wr_deps(XQ_reg))
        mk.set_wr(XQ_reg, xql)
        acdeps = mk.wr_deps(AC_reg)
        AC_reg.wrs = []
        AC_reg.rds = []
        for ve in range(NE):
            e, hh, Ip = vex[ve]
            sl = gidx % 2
            wg, wu, wd = WH3[sl]
            nic = Ip // 128
            if gidx + 1 < nq * NE:
                load_exp(gidx + 1, (gidx + 1) % 2)
            for tb in range(tiles_per_q // 4):
                hfdeps = mk.wr_deps(HF_reg)
                HF_reg.wrs = []
                HF_reg.rds = []
                for ic in range(nic):
                    bg = (ic % 2) * 2
                    gt = mm_chain(mk, B(bg), [(wg.ck(c, o0=128 * ic, n=128),
                                               XQ[:, c * QTOK + tb * 512: c * QTOK + tb * 512 + 512])
                                              for c in range(NCH)],
                                  [BR[bg]], wait=(wg.tok, xql))
                    ut = mm_chain(mk, B(bg + 1), [(wu.ck(c, o0=128 * ic, n=128),
                                                   XQ[:, c * QTOK + tb * 512: c * QTOK + tb * 512 + 512])
                                                  for c in range(NCH)],
                                  [BR[bg + 1]], wait=(wu.tok, xql))
                    mk.add_rd(XQ_reg, gt)
                    mk.add_rd(XQ_reg, ut)
                    gsl = ic % 2
                    sg = mk.emit("scalar", "activation", GS[gsl], B(bg), AF.Silu,
                                 wait=(gt,) + mk.wr_deps(GS_reg[gsl]), inc=1)
                    mk.add_rd(BR[bg], sg)
                    hm = mk.emit("vector", "tensor_mul", HF[:, 512 * ic:512 * ic + 512],
                                 B(bg + 1), GS[gsl], wait=(ut, sg) + hfdeps, inc=1)
                    mk.add_rd(BR[bg + 1], hm)
                    GS_reg[gsl].wrs = [sg]
                    GS_reg[gsl].rds = [hm]
                    HF_reg.wrs.append(hm)
                for tt in range(4):
                    gtile = qt * tiles_per_q + tb * 4 + tt
                    for nb in range(2):
                        bk = 2 + ((tt * 2 + nb) % 3)
                        dt = mm_chain(mk, B(bk),
                                      [(HF[:, 512 * ic + 128 * tt: 512 * ic + 128 * tt + 128],
                                        wd.ck(ic, o0=512 * nb, n=512)) for ic in range(nic)],
                                      [BR[bk]], wait=tuple(HF_reg.wrs) + (wd.tok,))
                        HF_reg.rds.append(dt)
                        wap = wgt_ap_fn(e, gtile)
                        colr = slice((tb * 4 + tt) * 1024 + 512 * nb,
                                     (tb * 4 + tt) * 1024 + 512 * nb + 512)
                        ev = "vector" if (tt + nb) % 2 == 0 else "scalar"
                        if ve == 0:
                            if wap is None:
                                ae = mk.emit(ev, "tensor_copy", AC[:, colr], B(bk),
                                             wait=(dt,) + acdeps, inc=1)
                            else:
                                ae = mk.emit("vector", "tensor_scalar_mul", AC[:, colr],
                                             B(bk), wap, wait=(dt,) + acdeps, inc=1)
                        else:
                            if wap is None:
                                ae = mk.emit("vector", "tensor_tensor", AC[:, colr], AC[:, colr],
                                             B(bk), ALU.add, wait=(dt,), inc=1)
                            else:
                                ae = mk.emit("vector", "scalar_tensor_tensor", AC[:, colr],
                                             B(bk), wap, AC[:, colr], ALU.mult, ALU.add,
                                             wait=(dt,), inc=1)
                        mk.add_rd(BR[bk], ae)
                        AC_reg.wrs.append(ae)
                        mk.add_rd(W_reg[sl], dt)
            gidx += 1
        for tt in range(tiles_per_q):
            gtile = qt * tiles_per_q + tt
            sl = tt % 2
            otdeps = mk.wr_deps(OT_reg[sl])
            OT_reg[sl].wrs = []
            OT_reg[sl].rds = []
            first = True
            prev = None
            for a, adram in enumerate(src_adds):
                ald = mk.dma("gpsimd", AD[sl], adram[gtile * 128:(gtile + 1) * 128, :],
                             wait=mk.wr_deps(AD_reg[sl]))
                mk.set_wr(AD_reg[sl], ald)
                if first:
                    oe = mk.emit("vector", "tensor_add", OTm[sl], AD[sl],
                                 AC[:, tt * 1024:(tt + 1) * 1024],
                                 wait=(ald,) + tuple(AC_reg.wrs) + otdeps, inc=1)
                    AC_reg.rds.append(oe)
                    first = False
                else:
                    oe = mk.emit("vector", "tensor_add", OTm[sl], OTm[sl], AD[sl],
                                 wait=(ald,), inc=1)
                mk.add_rd(AD_reg[sl], oe)
                prev = oe
            OT_reg[sl].wrs = [prev]
            st = mk.dma("sync", dst[gtile * 128:(gtile + 1) * 128, :], OTm[sl], wait=(prev,))
            mk.add_rd(OT_reg[sl], st)


def route_l1(mk, psum_ap, crow, wgt_dst, rs):
    BIG = 1.0e30
    S = rs
    mk.emit("scalar", "activation", S["sco"], psum_ap, AF.Sigmoid, wait=rs["_wait"])
    mk.emit("vector", "tensor_add", S["sb"], S["sco"], crow, wait=(mk.stok(),))
    sb3 = S["sb"].rearrange("p (g j) -> p g j", g=4)
    mk.emit("vector", "tensor_reduce", S["g4a"], sb3, axis=AX.X, op=ALU.max)
    g4r = S["tmp"].rearrange("p (g j) -> p g j", g=4)
    for j in range(4):
        mk.emit("vector", "tensor_copy", g4r[:, :, j:j + 1],
                S["g4a"].rearrange("p (g x) -> p g x", x=1))
    mk.emit("vector", "tensor_tensor", S["eq"], S["sb"], S["tmp"], ALU.is_ge)
    mk.emit("vector", "scalar_tensor_tensor", S["mb"], S["eq"], -BIG, S["sb"],
            ALU.mult, ALU.add)
    mk.emit("vector", "tensor_reduce", S["g4b"], S["mb"].rearrange("p (g j) -> p g j", g=4),
            axis=AX.X, op=ALU.max)
    mk.emit("vector", "tensor_add", S["g4a"], S["g4a"], S["g4b"])
    mk.emit("vector", "reduce_max", S["c1"], S["g4a"], axis=AX.X)
    mk.emit("vector", "tensor_single_scalar", S["g4b"], S["g4a"], S["c1"], ALU.is_ge)
    mk.emit("vector", "scalar_tensor_tensor", S["g4b"], S["g4b"], -BIG, S["g4a"],
            ALU.mult, ALU.add)
    mk.emit("vector", "reduce_max", S["c1"], S["g4b"], axis=AX.X)
    mk.emit("vector", "tensor_single_scalar", S["g4b"], S["g4a"], S["c1"], ALU.is_ge)
    for j in range(4):
        mk.emit("vector", "tensor_copy", g4r[:, :, j:j + 1],
                S["g4b"].rearrange("p (g x) -> p g x", x=1))
    mk.emit("vector", "tensor_scalar", S["tmp"], S["tmp"], BIG, -BIG, ALU.mult, ALU.add)
    mk.emit("vector", "tensor_add", S["msb"], S["sb"], S["tmp"])
    mk.emit("vector", "tensor_copy", S["msb0"], S["msb"])
    for _ in range(3):
        mk.emit("vector", "reduce_max", S["c1"], S["msb"], axis=AX.X)
        mk.emit("vector", "tensor_single_scalar", S["eq"], S["msb"], S["c1"], ALU.is_ge)
        mk.emit("vector", "scalar_tensor_tensor", S["msb"], S["eq"], -BIG, S["msb"],
                ALU.mult, ALU.add)
    mk.emit("vector", "reduce_max", S["c1"], S["msb"], axis=AX.X)
    mk.emit("vector", "tensor_single_scalar", S["eq"], S["msb0"], S["c1"], ALU.is_ge)
    mk.emit("vector", "tensor_mul", S["eq"], S["eq"], S["sco"])
    mk.emit("vector", "reduce_sum", S["c1"], S["eq"], axis=AX.X)
    mk.emit("vector", "tensor_scalar_add", S["c1"], S["c1"], 1.0e-20)
    mk.emit("vector", "reciprocal", S["c1"], S["c1"])
    tok = mk.emit("vector", "tensor_scalar_mul", wgt_dst, S["eq"], S["c1"], inc=1)
    return tok


def route_scr(mk, name):
    h = mk.sbuf(name, [128, 128], F32)[:]
    return {"sco": h[:, 0:16], "sb": h[:, 16:32], "eq": h[:, 32:48], "mb": h[:, 48:64],
            "msb": h[:, 64:80], "msb0": h[:, 80:96], "tmp": h[:, 96:112],
            "g4a": h[:, 112:116], "g4b": h[:, 116:120], "c1": h[:, 120:121], "_wait": ()}


def route_l0(mk, psum_ap, brow, wgt_dst, rs):
    BIG = 1.0e30
    S = rs
    mk.emit("vector", "scalar_tensor_tensor", S["sco"][:, 0:7], psum_ap, 1.0, brow,
            ALU.mult, ALU.add, wait=rs["_wait"])
    mk.emit("scalar", "activation", S["sb"][:, 0:7], S["sco"][:, 0:7], AF.Sigmoid,
            wait=(mk.vtok(),))
    prb = S["sb"][:, 0:7]
    mk.emit("vector", "reduce_max", S["c1"], prb, axis=AX.X, wait=(mk.stok(),))
    mk.emit("vector", "tensor_single_scalar", S["eq"][:, 0:7], prb, S["c1"], ALU.is_ge)
    mk.emit("vector", "scalar_tensor_tensor", S["mb"][:, 0:7], S["eq"][:, 0:7], -BIG, prb,
            ALU.mult, ALU.add)
    mk.emit("vector", "reduce_max", S["c1"], S["mb"][:, 0:7], axis=AX.X)
    mk.emit("vector", "tensor_single_scalar", S["eq"][:, 0:7], prb, S["c1"], ALU.is_ge)
    mk.emit("vector", "tensor_mul", S["eq"][:, 0:7], S["eq"][:, 0:7], prb)
    mk.emit("vector", "reduce_sum", S["c1"], S["eq"][:, 0:7], axis=AX.X)
    mk.emit("vector", "reciprocal", S["c1"], S["c1"])
    tok = mk.emit("vector", "tensor_scalar_mul", wgt_dst, S["eq"][:, 0:7], S["c1"], inc=1)
    return tok


def rope_heads(mk, x_psum, crb_tile, dst):
    x3 = x_psum.rearrange("p (h d) -> p h d", d=32)
    x1 = x3[:, :, 0:16]
    x2 = x3[:, :, 16:32]
    cos3 = crb_tile[:, 0:256].rearrange("p (h d) -> p h d", d=16)
    sin3 = crb_tile[:, 256:512].rearrange("p (h d) -> p h d", d=16)
    sc = mk.sb["ropescr"]
    m1 = sc[:, 0:256].rearrange("p (h d) -> p h d", d=16)
    m2 = sc[:, 256:512].rearrange("p (h d) -> p h d", d=16)
    d3 = dst.rearrange("p (h d) -> p h d", d=64)
    o1 = d3[:, :, 0:16]
    o2 = d3[:, :, 16:32]
    mk.emit("vector", "tensor_mul", m1, x1, cos3)
    mk.emit("vector", "tensor_mul", m2, x2, sin3)
    mk.emit("vector", "tensor_sub", o1, m1, m2)
    mk.emit("vector", "tensor_mul", m1, x2, cos3)
    mk.emit("vector", "tensor_mul", m2, x1, sin3)
    tok = mk.emit("vector", "tensor_add", o2, m1, m2, inc=1)
    return tok



def sub_l1_ffn(mk, src, dst):
    T = mk.T
    B = lambda i: mk.P8[:, 512 * i:512 * (i + 1)]
    BR = mk.p8_reg
    mk.begin_sub()
    WGT = mk.sbuf("gWGT", [128, mk.TILES * 16], F32, scope="sub")[:]
    mk.begin_phase()
    wrt = load_w(mk, "g_rt")
    crow = load_f32(mk, "g_cb_row")
    mk.weights_ready()
    tc = TC(mk, "gx")
    rs = route_scr(mk, "grs")
    for j in range(mk.TILES):
        ti = tc.tile_in(j, src, fm=True)
        sl = ti["sl"]
        xnf = ti["xnf"]
        fmw = tuple(mk.wr_deps(ti["xnf_reg"]))
        st = mk.dma("sync", mk.dram["XNF"][:, :, j * 128:(j + 1) * 128],
                    xnf.rearrange("p (c t) -> p c t", c=NCH), wait=fmw)
        mk.add_rd(ti["xnf_reg"], st)
        lt = mm_chain(mk, B(4)[:, 0:16], [(xnf[:, 128 * c:128 * c + 128], wrt.ck(c))
                                          for c in range(NCH)],
                      [BR[4]], wait=fmw + (wrt.tok,))
        mk.add_rd(ti["xnf_reg"], lt)
        rs["_wait"] = (lt,)
        rt = route_l1(mk, B(4)[:, 0:16], crow[:], WGT[:, 16 * j:16 * j + 16], rs)
        mk.add_rd(BR[4], rt)
    mk.end_phase()
    mk.begin_phase()
    mk.weights_ready()
    experts = [(mk.dram["bf_g_gw"][e], mk.dram["bf_g_uw"][e], mk.dram["bf_g_dw"][e],
                "routed") for e in range(16)]
    experts.append((mk.dram["bf_g_sg"], mk.dram["bf_g_su"], mk.dram["bf_g_sd"], "shared"))

    def wgt_fn(e, gtile):
        if e == 16:
            return None
        return WGT[:, 16 * gtile + e:16 * gtile + e + 1]

    moe_quarters(mk, [src], dst, experts, wgt_fn, lambda e: 256 if e < 16 else 2048)
    mk.end_phase()
    mk.end_sub()


SUBS["g1"] = sub_l1_ffn
SUBS["g2"] = sub_l1_ffn



def sub_l0_ffn(mk, src, dst):
    T = mk.T
    B = lambda i: mk.P8[:, 512 * i:512 * (i + 1)]
    BR = mk.p8_reg
    Y = mk.dram["Y"]
    Y1 = mk.dram["Y1"]
    mk.begin_sub()
    KB = mk.sbuf("fKBUF", [128, NCH * T], BF16, scope="sub")[:]
    VB = mk.sbuf("fVBUF", [128, mk.TILES * 1040], BF16, scope="sub")[:]
    mk.begin_phase()
    Wp = {n: load_w(mk, n) for n in ["f_kvd", "f_ku", "f_vu", "f_rk"]}
    ln1 = load_f32(mk, "f_ln1_row")
    mk.weights_ready()
    tc = TC(mk, "fp1")
    xw = mk.sbuf("f_xw", [128, D], F32)[:]
    xw_reg = Reg()
    mk.sbuf("ropescr", [128, 512], F32)
    ropek = mk.sbuf("f_ropek", [128, 1024], BF16)[:]
    ropek_reg = Reg()
    crb = [mk.sbuf(f"f_crb{i}", [128, 512], F32)[:] for i in range(2)]
    crb_reg = [Reg(), Reg()]
    kvl = [mk.sbuf(f"f_kvl{i}", [128, 256], BF16)[:] for i in range(2)]
    kvl_reg = [Reg(), Reg()]
    mk.emit("vector", "memset", VB, 1.0, inc=1)
    for j in range(mk.TILES):
        ti = tc.tile_in(j, src, gamma=ln1[:], fm=True, double=True, xin=xw)
        sl = ti["sl"]
        xnf = ti["xnf"]
        ya = ti["ln"]["ya_tok"]
        yst = mk.dma("sync", Y[j * 128:(j + 1) * 128, :], xw, wait=(ya,) + mk.wr_deps(xw_reg))
        mk.set_wr(xw_reg, ya)
        mk.add_rd(xw_reg, yst)
        fmw = tuple(mk.wr_deps(ti["xnf_reg"]))
        cld = mk.dma("gpsimd", crb[sl], mk.dram["crB"][j * 128:(j + 1) * 128, :],
                     wait=mk.wr_deps(crb_reg[sl]))
        mk.set_wr(crb_reg[sl], cld)
        kdeps = mk.wr_deps(kvl_reg[sl])
        kvl_reg[sl].wrs = []
        kvl_reg[sl].rds = []
        kts = []
        for lc in range(2):
            kt = mm_chain(mk, B(4)[:, 128 * lc:128 * lc + 128],
                          [(Wp["f_kvd"].ck(c, o0=128 * lc, n=128),
                            xnf[:, 128 * c:128 * c + 128]) for c in range(NCH)],
                          [BR[4]] if lc == 0 else [], wait=(fmw if lc == 0 else ()))
            kts.append(kt)
            mk.add_rd(ti["xnf_reg"], kt)
        mk.set_wr(BR[4], kts[-1])
        ke = mk.emit("vector", "tensor_copy", kvl[sl], B(4)[:, 0:256],
                     wait=(kts[-1],) + kdeps, inc=1)
        mk.add_rd(BR[4], ke)
        kvl_reg[sl].wrs = [ke]
        for nb in range(2):
            gts = []
            for q in range(4):
                co = 4 * nb + q
                gt = mm_chain(mk, B(nb)[:, 128 * q:128 * q + 128],
                              [(Wp["f_ku"].ck(lc, o0=128 * co, n=128),
                                kvl[sl][:, 128 * lc:128 * lc + 128]) for lc in range(2)],
                              [BR[nb]] if q == 0 else [], wait=((ke,) if q == 0 else ()))
                gts.append(gt)
            mk.set_wr(BR[nb], gts[-1])
            kvl_reg[sl].rds.append(gts[-1])
            for q in range(4):
                co = 4 * nb + q
                ket = mk.emit("vector", "tensor_copy",
                              KB[:, co * T + 128 * j: co * T + 128 * j + 128],
                              B(nb)[:, 128 * q:128 * q + 128], wait=(gts[-1],), inc=1)
                mk.add_rd(BR[nb], ket)
        for nb in range(2):
            vt = mm_chain(mk, B(2 + nb), [(kvl[sl][:, 128 * lc:128 * lc + 128],
                                           Wp["f_vu"].ck(lc, o0=512 * nb, n=512))
                                          for lc in range(2)],
                          [BR[2 + nb]], wait=(ke,))
            kvl_reg[sl].rds.append(vt)
            dv = VB[:, 1040 * j:1040 * (j + 1)].rearrange("p (h d) -> p h d", d=65)
            dv = dv[:, 8 * nb:8 * nb + 8, 0:64]
            sv = B(2 + nb).rearrange("p (h d) -> p h d", d=64)
            ve = mk.emit("vector", "tensor_copy", dv, sv, wait=(vt,), inc=1)
            mk.add_rd(BR[2 + nb], ve)
        krt = mm_chain(mk, B(0), [(xnf[:, 128 * c:128 * c + 128], Wp["f_rk"].ck(c))
                                  for c in range(NCH)],
                       [BR[0]], wait=fmw)
        mk.add_rd(ti["xnf_reg"], krt)
        mk.raw_wait("vector", krt)
        mk.raw_wait("vector", cld)
        for t in mk.wr_deps(ropek_reg):
            mk.raw_wait("vector", t)
        rtok = rope_heads(mk, B(0), crb[sl], ropek)
        mk.add_rd(BR[0], rtok)
        crb_reg[sl].rds = [rtok]
        pairs = []
        for c in range(NCH):
            pairs.append((ropek[:, 128 * c:128 * c + 128],
                          [(0, 32, KB[0:32, c * T + 128 * j: c * T + 128 * j + 128]),
                           (64, 96, KB[64:96, c * T + 128 * j: c * T + 128 * j + 128])]))
        ktt = transposes(mk, pairs, wait=(rtok,))
        ropek_reg.wrs = []
        ropek_reg.rds = list(ktt)
    mk.end_phase()
    mk.begin_phase()
    Wq = {n: load_w(mk, n) for n in ["f_qd", "f_qu", "f_rq"]}
    ln1b = load_f32(mk, "f_ln1_row")
    mk.weights_ready()
    tc2 = TC(mk, "fp2")
    mk.sbuf("ropescr", [128, 512], F32)
    ropeq = mk.sbuf("f_ropeq", [128, 1024], BF16)[:]
    ropeq_reg = Reg()
    crb2 = [mk.sbuf(f"f_crb2{i}", [128, 512], F32)[:] for i in range(2)]
    crb2_reg = [Reg(), Reg()]
    qvl = [mk.sbuf(f"f_qvl{i}", [128, 256], BF16)[:] for i in range(2)]
    qvl_reg = [Reg(), Reg()]
    QA = [mk.sbuf(f"fQA{i}", [128, 1024], BF16)[:] for i in range(2)]
    QA_reg = [Reg(), Reg()]
    for i in range(mk.TILES):
        ti = tc2.tile_in(i, src, gamma=ln1b[:], fm=True, double=True)
        sl = ti["sl"]
        xnf = ti["xnf"]
        fmw = tuple(mk.wr_deps(ti["xnf_reg"]))
        cld = mk.dma("gpsimd", crb2[sl], mk.dram["crB"][i * 128:(i + 1) * 128, :],
                     wait=mk.wr_deps(crb2_reg[sl]))
        mk.set_wr(crb2_reg[sl], cld)
        qdeps0 = mk.wr_deps(qvl_reg[sl])
        qvl_reg[sl].wrs = []
        qvl_reg[sl].rds = []
        kts = []
        for lc in range(2):
            kt = mm_chain(mk, B(4)[:, 128 * lc:128 * lc + 128],
                          [(Wq["f_qd"].ck(c, o0=128 * lc, n=128),
                            xnf[:, 128 * c:128 * c + 128]) for c in range(NCH)],
                          [BR[4]] if lc == 0 else [], wait=(fmw if lc == 0 else ()))
            kts.append(kt)
            mk.add_rd(ti["xnf_reg"], kt)
        mk.set_wr(BR[4], kts[-1])
        ke = mk.emit("vector", "tensor_copy", qvl[sl], B(4)[:, 0:256],
                     wait=(kts[-1],) + qdeps0, inc=1)
        mk.add_rd(BR[4], ke)
        qvl_reg[sl].wrs = [ke]
        qadeps = mk.wr_deps(QA_reg[sl])
        QA_reg[sl].wrs = []
        QA_reg[sl].rds = []
        for nb in range(2):
            gts = []
            for q in range(4):
                co = 4 * nb + q
                gt = mm_chain(mk, B(nb)[:, 128 * q:128 * q + 128],
                              [(Wq["f_qu"].ck(lc, o0=128 * co, n=128),
                                qvl[sl][:, 128 * lc:128 * lc + 128]) for lc in range(2)],
                              [BR[nb]] if q == 0 else [], wait=((ke,) if q == 0 else ()))
                gts.append(gt)
            mk.set_wr(BR[nb], gts[-1])
            qvl_reg[sl].rds.append(gts[-1])
            qe = mk.emit("vector", "tensor_copy", QA[sl][:, 512 * nb:512 * nb + 512],
                         B(nb), wait=(gts[-1],) + qadeps, inc=1)
            mk.add_rd(BR[nb], qe)
            QA_reg[sl].wrs.append(qe)
        qrt = mm_chain(mk, B(0), [(qvl[sl][:, 128 * lc:128 * lc + 128], Wq["f_rq"].ck(lc))
                                  for lc in range(2)],
                       [BR[0]], wait=(ke,))
        qvl_reg[sl].rds.append(qrt)
        mk.raw_wait("vector", qrt)
        mk.raw_wait("vector", cld)
        for t in mk.wr_deps(ropeq_reg):
            mk.raw_wait("vector", t)
        rtok = rope_heads(mk, B(0), crb2[sl], ropeq)
        mk.add_rd(BR[0], rtok)
        crb2_reg[sl].rds = [rtok]
        pairs = []
        for c in range(NCH):
            pairs.append((ropeq[:, 128 * c:128 * c + 128],
                          [(0, 32, QA[sl][0:32, 128 * c:128 * c + 128]),
                           (64, 96, QA[sl][64:96, 128 * c:128 * c + 128])]))
        qtt = transposes(mk, pairs, wait=(rtok,) + tuple(QA_reg[sl].wrs))
        ropeq_reg.wrs = []
        ropeq_reg.rds = list(qtt)
        QA_reg[sl].wrs += list(qtt)
        st = mk.dma("sync", mk.dram["QS"][i * 128:(i + 1) * 128, 0:1024], QA[sl],
                    wait=tuple(QA_reg[sl].wrs))
        mk.add_rd(QA_reg[sl], st)
    mk.end_phase()
    mk.begin_phase()
    wo = load_w(mk, "f_ow")
    mk.weights_ready()
    attn_only_pass(mk, Y1, T, KB, VB, wo, None, 0.125, lambda i: 128 * (i + 1),
                   pmask_fn=lambda i, kc: mk.m_tril if kc == i else None,
                   kf=lambda h, k0, n: KB[(h % 2) * 64:(h % 2) * 64 + 64,
                                          T * (h // 2) + k0:T * (h // 2) + k0 + n],
                   vf=lambda kc, h: VB[:, 1040 * kc + 65 * h:1040 * kc + 65 * h + 65],
                   adds=(Y,))
    mk.end_phase()
    mk.end_sub()
    mk.begin_sub()
    WGT = mk.sbuf("fWGT", [128, mk.TILES * 8], F32, scope="sub")[:]
    mk.begin_phase()
    wrt = load_w(mk, "f_rt")
    brow = load_f32(mk, "f_rb_row")
    ln2 = load_f32(mk, "f_ln2_row")
    mk.weights_ready()
    tc3 = TC(mk, "fp3")
    rs = route_scr(mk, "frs")
    for j in range(mk.TILES):
        ti = tc3.tile_in(j, Y1, gamma=ln2[:], fm=True)
        sl = ti["sl"]
        xnf = ti["xnf"]
        fmw = tuple(mk.wr_deps(ti["xnf_reg"]))
        st = mk.dma("sync", mk.dram["XNF"][:, :, j * 128:(j + 1) * 128],
                    xnf.rearrange("p (c t) -> p c t", c=NCH), wait=fmw)
        mk.add_rd(ti["xnf_reg"], st)
        lt = mm_chain(mk, B(4)[:, 0:7], [(xnf[:, 128 * c:128 * c + 128], wrt.ck(c))
                                         for c in range(NCH)],
                      [BR[4]], wait=fmw + (wrt.tok,))
        mk.add_rd(ti["xnf_reg"], lt)
        rs["_wait"] = (lt,)
        rt = route_l0(mk, B(4)[:, 0:7], brow[:], WGT[:, 8 * j:8 * j + 7], rs)
        mk.add_rd(BR[4], rt)
    mk.end_phase()
    mk.begin_phase()
    mk.weights_ready()
    experts = [(mk.dram["bf_f_rg"][e], mk.dram["bf_f_ru"][e], mk.dram["bf_f_rd"][e],
                "routed") for e in range(7)]
    experts.append((mk.dram["bf_f_sg"], mk.dram["bf_f_su"], mk.dram["bf_f_sd"], "shared"))

    def wgt_fn(e, gtile):
        if e == 7:
            return None
        return WGT[:, 8 * gtile + e:8 * gtile + e + 1]

    moe_quarters(mk, [src, Y1], dst, experts, wgt_fn, lambda e: 2048)
    mk.end_phase()
    mk.end_sub()


SUBS["f1"] = sub_l0_ffn
SUBS["f2"] = sub_l0_ffn
